# version 20
# speedup vs baseline: 1.2368x; 1.0022x over previous
import numpy as np
from contextlib import ExitStack

import concourse.bass as bass
import concourse.mybir as mybir
from concourse.bass_utils import run_bass_kernel_spmd

F32 = mybir.dt.float32
BF16 = mybir.dt.bfloat16
AX = mybir.AxisListType
ALU = mybir.AluOpType
ACT = mybir.ActivationFunctionType


def _SIG():
    return ACT.Exp if _SIG2EXP else ACT.Sigmoid

T = 4096
D = 1024
H = 16
HID = 2048
MD = 64
NE = 8
EH = 1024
NT = T // 128
QB = 512
NB = T // QB
QC = 256
NCB = T // QC
EPS = 1e-5

import os as _os
_DEBUG = _os.environ.get("K_DEBUG", "") == "1"
_STOP = _os.environ.get("K_STOP", "")
_NORSD = _os.environ.get("K_NORSD", "") == "1"
_NOMV = _os.environ.get("K_NOMV", "") == "1"
_HPN = int(_os.environ.get("K_HP", "8"))
_SONLY = _os.environ.get("K_SONLY", "") == "1"
_NOEXP = _os.environ.get("K_NOEXP", "") == "1"
_NOB = _os.environ.get("K_NOB", "") == "1"
_CLVL = int(_os.environ.get("K_CLVL", "99"))
_NOEH = _os.environ.get("K_NOEH", "") == "1"
_NOEO = _os.environ.get("K_NOEO", "") == "1"
_NOBC = _os.environ.get("K_NOBC", "") == "1"
_SIG2EXP = _os.environ.get("K_SIG2EXP", "") == "1"
_NOENCACT = _os.environ.get("K_NOENCACT", "") == "1"
_NOENCMM = _os.environ.get("K_NOENCMM", "") == "1"


class Sem:
    def __init__(self, h):
        self.h = h
        self.n = 0


def build_core():
    nc = bass.Bass("TRN2", target_bir_lowering=False, detect_race_conditions=False)

    xd = nc.dram_tensor("x", [T, D], F32, kind="ExternalInput")
    w_qkv = nc.dram_tensor("w_qkv", [D, 3 * D], F32, kind="ExternalInput")
    w_proj = nc.dram_tensor("w_proj", [D, D], F32, kind="ExternalInput")
    b_proj = nc.dram_tensor("b_proj", [D], F32, kind="ExternalInput")
    enc_w = nc.dram_tensor("enc_w", [D, HID], F32, kind="ExternalInput")
    enc_b = nc.dram_tensor("enc_b", [HID], F32, kind="ExternalInput")
    s2c_w = nc.dram_tensor("s2c_w", [HID, MD], F32, kind="ExternalInput")
    s2c_b = nc.dram_tensor("s2c_b", [MD], F32, kind="ExternalInput")
    r_w1 = nc.dram_tensor("r_w1", [MD, 64], F32, kind="ExternalInput")
    r_b1 = nc.dram_tensor("r_b1", [64], F32, kind="ExternalInput")
    r_w2 = nc.dram_tensor("r_w2", [64, NE], F32, kind="ExternalInput")
    r_b2 = nc.dram_tensor("r_b2", [NE], F32, kind="ExternalInput")
    e_w1 = nc.dram_tensor("e_w1", [NE, MD, EH], F32, kind="ExternalInput")
    e_b1 = nc.dram_tensor("e_b1", [NE, EH], F32, kind="ExternalInput")
    e_w2 = nc.dram_tensor("e_w2", [NE, EH, MD], F32, kind="ExternalInput")
    e_b2 = nc.dram_tensor("e_b2", [NE, MD], F32, kind="ExternalInput")
    c2s_w = nc.dram_tensor("c2s_w", [MD, HID], F32, kind="ExternalInput")
    c2s_b = nc.dram_tensor("c2s_b", [HID], F32, kind="ExternalInput")
    dec_w = nc.dram_tensor("dec_w", [HID, D], F32, kind="ExternalInput")
    dec_b = nc.dram_tensor("dec_b", [D], F32, kind="ExternalInput")
    ln_g = nc.dram_tensor("ln_g", [D], F32, kind="ExternalInput")
    ln_b = nc.dram_tensor("ln_b", [D], F32, kind="ExternalInput")
    out = nc.dram_tensor("out", [T, D], F32, kind="ExternalOutput")

    qts = nc.dram_tensor("qts", [8, 128, T], BF16, kind="Internal")
    x2s = nc.dram_tensor("x2s", [T, D], F32, kind="Internal")
    wqk_pre = nc.dram_tensor("wqk_pre", [16, 8, 128, 128], BF16, kind="Internal")
    wv_nat = nc.dram_tensor("wv_nat", [8, 128, D], BF16, kind="Internal")
    wproj_n = nc.dram_tensor("wproj_n", [8, 128, D], BF16, kind="Internal")
    enc_pre = nc.dram_tensor("enc_pre", [16, 8, 128, 128], BF16, kind="Internal")
    s2c_pre = nc.dram_tensor("s2c_pre", [16, 128, 128], BF16, kind="Internal")
    ew1_pre = nc.dram_tensor("ew1_pre", [4, 8, 128, 128], BF16, kind="Internal")
    ew2_pre = nc.dram_tensor("ew2_pre", [NE, 8, 128, 64], BF16, kind="Internal")
    c2s_pre = nc.dram_tensor("c2s_pre", [16, 128, 128], BF16, kind="Internal")
    dec_nat = nc.dram_tensor("dec_nat", [16, 128, D], BF16, kind="Internal")
    rw_pre = nc.dram_tensor("rw_pre", [64, 64 + NE], BF16, kind="Internal")
    brow_pre = nc.dram_tensor("brow_pre", [2, D], BF16, kind="Internal")

    dbg = {}
    if _DEBUG:
        dbg["kt"] = nc.dram_tensor("dbg_kt", [128, 8 * T], BF16, kind="ExternalOutput")
        dbg["v"] = nc.dram_tensor("dbg_v", [128, NT * H * 65], BF16, kind="ExternalOutput")
        dbg["qt"] = nc.dram_tensor("dbg_qt", [8, 128, T], BF16, kind="ExternalOutput")
        dbg["cb"] = nc.dram_tensor("dbg_cb", [128, 64], F32, kind="ExternalOutput")
        dbg["an"] = nc.dram_tensor("dbg_an", [128, 8 * QB], BF16, kind="ExternalOutput")
        dbg["x2"] = nc.dram_tensor("dbg_x2", [T, D], F32, kind="ExternalOutput")
        dbg["spk"] = nc.dram_tensor("dbg_spk", [128, 16 * QC], BF16, kind="ExternalOutput")
        dbg["cont"] = nc.dram_tensor("dbg_cont", [128, QC], BF16, kind="ExternalOutput")
        dbg["comb"] = nc.dram_tensor("dbg_comb", [128, 2 * NE], F32, kind="ExternalOutput")
        dbg["moe"] = nc.dram_tensor("dbg_moe", [128, QC], F32, kind="ExternalOutput")
        dbg["dec"] = nc.dram_tensor("dbg_dec", [128, 2 * D], F32, kind="ExternalOutput")
        dbg["lgt"] = nc.dram_tensor("dbg_lgt", [128, 2 * NE], F32, kind="ExternalOutput")
        dbg["xlnT"] = nc.dram_tensor("dbg_xlnT", [128, 8 * QB], BF16, kind="ExternalOutput")
        dbg["wqk"] = nc.dram_tensor("dbg_wqk", [128, 2 * 8 * 128], BF16, kind="ExternalOutput")
        dbg["stat"] = nc.dram_tensor("dbg_stat", [128, 16], F32, kind="ExternalOutput")
        dbg["xb"] = nc.dram_tensor("dbg_xb", [128, 2 * D], F32, kind="ExternalOutput")
        dbg["rh"] = nc.dram_tensor("dbg_rh", [64, QC], BF16, kind="ExternalOutput")

    pstk = ExitStack()
    with pstk:
        sbp = lambda n, s, d: pstk.enter_context(nc.sbuf_tensor(n, s, d))
        identb = sbp("identb", [128, 128], BF16)
        identf = sbp("identf", [128, 128], F32)
        onesf = sbp("onesf", [1, 128], F32)
        onesb = sbp("onesb", [1, 128], BF16)
        bo2 = sbp("bo2", [128, 2], BF16)
        selh = sbp("selh", [16, 16, 128], F32)
        brows = sbp("brows", [1, D], BF16)
        glc = sbp("glc", [128, 17], F32)
        growt = sbp("growt", [17, 128], F32)
        encb_c = sbp("encb_c", [128, 16], F32)
        s2cb_c = sbp("s2cb_c", [128, 1], F32)
        rb1_c = sbp("rb1_c", [64, 1], F32)
        eb1_c = sbp("eb1_c", [128, NE, 8], F32)
        eb2_c = sbp("eb2_c", [128, 4], F32)
        c2sb_c = sbp("c2sb_c", [128, 16], F32)
        m2r2 = sbp("m2r2", [128, NT, 2], F32)
        s12 = sbp("s12", [128, NT, 2], F32)
        cball = sbp("cball", [128, 64], F32)
        qsqm = sbp("qsqm", [2, 64], F32)
        ksqm = sbp("ksqm", [2, 8], F32)
        stat = sbp("stat", [128, 16], F32)
        cst3 = sbp("cst3", [128, 16], F32)
        rs2 = sbp("rs2", [16, QB], F32)
        rr2 = sbp("rr2", [16, QB], F32)
        lnscr = sbp("lnscr", [128, D], F32)
        lnscr2 = sbp("lnscr2", [128, D], F32)
        smrow = sbp("smrow", [64, 4], F32)
        cns = sbp("cns", [128, 4], F32)
        crow = sbp("crow", [1, 64], F32)
        crow2 = sbp("crow2", [1, 8], F32)
        abstk = ExitStack()
        kT = abstk.enter_context(nc.sbuf_tensor("kT", [128, 8, T], BF16))
        vres = abstk.enter_context(nc.sbuf_tensor("vres", [128, NT * H * 65], BF16))
        VP = NT * H * 65

        s_dma = Sem(pstk.enter_context(nc.semaphore("s_dma")))
        s_w = Sem(pstk.enter_context(nc.semaphore("s_w")))
        s_pe = Sem(pstk.enter_context(nc.semaphore("s_pe")))
        s_act = Sem(pstk.enter_context(nc.semaphore("s_act")))
        s_dve = Sem(pstk.enter_context(nc.semaphore("s_dve")))
        s_mv = Sem(pstk.enter_context(nc.semaphore("s_mv")))
        s_out = Sem(pstk.enter_context(nc.semaphore("s_out")))
        s_bar = Sem(pstk.enter_context(nc.semaphore("s_bar")))
        s_g = Sem(pstk.enter_context(nc.semaphore("s_g")))

        _cur = {"b": None, "auto": False}

        def step(**fns):
            if _cur["b"] is None:
                _cur["b"] = nc.Block()
                _cur["b"].__enter__()
            for eng in ("sync", "gpsimd", "tensor", "vector", "scalar"):
                if eng in fns:
                    getattr(_cur["b"], eng)(fns[eng])
            if _cur["auto"]:
                _cur["b"].__exit__(None, None, None)
                _cur["b"] = None

        def flush():
            if _cur["b"] is not None:
                _cur["b"].__exit__(None, None, None)
                _cur["b"] = None

        evq = {}

        iostk = ExitStack()
        iot_p = iostk.enter_context(nc.sbuf_tensor("iot_p", [128, 128], F32))
        iot_f = iostk.enter_context(nc.sbuf_tensor("iot_f", [128, 128], F32))
        p0stk = ExitStack()
        pc0 = p0stk.enter_context(nc.psum_tensor("pc0", [128, 512], F32))

        def p0_g(g):
            g.iota(iot_p[:], [[0, 128]], channel_multiplier=1,
                   allow_small_or_imprecise_dtypes=True)
            g.iota(iot_f[:], [[1, 128]], channel_multiplier=0,
                   allow_small_or_imprecise_dtypes=True)
            g.memset(onesf[:], 1.0)
            g.memset(onesb[:], 1.0)
            g.memset(bo2[0:64, 0:1], 1.0)
            g.memset(bo2[0:64, 1:2], 0.0)
            g.memset(bo2[64:128, 0:1], 0.0)
            g.memset(bo2[64:128, 1:2], 1.0)
            g.memset(ksqm[:], 0.0)
            g.memset(cns[:, 0:1], EPS)
            g.memset(cns[:, 1:2], 1e-30)
            g.memset(cns[:, 2:3], -0.5)
            g.memset(cns[:, 3:4], 0.5)
            g.iota(selh[:], [[1, 16], [0, 128]], channel_multiplier=-1,
                   allow_small_or_imprecise_dtypes=True)
            for kt_ in range(NT):
                g.memset(bass.AP(vres, kt_ * H * 65 + 64, [[VP, 128], [65, H]]), 1.0)
            def cdma(dst_ap, src_ap):
                g.dma_start(dst_ap, src_ap).then_inc(s_w.h, 16)
                s_w.n += 16

            for oc in range(16):
                cdma(wqk_pre.ap()[oc],
                     bass.AP(w_qkv, oc * 128, [[128 * 3072, 8], [3072, 128], [1, 128]]))
            for dc in range(8):
                cdma(wv_nat.ap()[dc],
                     bass.AP(w_qkv, 2048 + dc * 128 * 3072, [[3072, 128], [1, D]]))
            for dc in range(8):
                cdma(wproj_n.ap()[dc], bass.AP(w_proj, dc * 128 * D, [[D, 128], [1, D]]))
            for hc in range(16):
                cdma(enc_pre.ap()[hc],
                     bass.AP(enc_w, hc * 128, [[128 * HID, 8], [HID, 128], [1, 128]]))
            for half in range(2):
                cdma(bass.AP(s2c_pre, half * 64, [[128 * 128, 16], [128, 128], [1, 64]]),
                     bass.AP(s2c_w, 0, [[128 * MD, 16], [MD, 128], [1, 64]]))
            for pr in range(4):
                cdma(ew1_pre.ap()[pr],
                     bass.AP(e_w1, 2 * pr * MD * EH, [[128, 8], [EH, 128], [1, 128]]))
            for e in range(NE):
                cdma(ew2_pre.ap()[e],
                     bass.AP(e_w2, e * EH * MD, [[128 * MD, 8], [MD, 128], [1, 64]]))
            for half in range(2):
                cdma(bass.AP(c2s_pre, half * 64 * 128, [[128 * 128, 16], [128, 64], [1, 128]]),
                     bass.AP(c2s_w, 0, [[128, 16], [HID, 64], [1, 128]]))
            for hc in range(16):
                cdma(dec_nat.ap()[hc], bass.AP(dec_w, hc * 128 * D, [[D, 128], [1, D]]))
            cdma(bass.AP(rw_pre, 0, [[64 + NE, 64], [1, 64]]),
                 bass.AP(r_w1, 0, [[64, 64], [1, 64]]))
            cdma(bass.AP(rw_pre, 64, [[64 + NE, 64], [1, NE]]),
                 bass.AP(r_w2, 0, [[NE, 64], [1, NE]]))
            cdma(bass.AP(brow_pre, 0, [[D, 1], [1, D]]), bass.AP(b_proj, 0, [[D, 1], [1, D]]))
            cdma(bass.AP(brow_pre, D, [[D, 1], [1, D]]), bass.AP(dec_b, 0, [[D, 1], [1, D]]))
            with nc.allow_non_contiguous_dma(reason="tiny bias column loads"):
                cdma(encb_c[:], bass.AP(enc_b, 0, [[1, 128], [128, 16]]))
                for half in range(2):
                    cdma(s2cb_c[half * 64:(half + 1) * 64, :],
                         bass.AP(s2c_b, 0, [[1, 64], [64, 1]]))
                cdma(rb1_c[:], bass.AP(r_b1, 0, [[1, 64], [64, 1]]))
                cdma(eb1_c[:], bass.AP(e_b1, 0, [[1, 128], [EH, NE], [128, 8]]))
                cdma(eb2_c[:], bass.AP(e_b2, 0, [[1, 128], [128, 4]]))
                cdma(c2sb_c[:], bass.AP(c2s_b, 0, [[1, 128], [128, 16]]))
                cdma(glc[:, 0:8], bass.AP(ln_g, 0, [[1, 128], [128, 8]]))
                cdma(glc[:, 8:16], bass.AP(ln_b, 0, [[1, 128], [128, 8]]))
                cdma(glc[0:8, 16:17], bass.AP(r_b2, 0, [[1, 8], [8, 1]]))

        step(gpsimd=p0_g)
        flush()

        def p0_guard(g):
            g.drain().then_inc(s_g.h, 1)
            s_g.n += 1

        step(gpsimd=p0_guard)
        flush()
        G_CONST = s_g.n
        W_CASTS = s_w.n

        def p0_brows(sy):
            sy.wait_ge(s_w.h, W_CASTS)
            sy.wait_ge(s_g.h, G_CONST)
            sy.dma_start(brows[:], bass.AP(brow_pre, 0, [[D, 1], [1, D]]))\
              .then_inc(s_dma.h, 16)
            s_dma.n += 16

        step(sync=p0_brows)

        def p0_v(v):
            v.wait_ge(s_g.h, G_CONST)
            v.tensor_tensor(identb[:], iot_p[:], iot_f[:], ALU.is_equal)
            v.tensor_copy(identf[:], identb[:])
            v.tensor_single_scalar(selh[:], selh[:], 0.0, ALU.is_equal)
            v.drain().then_inc(s_dve.h, 1)
            s_dve.n += 1

        step(vector=p0_v)
        DVE_ID = s_dve.n

        def p0_t(t):
            t.wait_ge(s_w.h, W_CASTS)
            t.wait_ge(s_g.h, G_CONST)
            t.wait_ge(s_dve.h, DVE_ID)
            t.transpose(pc0[0:16, 0:128], glc[:, 0:16], identf[:])
            t.matmul(pc0[0:4, 480:484], onesf[0:1, 0:4], onesf[0:1, 0:4], start=True, stop=True)
            t.drain().then_inc(s_pe.h, 1)
            s_pe.n += 1

        step(tensor=p0_t)
        PE_GT = s_pe.n

        def p0_v2(v):
            v.wait_ge(s_pe.h, PE_GT)
            v.tensor_copy(growt[0:16, :], pc0[0:16, 0:128])
            v.drain().then_inc(s_dve.h, 1)
            s_dve.n += 1

        step(vector=p0_v2)

        def p0_t1b(t):
            t.wait_ge(s_dve.h, s_dve.n)
            t.transpose(pc0[0:1, 128:256], glc[:, 16:17], identf[:])
            t.matmul(pc0[0:4, 480:484], onesf[0:1, 0:4], onesf[0:1, 0:4], start=True, stop=True)
            t.drain().then_inc(s_pe.h, 1)
            s_pe.n += 1

        step(tensor=p0_t1b)
        PE_GT2 = s_pe.n

        def p0_v2b(v):
            v.wait_ge(s_pe.h, PE_GT2)
            v.tensor_copy(crow2[0:1, 0:8], pc0[0:1, 128:136])
            v.drain().then_inc(s_dve.h, 1)
            s_dve.n += 1

        step(vector=p0_v2b)
        DVE_GR = s_dve.n
        DVE_P0 = s_dve.n

        DVE_P0 = s_dve.n
        flush()
        p0stk.close()
        iostk.close()

        if _STOP == "0":
            def fin0(g):
                g.wait_ge(s_w.h, s_w.n)
                g.wait_ge(s_g.h, s_g.n)
            step(gpsimd=fin0)
            flush()
            abstk.close()
            return nc

        astk = ExitStack()
        sba = lambda n, s, d: astk.enter_context(nc.sbuf_tensor(n, s, d))
        xb = sba("xb", [128, 2, D], F32)
        xlnb = sba("xlnb", [128, D], BF16)
        xlnT = sba("xlnT", [128, 8, QB], BF16)
        wqk_sb = sba("wqk_sb", [128, 2, 8, 128], BF16)
        wv_sb = sba("wv_sb", [128, 8, D], BF16)
        qtb = sba("qtb", [128, 8, QB], BF16)
        sqscr = sba("sqscr", [128, QB], BF16)
        pqk0 = astk.enter_context(nc.psum_tensor("pqk0", [128, 512], F32))
        pqk1 = astk.enter_context(nc.psum_tensor("pqk1", [128, 512], F32))
        pv2a = astk.enter_context(nc.psum_tensor("pv2a", [128, 1024], F32))
        pv2b = astk.enter_context(nc.psum_tensor("pv2b", [128, 1024], F32))
        pnrm = astk.enter_context(nc.psum_tensor("pnrm", [2, 512], F32))
        ptp = astk.enter_context(nc.psum_tensor("ptp", [128, 640], BF16))

        _cur["auto"] = False

        def pa_wv(sy):
            sy.wait_ge(s_w.h, W_CASTS)
            sy.wait_ge(s_g.h, G_CONST)
            sy.dma_start(wv_sb[:], bass.AP(wv_nat, 0, [[D, 128], [128 * D, 8], [1, D]]))\
              .then_inc(s_dma.h, 16)
            s_dma.n += 16

        step(sync=pa_wv)
        DMA_WV = s_dma.n

        x_ld = {}
        for b in range(NB):
            ev_marks = []
            pe_tp = [0]
            ap_marks = {}
            for tt in range(4):
                slx = tt % 2

                def pa_ldx(sy, b=b, tt=tt, slx=slx):
                    prev = ap_marks.get(tt - 2, evq.get(("lnap", b - 1, tt + 2)))
                    if prev:
                        sy.wait_ge(s_dve.h, prev)
                    elif b == 0 and tt == 0:
                        sy.wait_ge(s_dve.h, DVE_P0)
                    sy.dma_start(xb[:, slx, :],
                                 bass.AP(xd, (b * 4 + tt) * 128 * D, [[D, 128], [1, D]]))\
                      .then_inc(s_dma.h, 16)
                    s_dma.n += 16

                step(sync=pa_ldx)
                x_ld[(b, tt)] = s_dma.n

                def pa_ln_v(v, b=b, tt=tt, slx=slx):
                    v.wait_ge(s_dma.h, x_ld[(b, tt)])
                    prevact = evq.get(("rstd", b - 1, tt))
                    if prevact:
                        v.wait_ge(s_act.h, prevact)
                    c0 = 4 * tt
                    v.reduce_sum(stat[:, c0:c0 + 1], xb[:, slx, :], axis=AX.X)
                    v.tensor_mul(lnscr[:], xb[:, slx, :], xb[:, slx, :])
                    v.drain()
                    v.reduce_sum(stat[:, c0 + 1:c0 + 2], lnscr[:], axis=AX.X)
                    v.drain()
                    v.tensor_scalar_mul(stat[:, c0:c0 + 1], stat[:, c0:c0 + 1], 1.0 / D)
                    v.tensor_scalar_mul(stat[:, c0 + 1:c0 + 2], stat[:, c0 + 1:c0 + 2], 1.0 / D)
                    v.drain()
                    v.tensor_mul(stat[:, c0 + 2:c0 + 3], stat[:, c0:c0 + 1], stat[:, c0:c0 + 1])
                    v.drain()
                    v.tensor_sub(stat[:, c0 + 2:c0 + 3], stat[:, c0 + 1:c0 + 2],
                                 stat[:, c0 + 2:c0 + 3])
                    v.drain().then_inc(s_dve.h, 1)
                    s_dve.n += 1

                step(vector=pa_ln_v)
                flush()
                dve_var = s_dve.n

                def pa_ln_s(s, b=b, tt=tt, dve_var=dve_var):
                    s.wait_ge(s_dve.h, dve_var)
                    c0 = 4 * tt
                    s.activation(stat[:, c0 + 3:c0 + 4], stat[:, c0 + 2:c0 + 3], ACT.Ln,
                                 bias=cns[:, 0:1], scale=1.0)
                    s.drain()
                    s.activation(stat[:, c0 + 2:c0 + 3], stat[:, c0 + 3:c0 + 4], ACT.Exp,
                                 scale=cns[:, 2:3])
                    s.drain().then_inc(s_act.h, 1)
                    s_act.n += 1

                step(scalar=pa_ln_s)
                flush()
                act_rstd = s_act.n
                evq[("rstd", b, tt)] = act_rstd

                def pa_lnap(v, b=b, tt=tt, slx=slx, act_rstd=act_rstd):
                    v.wait_ge(s_act.h, act_rstd)
                    if pe_tp[0]:
                        v.wait_ge(s_pe.h, pe_tp[0])
                    elif b > 0:
                        v.wait_ge(s_pe.h, evq[("tppe", b - 1)])
                    v.tensor_scalar(xlnb[:], xb[:, slx, :],
                                    stat[:, 4 * tt:4 * tt + 1], stat[:, 4 * tt + 2:4 * tt + 3],
                                    ALU.subtract, ALU.mult)
                    v.drain().then_inc(s_dve.h, 1)
                    s_dve.n += 1
                    ap_marks[tt] = s_dve.n
                    evq[("lnap", b, tt)] = s_dve.n

                step(vector=pa_lnap)
                dve_ap = s_dve.n

                for grp in range(2):
                    def pa_tg_t(t, b=b, tt=tt, grp=grp, dve_ap=dve_ap):
                        t.wait_ge(s_dve.h, dve_ap)
                        if tt == 0 and grp == 0 and b > 0:
                            t.wait_ge(s_dve.h, evq[("vfree", b - 1)])
                        if ev_marks:
                            t.wait_ge(s_dve.h, ev_marks[-1])
                        for i in range(4):
                            dc = grp * 4 + i
                            t.transpose(ptp[:, i * 128:i * 128 + 128],
                                        xlnb[:, dc * 128:dc * 128 + 128], identb[:])
                        t.matmul(ptp[:].bitcast(F32)[0:4, 256:260], onesb[0:1, 0:4],
                                 onesb[0:1, 0:4],
                                 start=True, stop=True)
                        t.drain().then_inc(s_pe.h, 1)
                        s_pe.n += 1
                        pe_tp[0] = s_pe.n

                    step(tensor=pa_tg_t)
                    pem = s_pe.n

                    def pa_tg_v(v, b=b, tt=tt, grp=grp, pem=pem):
                        v.wait_ge(s_pe.h, pem)
                        for i in range(4):
                            dc = grp * 4 + i
                            v.tensor_copy(xlnT[:, dc, tt * 128:tt * 128 + 128],
                                          ptp[:, i * 128:i * 128 + 128])
                        v.drain().then_inc(s_dve.h, 1)
                        s_dve.n += 1
                        ev_marks.append(s_dve.n)

                    step(vector=pa_tg_v)
            evq[("tppe", b)] = s_pe.n
            evq[("tpdone", b)] = s_dve.n

            qk_ev = {}
            for oc in range(16):
                slot = oc % 2

                def pa_qk_ld(sy, b=b, oc=oc, slot=slot):
                    if b == 0 and oc <= 1:
                        sy.wait_ge(s_w.h, W_CASTS)
                        sy.wait_ge(s_g.h, G_CONST)
                    prev = qk_ev.get(oc - 2, evq.get(("qkev", b - 1, 14 + slot)))
                    if prev:
                        sy.wait_ge(s_dve.h, prev)
                    sy.dma_start(wqk_sb[:, slot, :, :],
                                 bass.AP(wqk_pre, oc * 8 * 128 * 128,
                                         [[128, 128], [128 * 128, 8], [1, 128]]))\
                      .then_inc(s_dma.h, 16)
                    s_dma.n += 16

                step(sync=pa_qk_ld)
                qk_ld = s_dma.n

                def pa_qk_t(t, b=b, oc=oc, slot=slot, qk_ld=qk_ld):
                    t.wait_ge(s_dma.h, qk_ld)
                    t.wait_ge(s_dve.h, evq[("tpdone", b)])
                    prev = qk_ev.get(oc - 2)
                    if prev:
                        t.wait_ge(s_dve.h, prev)
                    ppsum = pqk0 if slot == 0 else pqk1
                    for dc in range(8):
                        t.matmul(ppsum[:], wqk_sb[:, slot, dc, :], xlnT[:, dc, :],
                                 start=(dc == 0), stop=(dc == 7))
                    t.drain().then_inc(s_pe.h, 1)
                    s_pe.n += 1

                step(tensor=pa_qk_t)
                pe_qk = s_pe.n

                def pa_qk_v(v, b=b, oc=oc, slot=slot, pe_qk=pe_qk):
                    v.wait_ge(s_pe.h, pe_qk)
                    ppsum = pqk0 if slot == 0 else pqk1
                    if oc < 8:
                        if b > 0 and oc == 0:
                            v.wait_ge(s_mv.h, evq[("qst", b - 1)])
                        v.tensor_scalar_mul(qtb[:, oc, :], ppsum[:], 0.125)
                        v.drain()
                        v.tensor_mul(sqscr[:], qtb[:, oc, :], qtb[:, oc, :])
                    else:
                        c = oc - 8
                        v.tensor_copy(kT[:, c, b * QB:(b + 1) * QB], ppsum[:])
                        v.drain()
                        v.tensor_mul(sqscr[:], kT[:, c, b * QB:(b + 1) * QB],
                                     kT[:, c, b * QB:(b + 1) * QB])
                    v.drain().then_inc(s_dve.h, 1)
                    s_dve.n += 1

                step(vector=pa_qk_v)
                dve_sq = s_dve.n

                def pa_nrm_t(t, b=b, oc=oc, dve_sq=dve_sq):
                    t.wait_ge(s_dve.h, dve_sq)
                    t.matmul(pnrm[:], bo2[:], sqscr[:], start=True, stop=True)
                    t.drain().then_inc(s_pe.h, 1)
                    s_pe.n += 1

                step(tensor=pa_nrm_t)
                pe_nrm = s_pe.n

                def pa_nrm_v(v, b=b, oc=oc, pe_nrm=pe_nrm):
                    v.wait_ge(s_pe.h, pe_nrm)
                    if oc < 8:
                        v.reduce_max(qsqm[0:2, oc * 8 + b:oc * 8 + b + 1], pnrm[:], axis=AX.X)
                    else:
                        c = oc - 8
                        v.reduce_max(cst3[0:2, 15:16], pnrm[:], axis=AX.X)
                        v.drain()
                        v.tensor_tensor(ksqm[0:2, c:c + 1], ksqm[0:2, c:c + 1],
                                        cst3[0:2, 15:16], ALU.max)
                    v.drain().then_inc(s_dve.h, 1)
                    s_dve.n += 1

                step(vector=pa_nrm_v)
                qk_ev[oc] = s_dve.n
                evq[("qkev", b, oc)] = s_dve.n

            def pa_qst(g, b=b, after=qk_ev[7]):
                g.wait_ge(s_dve.h, qk_ev[7])
                if b > 0:
                    pass
                g.dma_start(bass.AP(qts, b * QB, [[T, 128], [128 * T, 8], [1, QB]]), qtb[:])\
                  .then_inc(s_mv.h, 16)
                s_mv.n += 16

            step(gpsimd=pa_qst)
            evq[("qst", b)] = s_mv.n

            v_ev = {}
            for tt in range(4):
                ppsum = pv2a if tt % 2 == 0 else pv2b

                def pa_v_t(t, b=b, tt=tt, ppsum=ppsum):
                    if tt == 0:
                        t.wait_ge(s_dma.h, DMA_WV)
                        t.wait_ge(s_dve.h, evq[("tpdone", b)])
                    prev = v_ev.get(tt - 2, evq.get(("vev", b - 1, tt + 2)))
                    if prev:
                        t.wait_ge(s_dve.h, prev)
                    for dc in range(8):
                        lhs = xlnT[:, dc, tt * 128:tt * 128 + 128]
                        t.matmul(ppsum[:, 0:512], lhs, wv_sb[:, dc, 0:512],
                                 start=(dc == 0), stop=(dc == 7))
                        t.matmul(ppsum[:, 512:1024], lhs, wv_sb[:, dc, 512:1024],
                                 start=(dc == 0), stop=(dc == 7))
                    t.drain().then_inc(s_pe.h, 1)
                    s_pe.n += 1

                step(tensor=pa_v_t)
                pe_v = s_pe.n

                def pa_v_v(v, b=b, tt=tt, ppsum=ppsum, pe_v=pe_v):
                    v.wait_ge(s_pe.h, pe_v)
                    kt_ = b * 4 + tt
                    v.tensor_copy(bass.AP(vres, kt_ * H * 65, [[VP, 128], [65, H], [1, 64]]),
                                  ppsum[:])
                    v.drain().then_inc(s_dve.h, 1)
                    s_dve.n += 1

                step(vector=pa_v_v)
                v_ev[tt] = s_dve.n
                evq[("vev", b, tt)] = s_dve.n
            evq[("vfree", b)] = s_dve.n
            evq[("tpvA", b)] = ev_marks[-1]
            flush()

            def pa_qguard(v, b=b):
                pass

            def pa_qg2(sy, b=b):
                sy.wait_ge(s_mv.h, evq[("qst", b)])
                sy.nop().then_inc(s_dma.h, 0) if False else None

            evq[("qstore", b)] = evq[("qst", b)]

        if _DEBUG:
            def dbg_pa(g):
                g.wait_ge(s_dve.h, s_dve.n)
                g.dma_start(dbg["xlnT"].ap(), xlnT[:]).then_inc(s_mv.h, 16)
                g.dma_start(dbg["wqk"].ap(), wqk_sb[:]).then_inc(s_mv.h, 16)
                g.dma_start(dbg["stat"].ap(), stat[:]).then_inc(s_mv.h, 16)
                g.dma_start(dbg["xb"].ap(), xb[:]).then_inc(s_mv.h, 16)
                s_mv.n += 64

            step(gpsimd=dbg_pa)

        def pcb_v1(v):
            v.wait_ge(s_dve.h, s_dve.n)
            for c in range(8):
                v.tensor_scalar(qsqm[0:2, c * 8:c * 8 + 8], qsqm[0:2, c * 8:c * 8 + 8],
                                ksqm[0:2, c:c + 1], None, ALU.mult)
            v.drain().then_inc(s_dve.h, 1)
            s_dve.n += 1

        step(vector=pcb_v1)
        dve_prod = s_dve.n

        def pcb_t1(t):
            t.wait_ge(s_dve.h, dve_prod)
            t.transpose(pqk0[0:64, 0:2], qsqm[0:2, 0:64], identf[0:2, 0:2])
            t.matmul(pqk0[0:4, 480:484], onesf[0:1, 0:4], onesf[0:1, 0:4], start=True, stop=True)
            t.drain().then_inc(s_pe.h, 1)
            s_pe.n += 1

        step(tensor=pcb_t1)
        pe_cp = s_pe.n

        def pcb_v2(v, pe_cp=pe_cp):
            v.wait_ge(s_pe.h, pe_cp)
            v.tensor_copy(smrow[:, 0:2], pqk0[0:64, 0:2])
            v.drain()
            v.tensor_tensor(smrow[:, 2:3], smrow[:, 0:1], smrow[:, 1:2], ALU.max)
            v.drain().then_inc(s_dve.h, 1)
            s_dve.n += 1

        step(vector=pcb_v2)
        dve_pm = s_dve.n

        def pcb_s1(s, dve_pm=dve_pm):
            s.wait_ge(s_dve.h, dve_pm)
            s.activation(smrow[:, 3:4], smrow[:, 2:3], ACT.Ln, bias=cns[0:64, 1:2], scale=1.0)
            s.drain()
            s.activation(smrow[:, 2:3], smrow[:, 3:4], ACT.Exp, scale=cns[0:64, 3:4])
            s.drain().then_inc(s_act.h, 1)
            s_act.n += 1

        step(scalar=pcb_s1)
        act_c = s_act.n

        def pcb_v3(v, act_c=act_c):
            v.wait_ge(s_act.h, act_c)
            v.tensor_scalar_mul(smrow[:, 1:2], smrow[:, 2:3], -1.0)
            v.drain().then_inc(s_dve.h, 1)
            s_dve.n += 1

        step(vector=pcb_v3)
        dve_cn = s_dve.n

        def pcb_t2(t, dve_cn=dve_cn):
            t.wait_ge(s_dve.h, dve_cn)
            t.transpose(pqk1[0:1, 0:64], smrow[:, 1:2], identf[0:64, 0:64])
            t.matmul(pqk1[0:4, 480:484], onesf[0:1, 0:4], onesf[0:1, 0:4], start=True, stop=True)
            t.drain().then_inc(s_pe.h, 1)
            s_pe.n += 1

        step(tensor=pcb_t2)
        pe_cr = s_pe.n

        def pcb_v4(v, pe_cr=pe_cr):
            v.wait_ge(s_pe.h, pe_cr)
            v.tensor_copy(crow[:], pqk1[0:1, 0:64])
            v.drain().then_inc(s_dve.h, 1)
            s_dve.n += 1

        step(vector=pcb_v4)
        dve_cr = s_dve.n

        def pcb_t3(t, dve_cr=dve_cr):
            t.wait_ge(s_dve.h, dve_cr)
            t.matmul(pqk0[:, 0:64], onesf[:], crow[:], start=True, stop=True)
            t.drain().then_inc(s_pe.h, 1)
            s_pe.n += 1

        step(tensor=pcb_t3)
        pe_cb = s_pe.n

        def pcb_v5(v, pe_cb=pe_cb):
            v.wait_ge(s_pe.h, pe_cb)
            v.tensor_copy(cball[:], pqk0[:, 0:64])
            v.drain().then_inc(s_dve.h, 1)
            s_dve.n += 1

        step(vector=pcb_v5)
        flush()
        _cur["auto"] = False
        DVE_CB = s_dve.n

        if _DEBUG:
            def dbg_a(g):
                g.wait_ge(s_dve.h, DVE_CB)
                g.dma_start(dbg["kt"].ap(), bass.AP(kT, 0, [[8 * T, 128], [1, 8 * T]]))\
                 .then_inc(s_mv.h, 16)
                g.dma_start(dbg["v"].ap(), vres[:]).then_inc(s_mv.h, 16)
                g.dma_start(dbg["qt"].ap(), qts.ap()).then_inc(s_mv.h, 16)
                g.dma_start(dbg["cb"].ap(), cball[:]).then_inc(s_mv.h, 16)
                s_mv.n += 64

            step(gpsimd=dbg_a)

        tgt = s_bar.n + 5

        def mkbar(e):
            e.nop().then_inc(s_bar.h, 1)
            e.wait_ge(s_bar.h, tgt)

        step(sync=mkbar, gpsimd=mkbar, tensor=mkbar, vector=mkbar, scalar=mkbar)
        flush()
        s_bar.n += 5
        astk.close()

        if _STOP == "A":
            def finA(g):
                g.wait_ge(s_mv.h, s_mv.n)
            step(gpsimd=finA)
            flush()
            astk.close()
            abstk.close()
            return nc

        bstk = ExitStack()
        sbb = lambda n, s, d: bstk.enter_context(nc.sbuf_tensor(n, s, d))
        qt_buf = sbb("qt_buf", [128, 8, QB], BF16)
        pbuf = sbb("pbuf", [128, 3, 1024], BF16)
        attnn = sbb("attnn", [128, 8, QB], BF16)
        stgB = sbb("stgB", [64, QB], BF16)
        rsl = sbb("rsl", [128, 2, 512], F32)
        xb1 = sbb("xb1", [128, D], F32)
        x2b = sbb("x2b", [128, D], F32)
        wproj_sb = sbb("wproj_sb", [128, 8, D], BF16)
        psS0 = bstk.enter_context(nc.psum_tensor("psS0", [128, 1024], F32))
        psS1 = bstk.enter_context(nc.psum_tensor("psS1", [128, 1024], F32))
        pvA = bstk.enter_context(nc.psum_tensor("pvA", [128, 512], F32))
        pvB = bstk.enter_context(nc.psum_tensor("pvB", [128, 512], F32))
        pbc = bstk.enter_context(nc.psum_tensor("pbc", [128, 1024], F32))

        def pb_wp(sy):
            sy.wait_ge(s_w.h, W_CASTS)
            sy.dma_start(wproj_sb[:], bass.AP(wproj_n, 0, [[D, 128], [128 * D, 8], [1, D]]))\
              .then_inc(s_dma.h, 16)
            s_dma.n += 16

        step(sync=pb_wp)
        DMA_WP = s_dma.n

        expm = {}
        pvm = {}
        spm = {}
        pvdr = {}

        def emit_spair(t, b, hp, kc):
            sp = (psS0, psS1, pbc)[kc % 3]
            t.matmul(sp[:, 0:512], kT[0:64, hp, kc * 128:kc * 128 + 128],
                     qt_buf[0:64, hp, :], start=True, stop=True)
            t.matmul(sp[:, 512:1024],
                     kT[64:128, hp, kc * 128:kc * 128 + 128],
                     qt_buf[64:128, hp, :], start=True,
                     stop=True).then_inc(s_pe.h, 1)
            s_pe.n += 1
            spm[(b, hp, kc)] = s_pe.n

        for b in range(NB if _STOP not in ("Battn", "Bnrm", "B1") else 1):
            def pb_qld(sy, b=b):
                if b > 0:
                    sy.wait_ge(s_pe.h, evq[("battn", b - 1)])
                sy.dma_start(qt_buf[:], bass.AP(qts, b * QB, [[T, 128], [128 * T, 8], [1, QB]]))\
                  .then_inc(s_dma.h, 16)
                s_dma.n += 16

            step(sync=pb_qld)
            qld = s_dma.n

            base_act_b = s_act.n
            for hp in range(_HPN):
                for kc in range(NT):
                    expm[(b, hp, kc)] = base_act_b + hp * NT + kc + 1

            for hp in range(_HPN):
                def t_attn(t, b=b, hp=hp, qld=qld):
                    hoisted = hp > 0
                    for kc in range(NT + 1):
                        if kc < NT and not (hoisted and kc < 2):
                            if kc == 0 and hp == 0:
                                t.wait_ge(s_dma.h, qld)
                                if b == 0:
                                    t.wait_ge(s_dve.h, DVE_CB)
                                if b > 0:
                                    t.wait_ge(s_dve.h, evq[("x2ev", b - 1, 3)])
                            prev_exp = expm.get((b, hp, kc - 3),
                                                expm.get((b, hp - 1, NT - 3 + kc),
                                                         expm.get((b - 1, 7, NT - 3 + kc))))
                            if prev_exp:
                                t.wait_ge(s_act.h, prev_exp)
                            emit_spair(t, b, hp, kc)
                        if kc >= 1:
                            kcp = kc - 1
                            hoist_now = (kcp == NT - 1 and hp + 1 < _HPN)
                            if hoist_now:
                                for j in (0, 1):
                                    t.wait_ge(s_act.h, expm[(b, hp, NT - 2 + j)])
                                    emit_spair(t, b, hp + 1, j)
                            else:
                                t.wait_ge(s_act.h, expm[(b, hp, kcp)])
                            if kcp == 0:
                                prev_ev = evq.get(("pvev", b, hp - 1),
                                                  evq.get(("pvev", b - 1, 7)))
                                if prev_ev:
                                    t.wait_ge(s_dve.h, prev_ev)
                                if hp == 0 and b > 0:
                                    t.wait_ge(s_mv.h, evq[("mvBall", b - 1)])
                                    t.wait_ge(s_pe.h, evq[("pjdone", b - 1)])
                            h0, h1 = 2 * hp, 2 * hp + 1
                            t.matmul(pvA[0:65, :],
                                     bass.AP(vres, kcp * H * 65 + h0 * 65, [[VP, 128], [1, 65]]),
                                     pbuf[:, kcp % 3, 0:512],
                                     start=(kcp == 0), stop=(kcp == NT - 1))
                            mpv = t.matmul(pvB[0:65, :],
                                           bass.AP(vres, kcp * H * 65 + h1 * 65,
                                                   [[VP, 128], [1, 65]]),
                                           pbuf[:, kcp % 3, 512:1024],
                                           start=(kcp == 0), stop=(kcp == NT - 1))
                            mpv.then_inc(s_pe.h, 1)
                            s_pe.n += 1
                            pvm[(b, hp, kcp)] = s_pe.n
                            if kcp == NT - 1:
                                t.drain().then_inc(s_pe.h, 1)
                                s_pe.n += 1
                                pvdr[(b, hp)] = s_pe.n

                step(tensor=t_attn)
                pvdone = pvdr[(b, hp)]
                slr = hp % 2

                def pb_rs_v(v, b=b, hp=hp, pvdone=pvdone, slr=slr):
                    v.wait_ge(s_pe.h, pvdone)
                    prev_rsd = evq.get(("rsd", b, hp - 1), evq.get(("rsd", b - 1, 7)))
                    if prev_rsd:
                        v.wait_ge(s_dma.h, prev_rsd)
                    v.tensor_copy(rsl[64:65, 0, :], pvA[64:65, :])
                    v.tensor_copy(rsl[64:65, 1, :], pvB[64:65, :])
                    prev_mv = evq.get(("mvB", b, hp - 1), evq.get(("mvB", b - 1, 7)))
                    if prev_mv:
                        v.wait_ge(s_mv.h, prev_mv)
                    v.tensor_copy(attnn[0:64, hp, :], pvA[0:64, :])
                    v.tensor_copy(stgB[0:64, :], pvB[0:64, :]).then_inc(s_dve.h, 1)
                    s_dve.n += 1

                step(vector=pb_rs_v)
                evq[("pvev", b, hp)] = s_dve.n
                dve_rs = s_dve.n

                def pb_rs_d(sy, b=b, hp=hp, slr=slr, dve_rs=dve_rs):
                    sy.wait_ge(s_dve.h, dve_rs)
                    sy.dma_start(rs2[2 * hp:2 * hp + 2, :], rsl[64:65, :, :])\
                      .then_inc(s_dma.h, 16)
                    s_dma.n += 16
                    evq[("rsd", b, hp)] = s_dma.n

                if not _NORSD:
                    step(sync=pb_rs_d)
                else:
                    evq[("rsd", b, hp)] = 0

                def pb_mv(g, b=b, hp=hp):
                    g.wait_ge(s_dve.h, evq[("pvev", b, hp)])
                    g.dma_start(attnn[64:128, hp, :], stgB[:]).then_inc(s_mv.h, 16)
                    s_mv.n += 16
                    evq[("mvB", b, hp)] = s_mv.n

                if not _NOMV:
                    step(gpsimd=pb_mv)
                else:
                    evq[("mvB", b, hp)] = 0

            def s_attn_b(s, b=b):
                for hp in range(_HPN):
                    for kc in range(NT):
                        w = spm[(b, hp, kc)]
                        if kc >= 3:
                            pvreq = pvm.get((b, hp, kc - 3), 0)
                        else:
                            kstar = (30, 31, 29)[kc]
                            pvreq = pvm.get((b, hp - 1, kstar),
                                            pvm.get((b - 1, 7, kstar), 0)) or 0
                        s.wait_ge(s_pe.h, max(w, pvreq))
                        sp = (psS0, psS1, pbc)[kc % 3]
                        s.activation(pbuf[:, kc % 3, :], sp[:], ACT.Exp,
                                     bias=cball[:, hp * 8 + b:hp * 8 + b + 1],
                                     scale=1.0).then_inc(s_act.h, 1)
                        s_act.n += 1
                        assert expm[(b, hp, kc)] == s_act.n

            step(scalar=s_attn_b)

            evq[("battn", b)] = spm[(b, _HPN - 1, NT - 1)]
            evq[("mvBall", b)] = evq.get(("mvB", b, _HPN - 1), 0)
            if _STOP == "Battn":
                break

            def pb_rec(v, b=b):
                v.wait_ge(s_dma.h, evq[("rsd", b, 7)])
                v.reciprocal(rr2[:], rs2[:])
                v.drain().then_inc(s_dve.h, 1)
                s_dve.n += 1

            step(vector=pb_rec)
            dve_rec = s_dve.n

            nrm_marks = []
            for h in range(16):
                slb = h % 2

                def pb_nb_t(t, b=b, h=h, slb=slb, dve_rec=dve_rec):
                    if h == 0:
                        t.wait_ge(s_dve.h, dve_rec)
                        t.wait_ge(s_mv.h, evq[("mvBall", b)])
                    if h >= 2:
                        t.wait_ge(s_dve.h, nrm_marks[h - 2])
                    t.matmul(pbc[:, slb * 512:slb * 512 + 512], selh[:, h, :], rr2[:],
                             start=True, stop=True).then_inc(s_pe.h, 1)
                    s_pe.n += 1

                step(tensor=pb_nb_t)
                pe_nb = s_pe.n

                def pb_nb_v(v, b=b, h=h, slb=slb, pe_nb=pe_nb):
                    v.wait_ge(s_pe.h, pe_nb)
                    hp, par = divmod(h, 2)
                    rsel = slice(0, 64) if par == 0 else slice(64, 128)
                    v.tensor_tensor(attnn[rsel, hp, :], attnn[rsel, hp, :],
                                    pbc[rsel, slb * 512:slb * 512 + 512], ALU.mult)
                    v.drain().then_inc(s_dve.h, 1)
                    s_dve.n += 1
                    nrm_marks.append(s_dve.n)

                step(vector=pb_nb_v)
            evq[("anev", b)] = s_dve.n
            if _STOP == "Bnrm":
                break

            if _DEBUG and b == 0:
                def dbg_an(g, upto=s_dve.n):
                    g.wait_ge(s_dve.h, upto)
                    g.dma_start(dbg["an"].ap(), attnn[:]).then_inc(s_mv.h, 16)
                    s_mv.n += 16

                step(gpsimd=dbg_an)

            for tt in range(4):
                slx = tt % 2

                def pb_xld(sy, b=b, tt=tt, slx=slx):
                    prev = evq.get(("x2ev", b, tt - 1), evq.get(("x2ev", b - 1, 3)))
                    if prev:
                        sy.wait_ge(s_dve.h, prev)
                    sy.dma_start(xb1[:],
                                 bass.AP(xd, (b * 4 + tt) * 128 * D, [[D, 128], [1, D]]))\
                      .then_inc(s_dma.h, 16)
                    s_dma.n += 16

                step(sync=pb_xld)
                x1ld = s_dma.n

                def pb_pj_t(t, b=b, tt=tt):
                    if tt == 0:
                        t.wait_ge(s_dve.h, evq[("anev", b)])
                        t.wait_ge(s_dma.h, DMA_WP)
                    prev = evq.get(("x2ev", b, tt - 1), evq.get(("x2ev", b - 1, 3)))
                    if prev:
                        t.wait_ge(s_dve.h, prev)
                    for c in range(8):
                        lhs = attnn[:, c, tt * 128:tt * 128 + 128]
                        t.matmul(pbc[:, 0:512], lhs, wproj_sb[:, c, 0:512],
                                 start=(c == 0), stop=False)
                        t.matmul(pbc[:, 512:1024], lhs, wproj_sb[:, c, 512:1024],
                                 start=(c == 0), stop=False)
                    t.matmul(pbc[:, 0:512], onesb[:], brows[0:1, 0:512], start=False, stop=True)
                    t.matmul(pbc[:, 512:1024], onesb[:], brows[0:1, 512:1024],
                             start=False, stop=True)
                    t.drain().then_inc(s_pe.h, 1)
                    s_pe.n += 1

                step(tensor=pb_pj_t)
                pe_pj = s_pe.n

                def pb_pj_v(v, b=b, tt=tt, slx=slx, pe_pj=pe_pj, x1ld=x1ld):
                    v.wait_ge(s_pe.h, pe_pj)
                    v.wait_ge(s_dma.h, x1ld)
                    prev_st = evq.get(("x2st", b, tt - 1), evq.get(("x2st", b - 1, 3)))
                    if prev_st:
                        v.wait_ge(s_mv.h, prev_st)
                    v.tensor_tensor(x2b[:], pbc[:], xb1[:], ALU.add)
                    v.drain()
                    nt_ = b * 4 + tt
                    v.reduce_sum(s12[:, nt_, 0:1], x2b[:], axis=AX.X)
                    v.tensor_mul(lnscr[:], x2b[:], x2b[:])
                    v.drain()
                    v.reduce_sum(s12[:, nt_, 1:2], lnscr[:], axis=AX.X)
                    v.drain().then_inc(s_dve.h, 1)
                    s_dve.n += 1

                step(vector=pb_pj_v)
                evq[("x2ev", b, tt)] = s_dve.n
                dve_x2 = s_dve.n

                def pb_x2st(g, b=b, tt=tt, slx=slx, dve_x2=dve_x2):
                    g.wait_ge(s_dve.h, dve_x2)
                    g.dma_start(bass.AP(x2s, (b * 4 + tt) * 128 * D, [[D, 128], [1, D]]),
                                x2b[:]).then_inc(s_mv.h, 16)
                    s_mv.n += 16
                    evq[("x2st", b, tt)] = s_mv.n

                step(gpsimd=pb_x2st)
            evq[("pjdone", b)] = s_pe.n
            if _STOP == "B1":
                break

        if _STOP in ("Battn", "Bnrm", "B1"):
            def finBx(g):
                g.wait_ge(s_mv.h, s_mv.n)
                g.wait_ge(s_dve.h, s_dve.n)
                g.wait_ge(s_act.h, s_act.n)
            step(gpsimd=finBx)
            flush()
            bstk.close()
            abstk.close()
            return nc

        def pb_fin_v(v):
            v.wait_ge(s_dve.h, s_dve.n)
            v.tensor_scalar_mul(m2r2[:, :, 0], s12[:, :, 0], 1.0 / D)
            v.tensor_scalar_mul(lnscr[:, 0:NT], s12[:, :, 1], 1.0 / D)
            v.drain()
            v.tensor_mul(lnscr2[:, 0:NT], m2r2[:, :, 0], m2r2[:, :, 0])
            v.drain()
            v.tensor_sub(lnscr[:, 0:NT], lnscr[:, 0:NT], lnscr2[:, 0:NT])
            v.drain().then_inc(s_dve.h, 1)
            s_dve.n += 1

        step(vector=pb_fin_v)
        dve_var2 = s_dve.n

        def pb_fin_s(s, dve_var2=dve_var2):
            s.wait_ge(s_dve.h, dve_var2)
            s.activation(lnscr2[:, 0:NT], lnscr[:, 0:NT], ACT.Ln, bias=cns[:, 0:1], scale=1.0)
            s.drain()
            s.activation(lnscr[:, 0:NT], lnscr2[:, 0:NT], ACT.Exp, scale=cns[:, 2:3])
            s.drain().then_inc(s_act.h, 1)
            s_act.n += 1

        step(scalar=pb_fin_s)
        act_r2 = s_act.n

        def pb_fin_v2(v, act_r2=act_r2):
            v.wait_ge(s_act.h, act_r2)
            v.tensor_copy(m2r2[:, :, 1], lnscr[:, 0:NT])
            v.drain().then_inc(s_dve.h, 1)
            s_dve.n += 1

        step(vector=pb_fin_v2)
        flush()

        if _DEBUG:
            def dbg_b(g):
                g.wait_ge(s_mv.h, s_mv.n)
                g.dma_start(dbg["x2"].ap(), x2s.ap()).then_inc(s_mv.h, 16)
                s_mv.n += 16

            step(gpsimd=dbg_b)

        tgt2 = s_bar.n + 5

        def mkbar2(e):
            e.nop().then_inc(s_bar.h, 1)
            e.wait_ge(s_bar.h, tgt2)

        step(sync=mkbar2, gpsimd=mkbar2, tensor=mkbar2, vector=mkbar2, scalar=mkbar2)
        flush()
        s_bar.n += 5
        bstk.close()
        abstk.close()

        if _STOP == "B":
            def finB(g):
                g.wait_ge(s_mv.h, s_mv.n)
            step(gpsimd=finB)
            return nc

        cstk = ExitStack()
        sbc = lambda n, s, d: cstk.enter_context(nc.sbuf_tensor(n, s, d))
        enc_sb = sbc("enc_sb", [128, 16, 8, 128], BF16)
        dec_sb = sbc("dec_sb", [128, 16, D], BF16)
        s2c_sb = sbc("s2c_sb", [128, 16, 128], BF16)
        c2s_sb = sbc("c2s_sb", [128, 16, 128], BF16)
        ew1_sb = sbc("ew1_sb", [128, 4, 8, 128], BF16)
        ew2_sb = sbc("ew2_sb", [128, NE, 8, 64], BF16)
        rw_sb = sbc("rw_sb", [64, 64 + NE], BF16)
        x2c = sbc("x2c", [128, 2, 2, D], F32)
        xlnc = sbc("xlnc", [128, D], BF16)
        x2lT = sbc("x2lT", [128, 8, QC], BF16)
        spk = sbc("spk", [128, 16, QC], BF16)
        cont = sbc("cont", [128, QC], BF16)
        rh = sbc("rh", [64, QC], BF16)
        lgt = sbc("lgt", [128, 2, NE], F32)
        msk = sbc("msk", [128, NE], F32)
        cmb = sbc("cmb", [128, 2, NE], F32)
        msk2 = sbc("msk2", [128, 2, NE], F32)
        d12 = sbc("d12", [128, 4], F32)
        comb_t = sbc("comb_t", [8, QC], F32)
        ehs0 = sbc("ehs0", [128, 8, QC], BF16)
        ehs1 = sbc("ehs1", [128, 8, QC], BF16)
        eov = sbc("eov", [128, QC], F32)
        bcv = sbc("bcv", [128, QC], F32)
        ctr = sbc("ctr", [128, QC], F32)
        moe = sbc("moe", [128, QC], F32)
        moeb = sbc("moeb", [128, QC], BF16)
        rts = sbc("rts", [128, 16, QC], BF16)
        dcb = sbc("dcb", [128, 2, D], F32)
        outb = sbc("outb", [128, 2, D], F32)
        encscr = sbc("encscr", [128, 2, QC], F32)
        decscr = sbc("decscr", [128, D], F32)
        pE0 = cstk.enter_context(nc.psum_tensor("pE0", [128, 512], F32))
        pE1 = cstk.enter_context(nc.psum_tensor("pE1", [128, 512], F32))
        pC2 = cstk.enter_context(nc.psum_tensor("pC2", [128, 512], F32))
        pD = cstk.enter_context(nc.psum_tensor("pD", [128, 1024], F32))
        pM = cstk.enter_context(nc.psum_tensor("pM", [128, 512], F32))
        pTt = cstk.enter_context(nc.psum_tensor("pTt", [128, 640], BF16))
        selpr = sbc("selpr", [8, 4, 128], F32)
        gbc = sbc("gbc", [128, D], F32)
        bbc = sbc("bbc", [128, D], F32)
        rb2bc = sbc("rb2bc", [128, NE], F32)
        brow2 = sbc("brow2", [1, D], BF16)

        def pcw(sy):
            sy.wait_ge(s_w.h, W_CASTS)
            sy.dma_start(brow2[:], bass.AP(brow_pre, D, [[D, 1], [1, D]])).then_inc(s_dma.h, 16)
            s_dma.n += 16
            sy.dma_start(enc_sb[:], bass.AP(enc_pre, 0, [[128, 128], [8 * 128 * 128, 16],
                                                         [128 * 128, 8], [1, 128]]))\
              .then_inc(s_dma.h, 16)
            sy.dma_start(dec_sb[:], bass.AP(dec_nat, 0, [[D, 128], [128 * D, 16], [1, D]]))\
              .then_inc(s_dma.h, 16)
            sy.dma_start(s2c_sb[:], bass.AP(s2c_pre, 0, [[128, 128], [128 * 128, 16], [1, 128]]))\
              .then_inc(s_dma.h, 16)
            sy.dma_start(c2s_sb[:], bass.AP(c2s_pre, 0, [[128, 128], [128 * 128, 16], [1, 128]]))\
              .then_inc(s_dma.h, 16)
            sy.dma_start(ew1_sb[:], bass.AP(ew1_pre, 0, [[128, 128], [8 * 128 * 128, 4],
                                                         [128 * 128, 8], [1, 128]]))\
              .then_inc(s_dma.h, 16)
            sy.dma_start(ew2_sb[:], bass.AP(ew2_pre, 0, [[64, 128], [8 * 128 * 64, NE],
                                                         [128 * 64, 8], [1, 64]]))\
              .then_inc(s_dma.h, 16)
            sy.dma_start(rw_sb[:], bass.AP(rw_pre, 0, [[64 + NE, 64], [1, 64 + NE]]))\
              .then_inc(s_dma.h, 16)
            s_dma.n += 7 * 16

        step(sync=pcw)
        DMA_FW = s_dma.n

        def pc_selpr(g):
            g.iota(selpr[:], [[2, 4], [1, 2], [0, 64]], channel_multiplier=-1,
                   allow_small_or_imprecise_dtypes=True)

        step(gpsimd=pc_selpr)
        flush()

        def pc_selprg(g):
            g.drain().then_inc(s_g.h, 1)
            s_g.n += 1

        step(gpsimd=pc_selprg)
        flush()
        SELPR_I = s_g.n

        def pc_selpr2(v):
            v.wait_ge(s_g.h, SELPR_I)
            v.tensor_single_scalar(selpr[:], selpr[:], 0.0, ALU.is_equal)
            v.drain().then_inc(s_dve.h, 1)
            s_dve.n += 1

        step(vector=pc_selpr2)
        SELPR_OK = s_dve.n

        bc_specs = [(gbc, 0, 0), (gbc, 512, 4), (bbc, 0, 8), (bbc, 512, 12)]
        prev_copy = [0]
        for (dstt, off, r0) in bc_specs:
            def p0_bt(t, r0=r0):
                if prev_copy[0]:
                    t.wait_ge(s_dve.h, prev_copy[0])
                for i in range(4):
                    t.matmul(pD[:, i * 128:i * 128 + 128], selh[:, r0 + i, :],
                             growt[0:16, :], start=True, stop=True)
                t.drain().then_inc(s_pe.h, 1)
                s_pe.n += 1

            step(tensor=p0_bt)
            pem = s_pe.n

            def p0_bv(v, dstt=dstt, off=off, pem=pem):
                v.wait_ge(s_pe.h, pem)
                v.tensor_copy(dstt[:, off:off + 512], pD[:, 0:512])
                v.drain().then_inc(s_dve.h, 1)
                s_dve.n += 1

            step(vector=p0_bv)
            prev_copy[0] = s_dve.n

        def p0_rb2t(t):
            t.wait_ge(s_dve.h, prev_copy[0])
            t.matmul(pD[:, 0:8], onesf[:], crow2[0:1, 0:8], start=True, stop=True)
            t.drain().then_inc(s_pe.h, 1)
            s_pe.n += 1

        step(tensor=p0_rb2t)
        PE_RB2 = s_pe.n

        def p0_rb2v(v):
            v.wait_ge(s_pe.h, PE_RB2)
            v.tensor_copy(rb2bc[:], pD[:, 0:8])
            v.drain().then_inc(s_dve.h, 1)
            s_dve.n += 1

        step(vector=p0_rb2v)
        BCDONE = s_dve.n

        for j in range(NCB if _STOP != "C1" else 1):
            slj = j % 2

            def pc_x2ld(sy, j=j, slj=slj):
                tt_hi = ((j + 1) * QC - 1) // 128
                b_hi, t_hi = divmod(tt_hi, 4)
                sy.wait_ge(s_mv.h, evq[("x2st", b_hi, t_hi)])
                prev = evq.get(("outev", j - 2))
                if prev:
                    sy.wait_ge(s_dve.h, prev)
                sy.dma_start(x2c[:, slj, :, :],
                             bass.AP(x2s, j * QC * D, [[D, 128], [128 * D, 2], [1, D]]))\
                  .then_inc(s_dma.h, 16)
                s_dma.n += 16

            step(sync=pc_x2ld)
            x2ld = s_dma.n

            tp_marks = []
            for tt in range(2):
                def pc_ln_v(v, j=j, tt=tt, slj=slj, x2ld=x2ld):
                    if tt == 0:
                        v.wait_ge(s_dma.h, x2ld)
                    nt_ = j * 2 + tt
                    if tt > 0 or j > 0:
                        v.wait_ge(s_pe.h, evq.get(("tpc", j, tt - 1),
                                                  evq.get(("tpc", j - 1, 1), 0)))
                    v.tensor_scalar(xlnc[:], x2c[:, slj, tt, :],
                                    m2r2[:, nt_, 0:1], m2r2[:, nt_, 1:2],
                                    ALU.subtract, ALU.mult).then_inc(s_dve.h, 1)
                    s_dve.n += 1

                step(vector=pc_ln_v)
                dve_xlnc = s_dve.n

                for grp in range(2):
                    def pc_tp_t(t, j=j, tt=tt, grp=grp, dve_xlnc=dve_xlnc):
                        if grp == 0:
                            t.wait_ge(s_dve.h, dve_xlnc)
                        if tp_marks:
                            t.wait_ge(s_dve.h, tp_marks[-1])
                        else:
                            prevtp = evq.get(("tpv", j - 1))
                            if prevtp:
                                t.wait_ge(s_dve.h, prevtp)
                        for i in range(4):
                            dc = grp * 4 + i
                            t.transpose(pTt[:, i * 128:i * 128 + 128],
                                        xlnc[:, dc * 128:dc * 128 + 128], identb[:])
                        t.matmul(pTt[:].bitcast(F32)[0:4, 256:260], onesb[0:1, 0:4],
                                 onesb[0:1, 0:4],
                                 start=True, stop=True)
                        t.drain().then_inc(s_pe.h, 1)
                        s_pe.n += 1

                    step(tensor=pc_tp_t)
                    pem = s_pe.n
                    evq[("tpc", j, tt)] = s_pe.n

                    def pc_tp_v(v, j=j, tt=tt, grp=grp, pem=pem):
                        v.wait_ge(s_pe.h, pem)
                        for i in range(4):
                            dc = grp * 4 + i
                            cp = v.tensor_copy(x2lT[:, dc, tt * 128:tt * 128 + 128],
                                               pTt[:, i * 128:i * 128 + 128])
                        cp.then_inc(s_dve.h, 1)
                        s_dve.n += 1
                        tp_marks.append(s_dve.n)

                    step(vector=pc_tp_v)
            evq[("tpv", j)] = s_dve.n
            tplast = s_dve.n

            if _CLVL < 2:
                continue
            spk_marks = {}
            enc_dve = {}
            base_dve_enc = [0]

            def t_enc(t, j=j, tplast=tplast):
                for hc in range(16):
                    if hc == 0:
                        t.wait_ge(s_dve.h, tplast)
                        t.wait_ge(s_dma.h, DMA_FW)
                    pp = pE0 if hc % 2 == 0 else pE1
                    prev = enc_dve.get(hc - 2, evq.get(("encdve14", j - 1, hc)))
                    if prev:
                        t.wait_ge(s_dve.h, prev)
                    if hc <= 1:
                        pr = evq.get(("rts", j - 1, 14 + hc))
                        if pr:
                            t.wait_ge(s_act.h, pr)
                    mm = None
                    for dc in range(8):
                        mm = t.matmul(pp[:, 0:QC], enc_sb[:, hc, dc, :], x2lT[:, dc, :],
                                      start=(dc == 0), stop=(dc == 7))
                    if hc == 15:
                        t.drain().then_inc(s_pe.h, 1)
                    else:
                        mm.then_inc(s_pe.h, 1)
                    s_pe.n += 1
                    evq[("encpe", j, hc)] = s_pe.n

            def v_enc(v, j=j):
                for hc in range(16):
                    v.wait_ge(s_pe.h, evq[("encpe", j, min(hc + 1, 15))])
                    prevs = spk_marks.get(hc - 2, evq.get(("spk14", j - 1, hc)))
                    if prevs:
                        v.wait_ge(s_act.h, prevs)
                    pp = pE0 if hc % 2 == 0 else pE1
                    v.tensor_copy(encscr[:, hc % 2, :], pp[:, 0:QC]).then_inc(s_dve.h, 1)
                    s_dve.n += 1
                    enc_dve[hc] = s_dve.n
                    if hc >= 14:
                        evq[("encdve14", j, hc - 14)] = s_dve.n

            def s_enc(s, j=j):
                for hc in range(16):
                    s.wait_ge(s_dve.h, enc_dve[hc])
                    s.activation(spk[:, hc, :], encscr[:, hc % 2, :], _SIG(),
                                 bias=encb_c[:, hc:hc + 1], scale=1.0).then_inc(s_act.h, 1)
                    s_act.n += 1
                    spk_marks[hc] = s_act.n
                    if hc >= 14:
                        evq[("spk14", j, hc - 14)] = s_act.n

            base_pe_enc = s_pe.n
            for hc in range(16):
                evq[("encpe", j, hc)] = base_pe_enc + hc + 1
            base_dve = s_dve.n
            for hc in range(16):
                enc_dve[hc] = base_dve + hc + 1
            step(tensor=t_enc, vector=v_enc, scalar=s_enc)
            spk_marks15 = spk_marks[15]

            def pc_s2c_t(t, j=j):
                t.wait_ge(s_act.h, spk_marks15)
                prev = evq.get(("moedone", j - 1))
                if prev:
                    t.wait_ge(s_dve.h, prev)
                for hc in range(16):
                    t.matmul(pC2[:, 0:QC], s2c_sb[:, hc, :], spk[:, hc, :],
                             start=(hc == 0), stop=(hc == 15))
                t.drain().then_inc(s_pe.h, 1)
                s_pe.n += 1

            step(tensor=pc_s2c_t)
            pe_s2c = s_pe.n

            def pc_s2c_v(v, j=j, pe_s2c=pe_s2c):
                v.wait_ge(s_pe.h, pe_s2c)
                v.tensor_scalar(cont[:], pC2[:, 0:QC], s2cb_c[:, 0:1], None,
                                ALU.add).then_inc(s_dve.h, 1)
                s_dve.n += 1

            step(vector=pc_s2c_v)
            dve_cont = s_dve.n

            if _CLVL < 4:
                continue
            def pc_rt_t(t, j=j, dve_cont=dve_cont):
                t.wait_ge(s_dve.h, dve_cont)
                t.matmul(pM[0:64, 0:QC], rw_sb[0:64, 0:64], cont[0:64, :],
                         start=True, stop=True).then_inc(s_pe.h, 1)
                s_pe.n += 1

            step(tensor=pc_rt_t)
            pe_rh = s_pe.n

            def pc_rt_v(v, j=j, pe_rh=pe_rh):
                v.wait_ge(s_pe.h, pe_rh)
                v.tensor_scalar(rh[:], pM[0:64, 0:QC], rb1_c[:, 0:1], 0.0,
                                ALU.add, ALU.max).then_inc(s_dve.h, 1)
                s_dve.n += 1

            step(vector=pc_rt_v)
            dve_rh = s_dve.n

            if _CLVL < 5:
                continue
            def pc_lg_t(t, j=j, dve_rh=dve_rh):
                t.wait_ge(s_dve.h, dve_rh)
                for tt in range(2):
                    mm = t.matmul(pM[:, 256 + tt * 16:256 + tt * 16 + NE],
                                  rh[0:64, tt * 128:tt * 128 + 128], rw_sb[0:64, 64:64 + NE],
                                  start=True, stop=True)
                mm.then_inc(s_pe.h, 1)
                s_pe.n += 1

            step(tensor=pc_lg_t)
            pe_lg = s_pe.n

            def pc_top2(v, j=j, pe_lg=pe_lg):
                v.wait_ge(s_pe.h, pe_lg)
                if j == 0:
                    v.wait_ge(s_dve.h, BCDONE)
                for tt in range(2):
                    v.tensor_tensor(lgt[:, tt, :], pM[:, 256 + tt * 16:256 + tt * 16 + NE],
                                    rb2bc[:], ALU.add)
                v.drain()
                for tt in range(2):
                    lt = lgt[:, tt, :]
                    v.reduce_max(cst3[:, 0:1], lt, axis=AX.X)
                    v.drain()
                    v.tensor_single_scalar(cmb[:, tt, :], lt, cst3[:, 0:1], ALU.is_ge)
                    v.drain()
                    v.scalar_tensor_tensor(msk[:], cmb[:, tt, :], -1e30, lt, ALU.mult, ALU.add)
                    v.drain()
                    v.reduce_max(cst3[:, 1:2], msk[:], axis=AX.X)
                    v.drain()
                    v.tensor_single_scalar(msk2[:, tt, :], msk[:], cst3[:, 1:2], ALU.is_ge)
                    tl = v.tensor_tensor(d12[:, tt:tt + 1], cst3[:, 0:1], cst3[:, 1:2],
                                         ALU.subtract)
                tl.then_inc(s_dve.h, 1)
                s_dve.n += 1

            step(vector=pc_top2)
            dve_t2 = s_dve.n

            def pc_w1_s(s, j=j, dve_t2=dve_t2):
                s.wait_ge(s_dve.h, dve_t2)
                s.activation(d12[:, 2:4], d12[:, 0:2], _SIG()).then_inc(s_act.h, 1)
                s_act.n += 1

            step(scalar=pc_w1_s)
            act_w1 = s_act.n

            def pc_cmb_v(v, j=j, act_w1=act_w1):
                v.wait_ge(s_act.h, act_w1)
                for tt in range(2):
                    v.tensor_scalar(cst3[:, 4 + tt:5 + tt], d12[:, 2 + tt:3 + tt], -1.0, 1.0,
                                    ALU.mult, ALU.add)
                v.drain()
                for tt in range(2):
                    v.tensor_scalar(cmb[:, tt, :], cmb[:, tt, :], d12[:, 2 + tt:3 + tt],
                                    None, ALU.mult)
                v.drain()
                for tt in range(2):
                    v.tensor_scalar(msk2[:, tt, :], msk2[:, tt, :], cst3[:, 4 + tt:5 + tt],
                                    None, ALU.mult)
                v.drain()
                for tt in range(2):
                    tl = v.tensor_tensor(cmb[:, tt, :], cmb[:, tt, :], msk2[:, tt, :], ALU.add)
                tl.then_inc(s_dve.h, 1)
                s_dve.n += 1

            step(vector=pc_cmb_v)
            dve_cmb = s_dve.n

            if _CLVL < 6:
                continue
            def pc_ct_t(t, j=j, dve_cmb=dve_cmb):
                t.wait_ge(s_dve.h, dve_cmb)
                for tt in range(2):
                    t.transpose(pM[0:8, tt * 128:tt * 128 + 128], cmb[:, tt, :], identf[:])
                t.matmul(pM[0:4, 508:512], onesf[0:1, 0:4], onesf[0:1, 0:4], start=True, stop=True)
                t.drain().then_inc(s_pe.h, 1)
                s_pe.n += 1

            step(tensor=pc_ct_t)
            pe_ct = s_pe.n

            def pc_ct_v(v, j=j, pe_ct=pe_ct):
                v.wait_ge(s_pe.h, pe_ct)
                v.tensor_copy(comb_t[:], pM[0:8, 0:QC]).then_inc(s_dve.h, 1)
                s_dve.n += 1

            step(vector=pc_ct_v)
            dve_ct = s_dve.n

            if _CLVL < 7:
                continue
            moe_marks = []
            for p in range(4):
                eh_marks = []
                for hc in range(8):
                    pebox = []

                    def t_eh(t, j=j, p=p, hc=hc, pebox=pebox):
                        if p == 0 and hc == 0:
                            t.wait_ge(s_dve.h, dve_cont)
                        ppA = pE0 if hc % 2 == 0 else pC2
                        ppB = pE1 if hc % 2 == 0 else pM
                        prev = eh_marks[hc - 2] if hc >= 2 else \
                            evq.get(("ehm", j, p - 1, 6 + hc), evq.get(("ehm", j - 1, 3, 6 + hc)))
                        if prev:
                            t.wait_ge(s_dve.h, prev)
                        if hc <= 1:
                            prevmoe = moe_marks[p - 1] if p >= 1 else evq.get(("moedone", j - 1))
                            if prevmoe:
                                t.wait_ge(s_dve.h, prevmoe)
                            if p == 0:
                                t.wait_ge(s_dve.h, dve_ct)
                        if not _NOEH:
                            t.matmul(ppA[:, 0:QC], ew1_sb[0:64, p, hc, :], cont[0:64, :],
                                     start=True, stop=True)
                            t.matmul(ppB[:, 0:QC], ew1_sb[64:128, p, hc, :],
                                     cont[64:128, :], start=True,
                                     stop=True).then_inc(s_pe.h, 1)
                        else:
                            t.drain().then_inc(s_pe.h, 1)
                        s_pe.n += 1
                        pebox.append(s_pe.n)

                    def v_eh(v, j=j, p=p, hc=hc, pebox=pebox):
                        v.wait_ge(s_pe.h, pebox[0])
                        ppA = pE0 if hc % 2 == 0 else pC2
                        ppB = pE1 if hc % 2 == 0 else pM
                        v.tensor_scalar(ehs0[:, hc, :], ppA[:, 0:QC],
                                        eb1_c[:, 2 * p, hc:hc + 1], 0.0, ALU.add, ALU.max)
                        v.tensor_scalar(ehs1[:, hc, :], ppB[:, 0:QC],
                                        eb1_c[:, 2 * p + 1, hc:hc + 1], 0.0,
                                        ALU.add, ALU.max).then_inc(s_dve.h, 1)
                        s_dve.n += 1
                        eh_marks.append(s_dve.n)
                        evq[("ehm", j, p, hc)] = s_dve.n

                    step(tensor=t_eh, vector=v_eh)

                def pc_eo_t(t, j=j, p=p, eh_marks=eh_marks, dve_ct=dve_ct):
                    prev = moe_marks[p - 1] if p >= 1 else evq.get(("moedone", j - 1))
                    if prev:
                        t.wait_ge(s_dve.h, prev)
                    if p == 0:
                        pdd = evq.get(("decdve", j - 1, 1))
                        if pdd:
                            t.wait_ge(s_dve.h, pdd)
                    for hc in range(8):
                        t.wait_ge(s_dve.h, eh_marks[hc])
                        if not _NOEO:
                            t.matmul(pD[0:64, 0:QC], ew2_sb[:, 2 * p, hc, :], ehs0[:, hc, :],
                                     start=(hc == 0), stop=(hc == 7), tile_position=(0, 0))
                            t.matmul(pD[64:128, 512:512 + QC], ew2_sb[:, 2 * p + 1, hc, :],
                                     ehs1[:, hc, :],
                                     start=(hc == 0), stop=(hc == 7), tile_position=(0, 64))
                    t.wait_ge(s_dve.h, dve_ct)
                    if j == 0:
                        t.wait_ge(s_dve.h, SELPR_OK)
                    if not _NOBC:
                        t.matmul(pM[:, 256:256 + QC], selpr[:, p, :], comb_t[:],
                                 start=True, stop=True)
                    t.drain().then_inc(s_pe.h, 1)
                    s_pe.n += 1

                step(tensor=pc_eo_t)
                pe_eo = s_pe.n

                def pc_eo_v(v, j=j, p=p, pe_eo=pe_eo):
                    v.wait_ge(s_pe.h, pe_eo)
                    v.tensor_scalar(eov[0:64, :], pD[0:64, 0:QC], eb2_c[0:64, p:p + 1],
                                    None, ALU.add)
                    v.tensor_scalar(eov[64:128, :], pD[64:128, 512:512 + QC],
                                    eb2_c[64:128, p:p + 1], None, ALU.add)
                    v.tensor_copy(bcv[:], pM[:, 256:256 + QC])
                    v.drain()
                    if p == 0:
                        v.tensor_tensor(moe[:], eov[:], bcv[:],
                                        ALU.mult).then_inc(s_dve.h, 1)
                    else:
                        v.tensor_tensor(ctr[:], eov[:], bcv[:], ALU.mult)
                        v.drain()
                        v.tensor_tensor(moe[:], moe[:], ctr[:],
                                        ALU.add).then_inc(s_dve.h, 1)
                    s_dve.n += 1
                    moe_marks.append(s_dve.n)

                step(vector=pc_eo_v)
            evq[("moedone", j)] = moe_marks[-1]

            if _CLVL < 8:
                continue
            def pc_moeb(v, j=j):
                v.wait_ge(s_dve.h, moe_marks[-1])
                v.tensor_copy(moeb[:], moe[:]).then_inc(s_dve.h, 1)
                s_dve.n += 1

            step(vector=pc_moeb)
            dve_moeb = s_dve.n

            if _CLVL < 9:
                continue
            rts_marks = {}
            c2s_pe = {}

            def t_c2s(t, j=j, dve_moeb=dve_moeb):
                for hc in range(16):
                    if hc == 0:
                        t.wait_ge(s_dve.h, dve_moeb)
                    pp = pE0 if hc % 2 == 0 else pE1
                    prev = rts_marks.get(hc - 2)
                    if prev:
                        t.wait_ge(s_act.h, prev)
                    t.matmul(pp[:, 0:QC], c2s_sb[:, hc, :], moeb[:], start=True,
                             stop=True).then_inc(s_pe.h, 1)
                    s_pe.n += 1
                    c2s_pe[hc] = s_pe.n

            def s_c2s(s, j=j):
                for hc in range(16):
                    s.wait_ge(s_pe.h, c2s_pe[hc])
                    pp = pE0 if hc % 2 == 0 else pE1
                    s.activation(rts[:, hc, :], pp[:, 0:QC], _SIG(),
                                 bias=c2sb_c[:, hc:hc + 1], scale=1.0).then_inc(s_act.h, 1)
                    s_act.n += 1
                    rts_marks[hc] = s_act.n

            base_act_c2s = s_act.n
            for hc in range(16):
                rts_marks[hc] = base_act_c2s + hc + 1
            evq[("rts", j, 14)] = rts_marks[14]
            evq[("rts", j, 15)] = rts_marks[15]
            step(tensor=t_c2s, scalar=s_c2s)
            rts_last = rts_marks[15]

            dec_marks = []
            for tt in range(2):
                def pc_dec_t(t, j=j, tt=tt, rts_last=rts_last):
                    t.wait_ge(s_act.h, rts_last)
                    prev = evq.get(("decdve", j, tt - 1), evq.get(("decdve", j - 1, 1)))
                    if prev:
                        t.wait_ge(s_dve.h, prev)
                    for hc in range(16):
                        lhs = rts[:, hc, tt * 128:tt * 128 + 128]
                        t.matmul(pD[:, 0:512], lhs, dec_sb[:, hc, 0:512],
                                 start=(hc == 0), stop=False)
                        t.matmul(pD[:, 512:1024], lhs, dec_sb[:, hc, 512:1024],
                                 start=(hc == 0), stop=False)
                    t.matmul(pD[:, 0:512], onesb[:], brow2[0:1, 0:512],
                             start=False, stop=True)
                    t.matmul(pD[:, 512:1024], onesb[:], brow2[0:1, 512:1024],
                             start=False, stop=True)
                    t.drain().then_inc(s_pe.h, 1)
                    s_pe.n += 1

                step(tensor=pc_dec_t)
                pe_dec = s_pe.n

                def pc_dec_v(v, j=j, tt=tt, pe_dec=pe_dec):
                    v.wait_ge(s_pe.h, pe_dec)
                    prevs = evq.get(("decact", j, tt - 1), evq.get(("decact", j - 1, 1)))
                    if prevs:
                        v.wait_ge(s_act.h, prevs)
                    v.tensor_copy(decscr[:], pD[:]).then_inc(s_dve.h, 1)
                    s_dve.n += 1

                step(vector=pc_dec_v)
                dec_dve = s_dve.n
                evq[("decdve", j, tt)] = dec_dve

                def pc_dec_s(s, j=j, tt=tt, dec_dve=dec_dve):
                    s.wait_ge(s_dve.h, dec_dve)
                    s.activation(dcb[:, tt, :], decscr[:], _SIG()).then_inc(s_act.h, 1)
                    s_act.n += 1
                    dec_marks.append(s_act.n)
                    evq[("decact", j, tt)] = s_act.n

                step(scalar=pc_dec_s)

            if _CLVL < 11:
                continue
            def pc_l3_v(v, j=j, dec_marks=dec_marks):
                v.wait_ge(s_act.h, dec_marks[-1])
                for tt in range(2):
                    v.reduce_sum(cst3[:, 8 + 4 * tt:9 + 4 * tt], dcb[:, tt, :], axis=AX.X)
                    v.tensor_mul(lnscr[:], dcb[:, tt, :], dcb[:, tt, :])
                    v.drain()
                    v.reduce_sum(cst3[:, 9 + 4 * tt:10 + 4 * tt], lnscr[:], axis=AX.X)
                v.drain()
                for tt in range(2):
                    v.tensor_scalar_mul(cst3[:, 8 + 4 * tt:9 + 4 * tt],
                                        cst3[:, 8 + 4 * tt:9 + 4 * tt], 1.0 / D)
                    v.tensor_scalar_mul(cst3[:, 9 + 4 * tt:10 + 4 * tt],
                                        cst3[:, 9 + 4 * tt:10 + 4 * tt], 1.0 / D)
                v.drain()
                for tt in range(2):
                    v.tensor_mul(cst3[:, 10 + 4 * tt:11 + 4 * tt],
                                 cst3[:, 8 + 4 * tt:9 + 4 * tt], cst3[:, 8 + 4 * tt:9 + 4 * tt])
                v.drain()
                for tt in range(2):
                    tl = v.tensor_sub(cst3[:, 10 + 4 * tt:11 + 4 * tt],
                                      cst3[:, 9 + 4 * tt:10 + 4 * tt],
                                      cst3[:, 10 + 4 * tt:11 + 4 * tt])
                tl.then_inc(s_dve.h, 1)
                s_dve.n += 1

            step(vector=pc_l3_v)
            dve_v3 = s_dve.n

            def pc_l3_s(s, j=j, dve_v3=dve_v3):
                s.wait_ge(s_dve.h, dve_v3)
                for tt in range(2):
                    s.activation(cst3[:, 11 + 4 * tt:12 + 4 * tt],
                                 cst3[:, 10 + 4 * tt:11 + 4 * tt], ACT.Ln, bias=cns[:, 0:1], scale=1.0)
                s.drain()
                for tt in range(2):
                    al = s.activation(cst3[:, 10 + 4 * tt:11 + 4 * tt],
                                      cst3[:, 11 + 4 * tt:12 + 4 * tt], ACT.Exp,
                                      scale=cns[:, 2:3])
                al.then_inc(s_act.h, 1)
                s_act.n += 1

            step(scalar=pc_l3_s)
            act_r3 = s_act.n

            def pc_fin_v(v, j=j, slj=slj, act_r3=act_r3):
                v.wait_ge(s_act.h, act_r3)
                if j == 0:
                    v.wait_ge(s_dve.h, BCDONE)
                prev = evq.get(("ost", j - 1))
                if prev:
                    v.wait_ge(s_out.h, prev)
                for tt in range(2):
                    v.tensor_scalar(lnscr[:], dcb[:, tt, :],
                                    cst3[:, 8 + 4 * tt:9 + 4 * tt],
                                    cst3[:, 10 + 4 * tt:11 + 4 * tt],
                                    ALU.subtract, ALU.mult)
                    v.drain()
                    v.tensor_tensor(lnscr2[:], lnscr[:], gbc[:], ALU.mult)
                    v.drain()
                    v.tensor_tensor(lnscr[:], lnscr2[:], bbc[:], ALU.add)
                    v.drain()
                    tl = v.tensor_tensor(outb[:, tt, :], lnscr[:], x2c[:, slj, tt, :],
                                         ALU.add)
                tl.then_inc(s_dve.h, 1)
                s_dve.n += 1

            step(vector=pc_fin_v)
            evq[("outev", j)] = s_dve.n
            dve_out = s_dve.n

            def pc_ost(g, j=j, dve_out=dve_out):
                g.wait_ge(s_dve.h, dve_out)
                g.dma_start(bass.AP(out, j * QC * D, [[D, 128], [128 * D, 2], [1, D]]),
                            outb[:]).then_inc(s_out.h, 16)
                s_out.n += 16
                evq[("ost", j)] = s_out.n

            step(gpsimd=pc_ost)

            if _DEBUG and j == 0:
                def dbg_c(g, upto=s_dve.n):
                    g.wait_ge(s_dve.h, upto)
                    g.dma_start(dbg["spk"].ap(), spk[:]).then_inc(s_mv.h, 16)
                    g.dma_start(dbg["cont"].ap(), cont[:]).then_inc(s_mv.h, 16)
                    g.dma_start(dbg["comb"].ap(), cmb[:]).then_inc(s_mv.h, 16)
                    g.dma_start(dbg["moe"].ap(), moe[:]).then_inc(s_mv.h, 16)
                    g.dma_start(dbg["dec"].ap(), dcb[:]).then_inc(s_mv.h, 16)
                    g.dma_start(dbg["lgt"].ap(), lgt[:]).then_inc(s_mv.h, 16)
                    g.dma_start(dbg["rh"].ap(), rh[:]).then_inc(s_mv.h, 16)
                    s_mv.n += 112

                step(gpsimd=dbg_c)

        if _STOP == "C1":
            def finC(g):
                g.wait_ge(s_out.h, s_out.n)
                g.wait_ge(s_dve.h, s_dve.n)
                g.wait_ge(s_dma.h, s_dma.n)
            step(gpsimd=finC)
            flush()
            cstk.close()
            return nc

        def fin(g):
            g.wait_ge(s_out.h, s_out.n)

        step(gpsimd=fin)
        flush()
        cstk.close()

    return nc


def kernel(**inputs):
    x = np.ascontiguousarray(np.asarray(inputs["L0_attn_x"], dtype=np.float32))
    names = {
        "w_qkv": "L0_attn_w_qkv", "w_proj": "L0_attn_w_proj", "b_proj": "L0_attn_b_proj",
        "enc_w": "L0_ffn_enc_w", "enc_b": "L0_ffn_enc_b", "s2c_w": "L0_ffn_s2c_w",
        "s2c_b": "L0_ffn_s2c_b", "r_w1": "L0_ffn_r_w1", "r_b1": "L0_ffn_r_b1",
        "r_w2": "L0_ffn_r_w2", "r_b2": "L0_ffn_r_b2", "e_w1": "L0_ffn_e_w1",
        "e_b1": "L0_ffn_e_b1", "e_w2": "L0_ffn_e_w2", "e_b2": "L0_ffn_e_b2",
        "c2s_w": "L0_ffn_c2s_w", "c2s_b": "L0_ffn_c2s_b", "dec_w": "L0_ffn_dec_w",
        "dec_b": "L0_ffn_dec_b", "ln_g": "L0_ffn_ln_g", "ln_b": "L0_ffn_ln_b",
    }
    wmap = {k: np.ascontiguousarray(np.asarray(inputs[v], dtype=np.float32))
            for k, v in names.items()}
    nc = build_core()
    in_maps = []
    for c in range(8):
        m = {"x": np.ascontiguousarray(x[c])}
        m.update(wmap)
        in_maps.append(m)
    res = run_bass_kernel_spmd(nc, in_maps, core_ids=list(range(8)))
    if _DEBUG:
        kernel.dbg = res.results
    return np.stack([res.results[c]["out"] for c in range(8)], axis=0)


if __name__ == "__main__":
    build_core()
    print("built OK")


# version 22
# speedup vs baseline: 1.2502x; 1.0108x over previous
import numpy as np
from contextlib import ExitStack

import concourse.bass as bass
import concourse.mybir as mybir
from concourse.bass_utils import run_bass_kernel_spmd

F32 = mybir.dt.float32
BF16 = mybir.dt.bfloat16
AX = mybir.AxisListType
ALU = mybir.AluOpType
ACT = mybir.ActivationFunctionType


def _SIG():
    return ACT.Exp if _SIG2EXP else ACT.Sigmoid

T = 4096
D = 1024
H = 16
HID = 2048
MD = 64
NE = 8
EH = 1024
NT = T // 128
QB = 512
NB = T // QB
QC = 256
NCB = T // QC
EPS = 1e-5

import os as _os
_DEBUG = _os.environ.get("K_DEBUG", "") == "1"
_STOP = _os.environ.get("K_STOP", "")
_NORSD = _os.environ.get("K_NORSD", "") == "1"
_NOMV = _os.environ.get("K_NOMV", "") == "1"
_HPN = int(_os.environ.get("K_HP", "8"))
_SONLY = _os.environ.get("K_SONLY", "") == "1"
_NOEXP = _os.environ.get("K_NOEXP", "") == "1"
_NOB = _os.environ.get("K_NOB", "") == "1"
_CLVL = int(_os.environ.get("K_CLVL", "99"))
_NOEH = _os.environ.get("K_NOEH", "") == "1"
_NOEO = _os.environ.get("K_NOEO", "") == "1"
_NOBC = _os.environ.get("K_NOBC", "") == "1"
_SIG2EXP = _os.environ.get("K_SIG2EXP", "") == "1"
_NOENCACT = _os.environ.get("K_NOENCACT", "") == "1"
_NOENCMM = _os.environ.get("K_NOENCMM", "") == "1"


class Sem:
    def __init__(self, h):
        self.h = h
        self.n = 0


def build_core():
    nc = bass.Bass("TRN2", target_bir_lowering=False, detect_race_conditions=False)

    xd = nc.dram_tensor("x", [T, D], F32, kind="ExternalInput")
    w_qkv = nc.dram_tensor("w_qkv", [D, 3 * D], F32, kind="ExternalInput")
    w_proj = nc.dram_tensor("w_proj", [D, D], F32, kind="ExternalInput")
    b_proj = nc.dram_tensor("b_proj", [D], F32, kind="ExternalInput")
    enc_w = nc.dram_tensor("enc_w", [D, HID], F32, kind="ExternalInput")
    enc_b = nc.dram_tensor("enc_b", [HID], F32, kind="ExternalInput")
    s2c_w = nc.dram_tensor("s2c_w", [HID, MD], F32, kind="ExternalInput")
    s2c_b = nc.dram_tensor("s2c_b", [MD], F32, kind="ExternalInput")
    r_w1 = nc.dram_tensor("r_w1", [MD, 64], F32, kind="ExternalInput")
    r_b1 = nc.dram_tensor("r_b1", [64], F32, kind="ExternalInput")
    r_w2 = nc.dram_tensor("r_w2", [64, NE], F32, kind="ExternalInput")
    r_b2 = nc.dram_tensor("r_b2", [NE], F32, kind="ExternalInput")
    e_w1 = nc.dram_tensor("e_w1", [NE, MD, EH], F32, kind="ExternalInput")
    e_b1 = nc.dram_tensor("e_b1", [NE, EH], F32, kind="ExternalInput")
    e_w2 = nc.dram_tensor("e_w2", [NE, EH, MD], F32, kind="ExternalInput")
    e_b2 = nc.dram_tensor("e_b2", [NE, MD], F32, kind="ExternalInput")
    c2s_w = nc.dram_tensor("c2s_w", [MD, HID], F32, kind="ExternalInput")
    c2s_b = nc.dram_tensor("c2s_b", [HID], F32, kind="ExternalInput")
    dec_w = nc.dram_tensor("dec_w", [HID, D], F32, kind="ExternalInput")
    dec_b = nc.dram_tensor("dec_b", [D], F32, kind="ExternalInput")
    ln_g = nc.dram_tensor("ln_g", [D], F32, kind="ExternalInput")
    ln_b = nc.dram_tensor("ln_b", [D], F32, kind="ExternalInput")
    out = nc.dram_tensor("out", [T, D], F32, kind="ExternalOutput")

    qts = nc.dram_tensor("qts", [8, 128, T], BF16, kind="Internal")
    x2s = nc.dram_tensor("x2s", [T, D], F32, kind="Internal")
    wqk_pre = nc.dram_tensor("wqk_pre", [16, 8, 128, 128], BF16, kind="Internal")
    wv_nat = nc.dram_tensor("wv_nat", [8, 128, D], BF16, kind="Internal")
    wproj_n = nc.dram_tensor("wproj_n", [8, 128, D], BF16, kind="Internal")
    enc_pre = nc.dram_tensor("enc_pre", [16, 8, 128, 128], BF16, kind="Internal")
    s2c_pre = nc.dram_tensor("s2c_pre", [16, 128, 128], BF16, kind="Internal")
    ew1_pre = nc.dram_tensor("ew1_pre", [4, 8, 128, 128], BF16, kind="Internal")
    ew2_pre = nc.dram_tensor("ew2_pre", [NE, 8, 128, 64], BF16, kind="Internal")
    c2s_pre = nc.dram_tensor("c2s_pre", [16, 128, 128], BF16, kind="Internal")
    dec_nat = nc.dram_tensor("dec_nat", [16, 128, D], BF16, kind="Internal")
    rw_pre = nc.dram_tensor("rw_pre", [64, 64 + NE], BF16, kind="Internal")
    brow_pre = nc.dram_tensor("brow_pre", [2, D], BF16, kind="Internal")

    dbg = {}
    if _DEBUG:
        dbg["kt"] = nc.dram_tensor("dbg_kt", [128, 8 * T], BF16, kind="ExternalOutput")
        dbg["v"] = nc.dram_tensor("dbg_v", [128, NT * H * 65], BF16, kind="ExternalOutput")
        dbg["qt"] = nc.dram_tensor("dbg_qt", [8, 128, T], BF16, kind="ExternalOutput")
        dbg["cb"] = nc.dram_tensor("dbg_cb", [128, 64], F32, kind="ExternalOutput")
        dbg["an"] = nc.dram_tensor("dbg_an", [128, 8 * QB], BF16, kind="ExternalOutput")
        dbg["x2"] = nc.dram_tensor("dbg_x2", [T, D], F32, kind="ExternalOutput")
        dbg["spk"] = nc.dram_tensor("dbg_spk", [128, 16 * QC], BF16, kind="ExternalOutput")
        dbg["cont"] = nc.dram_tensor("dbg_cont", [128, QC], BF16, kind="ExternalOutput")
        dbg["comb"] = nc.dram_tensor("dbg_comb", [128, 2 * NE], F32, kind="ExternalOutput")
        dbg["moe"] = nc.dram_tensor("dbg_moe", [128, QC], F32, kind="ExternalOutput")
        dbg["dec"] = nc.dram_tensor("dbg_dec", [128, 2 * D], F32, kind="ExternalOutput")
        dbg["lgt"] = nc.dram_tensor("dbg_lgt", [128, 2 * NE], F32, kind="ExternalOutput")
        dbg["xlnT"] = nc.dram_tensor("dbg_xlnT", [128, 8 * QB], BF16, kind="ExternalOutput")
        dbg["wqk"] = nc.dram_tensor("dbg_wqk", [128, 2 * 8 * 128], BF16, kind="ExternalOutput")
        dbg["stat"] = nc.dram_tensor("dbg_stat", [128, 16], F32, kind="ExternalOutput")
        dbg["xb"] = nc.dram_tensor("dbg_xb", [128, 2 * D], F32, kind="ExternalOutput")
        dbg["rh"] = nc.dram_tensor("dbg_rh", [64, QC], BF16, kind="ExternalOutput")

    pstk = ExitStack()
    with pstk:
        sbp = lambda n, s, d: pstk.enter_context(nc.sbuf_tensor(n, s, d))
        identb = sbp("identb", [128, 128], BF16)
        identf = sbp("identf", [128, 128], F32)
        onesf = sbp("onesf", [1, 128], F32)
        onesb = sbp("onesb", [1, 128], BF16)
        bo2 = sbp("bo2", [128, 2], BF16)
        selh = sbp("selh", [16, 16, 128], F32)
        brows = sbp("brows", [1, D], BF16)
        glc = sbp("glc", [128, 17], F32)
        growt = sbp("growt", [17, 128], F32)
        encb_c = sbp("encb_c", [128, 16], F32)
        s2cb_c = sbp("s2cb_c", [128, 1], F32)
        rb1_c = sbp("rb1_c", [64, 1], F32)
        eb1_c = sbp("eb1_c", [128, NE, 8], F32)
        eb2_c = sbp("eb2_c", [128, 4], F32)
        c2sb_c = sbp("c2sb_c", [128, 16], F32)
        m2r2 = sbp("m2r2", [128, NT, 2], F32)
        s12 = sbp("s12", [128, NT, 2], F32)
        cball = sbp("cball", [128, 64], F32)
        qsqm = sbp("qsqm", [2, 64], F32)
        ksqm = sbp("ksqm", [2, 8], F32)
        stat = sbp("stat", [128, 16], F32)
        cst3 = sbp("cst3", [128, 16], F32)
        rs2 = sbp("rs2", [16, QB], F32)
        rr2 = sbp("rr2", [16, QB], F32)
        lnscr = sbp("lnscr", [128, D], F32)
        lnscr2 = sbp("lnscr2", [128, D], F32)
        smrow = sbp("smrow", [64, 4], F32)
        cns = sbp("cns", [128, 4], F32)
        crow = sbp("crow", [1, 64], F32)
        crow2 = sbp("crow2", [1, 8], F32)
        abstk = ExitStack()
        kT = abstk.enter_context(nc.sbuf_tensor("kT", [128, 8, T], BF16))
        vres = abstk.enter_context(nc.sbuf_tensor("vres", [128, NT * H * 65], BF16))
        VP = NT * H * 65

        s_dma = Sem(pstk.enter_context(nc.semaphore("s_dma")))
        s_w = Sem(pstk.enter_context(nc.semaphore("s_w")))
        s_pe = Sem(pstk.enter_context(nc.semaphore("s_pe")))
        s_act = Sem(pstk.enter_context(nc.semaphore("s_act")))
        s_dve = Sem(pstk.enter_context(nc.semaphore("s_dve")))
        s_mv = Sem(pstk.enter_context(nc.semaphore("s_mv")))
        s_out = Sem(pstk.enter_context(nc.semaphore("s_out")))
        s_bar = Sem(pstk.enter_context(nc.semaphore("s_bar")))
        s_g = Sem(pstk.enter_context(nc.semaphore("s_g")))

        _cur = {"b": None, "auto": False}

        def step(**fns):
            if _cur["b"] is None:
                _cur["b"] = nc.Block()
                _cur["b"].__enter__()
            for eng in ("sync", "gpsimd", "tensor", "vector", "scalar"):
                if eng in fns:
                    getattr(_cur["b"], eng)(fns[eng])
            if _cur["auto"]:
                _cur["b"].__exit__(None, None, None)
                _cur["b"] = None

        def flush():
            if _cur["b"] is not None:
                _cur["b"].__exit__(None, None, None)
                _cur["b"] = None

        evq = {}

        iostk = ExitStack()
        iot_p = iostk.enter_context(nc.sbuf_tensor("iot_p", [128, 128], F32))
        iot_f = iostk.enter_context(nc.sbuf_tensor("iot_f", [128, 128], F32))
        p0stk = ExitStack()
        pc0 = p0stk.enter_context(nc.psum_tensor("pc0", [128, 512], F32))

        def p0_g(g):
            g.iota(iot_p[:], [[0, 128]], channel_multiplier=1,
                   allow_small_or_imprecise_dtypes=True)
            g.iota(iot_f[:], [[1, 128]], channel_multiplier=0,
                   allow_small_or_imprecise_dtypes=True)
            g.memset(onesf[:], 1.0)
            g.memset(onesb[:], 1.0)
            g.memset(bo2[0:64, 0:1], 1.0)
            g.memset(bo2[0:64, 1:2], 0.0)
            g.memset(bo2[64:128, 0:1], 0.0)
            g.memset(bo2[64:128, 1:2], 1.0)
            g.memset(ksqm[:], 0.0)
            g.memset(cns[:, 0:1], EPS)
            g.memset(cns[:, 1:2], 1e-30)
            g.memset(cns[:, 2:3], -0.5)
            g.memset(cns[:, 3:4], 0.5)
            g.iota(selh[:], [[1, 16], [0, 128]], channel_multiplier=-1,
                   allow_small_or_imprecise_dtypes=True)
            for kt_ in range(NT):
                g.memset(bass.AP(vres, kt_ * H * 65 + 64, [[VP, 128], [65, H]]), 1.0)
            def cdma(dst_ap, src_ap):
                g.dma_start(dst_ap, src_ap).then_inc(s_w.h, 16)
                s_w.n += 16

            for oc in range(16):
                cdma(wqk_pre.ap()[oc],
                     bass.AP(w_qkv, oc * 128, [[128 * 3072, 8], [3072, 128], [1, 128]]))
            for dc in range(8):
                cdma(wv_nat.ap()[dc],
                     bass.AP(w_qkv, 2048 + dc * 128 * 3072, [[3072, 128], [1, D]]))
            for dc in range(8):
                cdma(wproj_n.ap()[dc], bass.AP(w_proj, dc * 128 * D, [[D, 128], [1, D]]))
            for hc in range(16):
                cdma(enc_pre.ap()[hc],
                     bass.AP(enc_w, hc * 128, [[128 * HID, 8], [HID, 128], [1, 128]]))
            for half in range(2):
                cdma(bass.AP(s2c_pre, half * 64, [[128 * 128, 16], [128, 128], [1, 64]]),
                     bass.AP(s2c_w, 0, [[128 * MD, 16], [MD, 128], [1, 64]]))
            for pr in range(4):
                cdma(ew1_pre.ap()[pr],
                     bass.AP(e_w1, 2 * pr * MD * EH, [[128, 8], [EH, 128], [1, 128]]))
            for e in range(NE):
                cdma(ew2_pre.ap()[e],
                     bass.AP(e_w2, e * EH * MD, [[128 * MD, 8], [MD, 128], [1, 64]]))
            for half in range(2):
                cdma(bass.AP(c2s_pre, half * 64 * 128, [[128 * 128, 16], [128, 64], [1, 128]]),
                     bass.AP(c2s_w, 0, [[128, 16], [HID, 64], [1, 128]]))
            for hc in range(16):
                cdma(dec_nat.ap()[hc], bass.AP(dec_w, hc * 128 * D, [[D, 128], [1, D]]))
            cdma(bass.AP(rw_pre, 0, [[64 + NE, 64], [1, 64]]),
                 bass.AP(r_w1, 0, [[64, 64], [1, 64]]))
            cdma(bass.AP(rw_pre, 64, [[64 + NE, 64], [1, NE]]),
                 bass.AP(r_w2, 0, [[NE, 64], [1, NE]]))
            cdma(bass.AP(brow_pre, 0, [[D, 1], [1, D]]), bass.AP(b_proj, 0, [[D, 1], [1, D]]))
            cdma(bass.AP(brow_pre, D, [[D, 1], [1, D]]), bass.AP(dec_b, 0, [[D, 1], [1, D]]))
            with nc.allow_non_contiguous_dma(reason="tiny bias column loads"):
                cdma(encb_c[:], bass.AP(enc_b, 0, [[1, 128], [128, 16]]))
                for half in range(2):
                    cdma(s2cb_c[half * 64:(half + 1) * 64, :],
                         bass.AP(s2c_b, 0, [[1, 64], [64, 1]]))
                cdma(rb1_c[:], bass.AP(r_b1, 0, [[1, 64], [64, 1]]))
                cdma(eb1_c[:], bass.AP(e_b1, 0, [[1, 128], [EH, NE], [128, 8]]))
                cdma(eb2_c[:], bass.AP(e_b2, 0, [[1, 128], [128, 4]]))
                cdma(c2sb_c[:], bass.AP(c2s_b, 0, [[1, 128], [128, 16]]))
                cdma(glc[:, 0:8], bass.AP(ln_g, 0, [[1, 128], [128, 8]]))
                cdma(glc[:, 8:16], bass.AP(ln_b, 0, [[1, 128], [128, 8]]))
                cdma(glc[0:8, 16:17], bass.AP(r_b2, 0, [[1, 8], [8, 1]]))

        step(gpsimd=p0_g)
        flush()

        def p0_guard(g):
            g.drain().then_inc(s_g.h, 1)
            s_g.n += 1

        step(gpsimd=p0_guard)
        flush()
        G_CONST = s_g.n
        W_CASTS = s_w.n

        def p0_brows(sy):
            sy.wait_ge(s_w.h, W_CASTS)
            sy.wait_ge(s_g.h, G_CONST)
            sy.dma_start(brows[:], bass.AP(brow_pre, 0, [[D, 1], [1, D]]))\
              .then_inc(s_dma.h, 16)
            s_dma.n += 16

        step(sync=p0_brows)

        def p0_v(v):
            v.wait_ge(s_g.h, G_CONST)
            v.tensor_tensor(identb[:], iot_p[:], iot_f[:], ALU.is_equal)
            v.tensor_copy(identf[:], identb[:])
            v.tensor_single_scalar(selh[:], selh[:], 0.0, ALU.is_equal)
            v.drain().then_inc(s_dve.h, 1)
            s_dve.n += 1

        step(vector=p0_v)
        DVE_ID = s_dve.n

        def p0_t(t):
            t.wait_ge(s_w.h, W_CASTS)
            t.wait_ge(s_g.h, G_CONST)
            t.wait_ge(s_dve.h, DVE_ID)
            t.transpose(pc0[0:16, 0:128], glc[:, 0:16], identf[:])
            t.matmul(pc0[0:4, 480:484], onesf[0:1, 0:4], onesf[0:1, 0:4], start=True, stop=True)
            t.drain().then_inc(s_pe.h, 1)
            s_pe.n += 1

        step(tensor=p0_t)
        PE_GT = s_pe.n

        def p0_v2(v):
            v.wait_ge(s_pe.h, PE_GT)
            v.tensor_copy(growt[0:16, :], pc0[0:16, 0:128])
            v.drain().then_inc(s_dve.h, 1)
            s_dve.n += 1

        step(vector=p0_v2)

        def p0_t1b(t):
            t.wait_ge(s_dve.h, s_dve.n)
            t.transpose(pc0[0:1, 128:256], glc[:, 16:17], identf[:])
            t.matmul(pc0[0:4, 480:484], onesf[0:1, 0:4], onesf[0:1, 0:4], start=True, stop=True)
            t.drain().then_inc(s_pe.h, 1)
            s_pe.n += 1

        step(tensor=p0_t1b)
        PE_GT2 = s_pe.n

        def p0_v2b(v):
            v.wait_ge(s_pe.h, PE_GT2)
            v.tensor_copy(crow2[0:1, 0:8], pc0[0:1, 128:136])
            v.drain().then_inc(s_dve.h, 1)
            s_dve.n += 1

        step(vector=p0_v2b)
        DVE_GR = s_dve.n
        DVE_P0 = s_dve.n

        DVE_P0 = s_dve.n
        flush()
        p0stk.close()
        iostk.close()

        if _STOP == "0":
            def fin0(g):
                g.wait_ge(s_w.h, s_w.n)
                g.wait_ge(s_g.h, s_g.n)
            step(gpsimd=fin0)
            flush()
            abstk.close()
            return nc

        astk = ExitStack()
        sba = lambda n, s, d: astk.enter_context(nc.sbuf_tensor(n, s, d))
        xb = sba("xb", [128, 2, D], F32)
        xlnb = sba("xlnb", [128, D], BF16)
        xlnT = sba("xlnT", [128, 8, QB], BF16)
        wqk_sb = sba("wqk_sb", [128, 2, 8, 128], BF16)
        wv_sb = sba("wv_sb", [128, 8, D], BF16)
        qtb = sba("qtb", [128, 8, QB], BF16)
        sqscr = sba("sqscr", [128, QB], BF16)
        pqk0 = astk.enter_context(nc.psum_tensor("pqk0", [128, 512], F32))
        pqk1 = astk.enter_context(nc.psum_tensor("pqk1", [128, 512], F32))
        pv2a = astk.enter_context(nc.psum_tensor("pv2a", [128, 1024], F32))
        pv2b = astk.enter_context(nc.psum_tensor("pv2b", [128, 1024], F32))
        pnrm = astk.enter_context(nc.psum_tensor("pnrm", [2, 512], F32))
        ptp = astk.enter_context(nc.psum_tensor("ptp", [128, 640], BF16))

        _cur["auto"] = False

        def pa_wv(sy):
            sy.wait_ge(s_w.h, W_CASTS)
            sy.wait_ge(s_g.h, G_CONST)
            sy.dma_start(wv_sb[:], bass.AP(wv_nat, 0, [[D, 128], [128 * D, 8], [1, D]]))\
              .then_inc(s_dma.h, 16)
            s_dma.n += 16

        step(sync=pa_wv)
        DMA_WV = s_dma.n

        x_ld = {}
        for b in range(NB):
            ev_marks = []
            pe_tp = [0]
            ap_marks = {}
            for tt in range(4):
                slx = tt % 2

                def pa_ldx(sy, b=b, tt=tt, slx=slx):
                    prev = ap_marks.get(tt - 2, evq.get(("lnap", b - 1, tt + 2)))
                    if prev:
                        sy.wait_ge(s_dve.h, prev)
                    elif b == 0 and tt == 0:
                        sy.wait_ge(s_dve.h, DVE_P0)
                    sy.dma_start(xb[:, slx, :],
                                 bass.AP(xd, (b * 4 + tt) * 128 * D, [[D, 128], [1, D]]))\
                      .then_inc(s_dma.h, 16)
                    s_dma.n += 16

                step(sync=pa_ldx)
                x_ld[(b, tt)] = s_dma.n

                def pa_ln_v(v, b=b, tt=tt, slx=slx):
                    v.wait_ge(s_dma.h, x_ld[(b, tt)])
                    prevact = evq.get(("rstd", b - 1, tt))
                    if prevact:
                        v.wait_ge(s_act.h, prevact)
                    c0 = 4 * tt
                    v.reduce_sum(stat[:, c0:c0 + 1], xb[:, slx, :], axis=AX.X)
                    v.tensor_mul(lnscr[:], xb[:, slx, :], xb[:, slx, :])
                    v.drain()
                    v.reduce_sum(stat[:, c0 + 1:c0 + 2], lnscr[:], axis=AX.X)
                    v.drain()
                    v.tensor_scalar_mul(stat[:, c0:c0 + 1], stat[:, c0:c0 + 1], 1.0 / D)
                    v.tensor_scalar_mul(stat[:, c0 + 1:c0 + 2], stat[:, c0 + 1:c0 + 2], 1.0 / D)
                    v.drain()
                    v.tensor_mul(stat[:, c0 + 2:c0 + 3], stat[:, c0:c0 + 1], stat[:, c0:c0 + 1])
                    v.drain()
                    v.tensor_sub(stat[:, c0 + 2:c0 + 3], stat[:, c0 + 1:c0 + 2],
                                 stat[:, c0 + 2:c0 + 3])
                    v.drain().then_inc(s_dve.h, 1)
                    s_dve.n += 1

                step(vector=pa_ln_v)
                dve_var = s_dve.n

                def pa_ln_s(s, b=b, tt=tt, dve_var=dve_var):
                    s.wait_ge(s_dve.h, dve_var)
                    c0 = 4 * tt
                    s.activation(stat[:, c0 + 3:c0 + 4], stat[:, c0 + 2:c0 + 3], ACT.Ln,
                                 bias=cns[:, 0:1], scale=1.0)
                    s.drain()
                    s.activation(stat[:, c0 + 2:c0 + 3], stat[:, c0 + 3:c0 + 4], ACT.Exp,
                                 scale=cns[:, 2:3])
                    s.drain().then_inc(s_act.h, 1)
                    s_act.n += 1

                step(scalar=pa_ln_s)
                flush()
                act_rstd = s_act.n
                evq[("rstd", b, tt)] = act_rstd

                def pa_lnap(v, b=b, tt=tt, slx=slx, act_rstd=act_rstd):
                    v.wait_ge(s_act.h, act_rstd)
                    if pe_tp[0]:
                        v.wait_ge(s_pe.h, pe_tp[0])
                    elif b > 0:
                        v.wait_ge(s_pe.h, evq[("tppe", b - 1)])
                    v.tensor_scalar(xlnb[:], xb[:, slx, :],
                                    stat[:, 4 * tt:4 * tt + 1], stat[:, 4 * tt + 2:4 * tt + 3],
                                    ALU.subtract, ALU.mult)
                    v.drain().then_inc(s_dve.h, 1)
                    s_dve.n += 1
                    ap_marks[tt] = s_dve.n
                    evq[("lnap", b, tt)] = s_dve.n

                step(vector=pa_lnap)
                dve_ap = s_dve.n

                for grp in range(2):
                    def pa_tg_t(t, b=b, tt=tt, grp=grp, dve_ap=dve_ap):
                        t.wait_ge(s_dve.h, dve_ap)
                        if tt == 0 and grp == 0 and b > 0:
                            t.wait_ge(s_dve.h, evq[("vfree", b - 1)])
                        if ev_marks:
                            t.wait_ge(s_dve.h, ev_marks[-1])
                        for i in range(4):
                            dc = grp * 4 + i
                            t.transpose(ptp[:, i * 128:i * 128 + 128],
                                        xlnb[:, dc * 128:dc * 128 + 128], identb[:])
                        t.matmul(ptp[:].bitcast(F32)[0:4, 256:260], onesb[0:1, 0:4],
                                 onesb[0:1, 0:4],
                                 start=True, stop=True)
                        t.drain().then_inc(s_pe.h, 1)
                        s_pe.n += 1
                        pe_tp[0] = s_pe.n

                    step(tensor=pa_tg_t)
                    pem = s_pe.n

                    def pa_tg_v(v, b=b, tt=tt, grp=grp, pem=pem):
                        v.wait_ge(s_pe.h, pem)
                        for i in range(4):
                            dc = grp * 4 + i
                            v.tensor_copy(xlnT[:, dc, tt * 128:tt * 128 + 128],
                                          ptp[:, i * 128:i * 128 + 128])
                        v.drain().then_inc(s_dve.h, 1)
                        s_dve.n += 1
                        ev_marks.append(s_dve.n)

                    step(vector=pa_tg_v)
            evq[("tppe", b)] = s_pe.n
            evq[("tpdone", b)] = s_dve.n

            qk_ev = {}
            for oc in range(16):
                slot = oc % 2

                def pa_qk_ld(sy, b=b, oc=oc, slot=slot):
                    if b == 0 and oc <= 1:
                        sy.wait_ge(s_w.h, W_CASTS)
                        sy.wait_ge(s_g.h, G_CONST)
                    prev = qk_ev.get(oc - 2, evq.get(("qkev", b - 1, 14 + slot)))
                    if prev:
                        sy.wait_ge(s_dve.h, prev)
                    sy.dma_start(wqk_sb[:, slot, :, :],
                                 bass.AP(wqk_pre, oc * 8 * 128 * 128,
                                         [[128, 128], [128 * 128, 8], [1, 128]]))\
                      .then_inc(s_dma.h, 16)
                    s_dma.n += 16

                step(sync=pa_qk_ld)
                qk_ld = s_dma.n

                def pa_qk_t(t, b=b, oc=oc, slot=slot, qk_ld=qk_ld):
                    t.wait_ge(s_dma.h, qk_ld)
                    t.wait_ge(s_dve.h, evq[("tpdone", b)])
                    prev = qk_ev.get(oc - 2)
                    if prev:
                        t.wait_ge(s_dve.h, prev)
                    ppsum = pqk0 if slot == 0 else pqk1
                    for dc in range(8):
                        t.matmul(ppsum[:], wqk_sb[:, slot, dc, :], xlnT[:, dc, :],
                                 start=(dc == 0), stop=(dc == 7))
                    t.drain().then_inc(s_pe.h, 1)
                    s_pe.n += 1

                step(tensor=pa_qk_t)
                pe_qk = s_pe.n

                def pa_qk_v(v, b=b, oc=oc, slot=slot, pe_qk=pe_qk):
                    v.wait_ge(s_pe.h, pe_qk)
                    ppsum = pqk0 if slot == 0 else pqk1
                    if oc < 8:
                        if b > 0 and oc == 0:
                            v.wait_ge(s_mv.h, evq[("qst", b - 1)])
                        v.tensor_scalar_mul(qtb[:, oc, :], ppsum[:], 0.125)
                        v.drain()
                        v.tensor_mul(sqscr[:], qtb[:, oc, :], qtb[:, oc, :])
                    else:
                        c = oc - 8
                        v.tensor_copy(kT[:, c, b * QB:(b + 1) * QB], ppsum[:])
                        v.drain()
                        v.tensor_mul(sqscr[:], kT[:, c, b * QB:(b + 1) * QB],
                                     kT[:, c, b * QB:(b + 1) * QB])
                    v.drain().then_inc(s_dve.h, 1)
                    s_dve.n += 1

                step(vector=pa_qk_v)
                dve_sq = s_dve.n

                def pa_nrm_t(t, b=b, oc=oc, dve_sq=dve_sq):
                    t.wait_ge(s_dve.h, dve_sq)
                    t.matmul(pnrm[:], bo2[:], sqscr[:], start=True, stop=True)
                    t.drain().then_inc(s_pe.h, 1)
                    s_pe.n += 1

                step(tensor=pa_nrm_t)
                pe_nrm = s_pe.n

                def pa_nrm_v(v, b=b, oc=oc, pe_nrm=pe_nrm):
                    v.wait_ge(s_pe.h, pe_nrm)
                    if oc < 8:
                        v.reduce_max(qsqm[0:2, oc * 8 + b:oc * 8 + b + 1], pnrm[:], axis=AX.X)
                    else:
                        c = oc - 8
                        v.reduce_max(cst3[0:2, 15:16], pnrm[:], axis=AX.X)
                        v.drain()
                        v.tensor_tensor(ksqm[0:2, c:c + 1], ksqm[0:2, c:c + 1],
                                        cst3[0:2, 15:16], ALU.max)
                    v.drain().then_inc(s_dve.h, 1)
                    s_dve.n += 1

                step(vector=pa_nrm_v)
                qk_ev[oc] = s_dve.n
                evq[("qkev", b, oc)] = s_dve.n

            def pa_qst(g, b=b, after=qk_ev[7]):
                g.wait_ge(s_dve.h, qk_ev[7])
                if b > 0:
                    pass
                g.dma_start(bass.AP(qts, b * QB, [[T, 128], [128 * T, 8], [1, QB]]), qtb[:])\
                  .then_inc(s_mv.h, 16)
                s_mv.n += 16

            step(gpsimd=pa_qst)
            evq[("qst", b)] = s_mv.n

            v_ev = {}
            for tt in range(4):
                ppsum = pv2a if tt % 2 == 0 else pv2b

                def pa_v_t(t, b=b, tt=tt, ppsum=ppsum):
                    if tt == 0:
                        t.wait_ge(s_dma.h, DMA_WV)
                        t.wait_ge(s_dve.h, evq[("tpdone", b)])
                    prev = v_ev.get(tt - 2, evq.get(("vev", b - 1, tt + 2)))
                    if prev:
                        t.wait_ge(s_dve.h, prev)
                    for dc in range(8):
                        lhs = xlnT[:, dc, tt * 128:tt * 128 + 128]
                        t.matmul(ppsum[:, 0:512], lhs, wv_sb[:, dc, 0:512],
                                 start=(dc == 0), stop=(dc == 7))
                        t.matmul(ppsum[:, 512:1024], lhs, wv_sb[:, dc, 512:1024],
                                 start=(dc == 0), stop=(dc == 7))
                    t.drain().then_inc(s_pe.h, 1)
                    s_pe.n += 1

                step(tensor=pa_v_t)
                pe_v = s_pe.n

                def pa_v_v(v, b=b, tt=tt, ppsum=ppsum, pe_v=pe_v):
                    v.wait_ge(s_pe.h, pe_v)
                    kt_ = b * 4 + tt
                    v.tensor_copy(bass.AP(vres, kt_ * H * 65, [[VP, 128], [65, H], [1, 64]]),
                                  ppsum[:])
                    v.drain().then_inc(s_dve.h, 1)
                    s_dve.n += 1

                step(vector=pa_v_v)
                v_ev[tt] = s_dve.n
                evq[("vev", b, tt)] = s_dve.n
            evq[("vfree", b)] = s_dve.n
            evq[("tpvA", b)] = ev_marks[-1]
            flush()

            def pa_qguard(v, b=b):
                pass

            def pa_qg2(sy, b=b):
                sy.wait_ge(s_mv.h, evq[("qst", b)])
                sy.nop().then_inc(s_dma.h, 0) if False else None

            evq[("qstore", b)] = evq[("qst", b)]

        if _DEBUG:
            def dbg_pa(g):
                g.wait_ge(s_dve.h, s_dve.n)
                g.dma_start(dbg["xlnT"].ap(), xlnT[:]).then_inc(s_mv.h, 16)
                g.dma_start(dbg["wqk"].ap(), wqk_sb[:]).then_inc(s_mv.h, 16)
                g.dma_start(dbg["stat"].ap(), stat[:]).then_inc(s_mv.h, 16)
                g.dma_start(dbg["xb"].ap(), xb[:]).then_inc(s_mv.h, 16)
                s_mv.n += 64

            step(gpsimd=dbg_pa)

        def pcb_v1(v):
            v.wait_ge(s_dve.h, s_dve.n)
            for c in range(8):
                v.tensor_scalar(qsqm[0:2, c * 8:c * 8 + 8], qsqm[0:2, c * 8:c * 8 + 8],
                                ksqm[0:2, c:c + 1], None, ALU.mult)
            v.drain().then_inc(s_dve.h, 1)
            s_dve.n += 1

        step(vector=pcb_v1)
        dve_prod = s_dve.n

        def pcb_t1(t):
            t.wait_ge(s_dve.h, dve_prod)
            t.transpose(pqk0[0:64, 0:2], qsqm[0:2, 0:64], identf[0:2, 0:2])
            t.matmul(pqk0[0:4, 480:484], onesf[0:1, 0:4], onesf[0:1, 0:4], start=True, stop=True)
            t.drain().then_inc(s_pe.h, 1)
            s_pe.n += 1

        step(tensor=pcb_t1)
        pe_cp = s_pe.n

        def pcb_v2(v, pe_cp=pe_cp):
            v.wait_ge(s_pe.h, pe_cp)
            v.tensor_copy(smrow[:, 0:2], pqk0[0:64, 0:2])
            v.drain()
            v.tensor_tensor(smrow[:, 2:3], smrow[:, 0:1], smrow[:, 1:2], ALU.max)
            v.drain().then_inc(s_dve.h, 1)
            s_dve.n += 1

        step(vector=pcb_v2)
        dve_pm = s_dve.n

        def pcb_s1(s, dve_pm=dve_pm):
            s.wait_ge(s_dve.h, dve_pm)
            s.activation(smrow[:, 3:4], smrow[:, 2:3], ACT.Ln, bias=cns[0:64, 1:2], scale=1.0)
            s.drain()
            s.activation(smrow[:, 2:3], smrow[:, 3:4], ACT.Exp, scale=cns[0:64, 3:4])
            s.drain().then_inc(s_act.h, 1)
            s_act.n += 1

        step(scalar=pcb_s1)
        act_c = s_act.n

        def pcb_v3(v, act_c=act_c):
            v.wait_ge(s_act.h, act_c)
            v.tensor_scalar_mul(smrow[:, 1:2], smrow[:, 2:3], -1.0)
            v.drain().then_inc(s_dve.h, 1)
            s_dve.n += 1

        step(vector=pcb_v3)
        dve_cn = s_dve.n

        def pcb_t2(t, dve_cn=dve_cn):
            t.wait_ge(s_dve.h, dve_cn)
            t.transpose(pqk1[0:1, 0:64], smrow[:, 1:2], identf[0:64, 0:64])
            t.matmul(pqk1[0:4, 480:484], onesf[0:1, 0:4], onesf[0:1, 0:4], start=True, stop=True)
            t.drain().then_inc(s_pe.h, 1)
            s_pe.n += 1

        step(tensor=pcb_t2)
        pe_cr = s_pe.n

        def pcb_v4(v, pe_cr=pe_cr):
            v.wait_ge(s_pe.h, pe_cr)
            v.tensor_copy(crow[:], pqk1[0:1, 0:64])
            v.drain().then_inc(s_dve.h, 1)
            s_dve.n += 1

        step(vector=pcb_v4)
        dve_cr = s_dve.n

        def pcb_t3(t, dve_cr=dve_cr):
            t.wait_ge(s_dve.h, dve_cr)
            t.matmul(pqk0[:, 0:64], onesf[:], crow[:], start=True, stop=True)
            t.drain().then_inc(s_pe.h, 1)
            s_pe.n += 1

        step(tensor=pcb_t3)
        pe_cb = s_pe.n

        def pcb_v5(v, pe_cb=pe_cb):
            v.wait_ge(s_pe.h, pe_cb)
            v.tensor_copy(cball[:], pqk0[:, 0:64])
            v.drain().then_inc(s_dve.h, 1)
            s_dve.n += 1

        step(vector=pcb_v5)
        flush()
        _cur["auto"] = False
        DVE_CB = s_dve.n

        if _DEBUG:
            def dbg_a(g):
                g.wait_ge(s_dve.h, DVE_CB)
                g.dma_start(dbg["kt"].ap(), bass.AP(kT, 0, [[8 * T, 128], [1, 8 * T]]))\
                 .then_inc(s_mv.h, 16)
                g.dma_start(dbg["v"].ap(), vres[:]).then_inc(s_mv.h, 16)
                g.dma_start(dbg["qt"].ap(), qts.ap()).then_inc(s_mv.h, 16)
                g.dma_start(dbg["cb"].ap(), cball[:]).then_inc(s_mv.h, 16)
                s_mv.n += 64

            step(gpsimd=dbg_a)

        tgt = s_bar.n + 5

        def mkbar(e):
            e.nop().then_inc(s_bar.h, 1)
            e.wait_ge(s_bar.h, tgt)

        step(sync=mkbar, gpsimd=mkbar, tensor=mkbar, vector=mkbar, scalar=mkbar)
        flush()
        s_bar.n += 5
        astk.close()

        if _STOP == "A":
            def finA(g):
                g.wait_ge(s_mv.h, s_mv.n)
            step(gpsimd=finA)
            flush()
            astk.close()
            abstk.close()
            return nc

        bstk = ExitStack()
        sbb = lambda n, s, d: bstk.enter_context(nc.sbuf_tensor(n, s, d))
        qt_buf = sbb("qt_buf", [128, 8, QB], BF16)
        pbuf = sbb("pbuf", [128, 3, 1024], BF16)
        attnn = sbb("attnn", [128, 8, QB], BF16)
        stgB = sbb("stgB", [64, QB], BF16)
        rsl = sbb("rsl", [128, 2, 512], F32)
        xb1 = sbb("xb1", [128, D], F32)
        x2b = sbb("x2b", [128, D], F32)
        wproj_sb = sbb("wproj_sb", [128, 8, D], BF16)
        psS0 = bstk.enter_context(nc.psum_tensor("psS0", [128, 1024], F32))
        psS1 = bstk.enter_context(nc.psum_tensor("psS1", [128, 1024], F32))
        pvA = bstk.enter_context(nc.psum_tensor("pvA", [128, 512], F32))
        pvB = bstk.enter_context(nc.psum_tensor("pvB", [128, 512], F32))
        pbc = bstk.enter_context(nc.psum_tensor("pbc", [128, 1024], F32))

        def pb_wp(sy):
            sy.wait_ge(s_w.h, W_CASTS)
            sy.dma_start(wproj_sb[:], bass.AP(wproj_n, 0, [[D, 128], [128 * D, 8], [1, D]]))\
              .then_inc(s_dma.h, 16)
            s_dma.n += 16

        step(sync=pb_wp)
        DMA_WP = s_dma.n

        expm = {}
        pvm = {}
        spm = {}
        pvdr = {}

        def emit_spair(t, b, hp, kc):
            sp = (psS0, psS1, pbc)[kc % 3]
            t.matmul(sp[:, 0:512], kT[0:64, hp, kc * 128:kc * 128 + 128],
                     qt_buf[0:64, hp, :], start=True, stop=True)
            t.matmul(sp[:, 512:1024],
                     kT[64:128, hp, kc * 128:kc * 128 + 128],
                     qt_buf[64:128, hp, :], start=True,
                     stop=True).then_inc(s_pe.h, 1)
            s_pe.n += 1
            spm[(b, hp, kc)] = s_pe.n

        for b in range(NB if _STOP not in ("Battn", "Bnrm", "B1") else 1):
            def pb_qld(sy, b=b):
                if b > 0:
                    sy.wait_ge(s_pe.h, evq[("battn", b - 1)])
                sy.dma_start(qt_buf[:], bass.AP(qts, b * QB, [[T, 128], [128 * T, 8], [1, QB]]))\
                  .then_inc(s_dma.h, 16)
                s_dma.n += 16

            step(sync=pb_qld)
            qld = s_dma.n

            base_act_b = s_act.n
            for hp in range(_HPN):
                for kc in range(NT):
                    expm[(b, hp, kc)] = base_act_b + hp * NT + kc + 1

            for hp in range(_HPN):
                def t_attn(t, b=b, hp=hp, qld=qld):
                    hoisted = hp > 0
                    for kc in range(NT + 1):
                        if kc < NT and not (hoisted and kc < 2):
                            if kc == 0 and hp == 0:
                                t.wait_ge(s_dma.h, qld)
                                if b == 0:
                                    t.wait_ge(s_dve.h, DVE_CB)
                                if b > 0:
                                    t.wait_ge(s_dve.h, evq[("x2ev", b - 1, 3)])
                            prev_exp = expm.get((b, hp, kc - 3),
                                                expm.get((b, hp - 1, NT - 3 + kc),
                                                         expm.get((b - 1, 7, NT - 3 + kc))))
                            if prev_exp:
                                t.wait_ge(s_act.h, prev_exp)
                            emit_spair(t, b, hp, kc)
                        if kc >= 1:
                            kcp = kc - 1
                            hoist_now = (kcp == NT - 1 and hp + 1 < _HPN)
                            if hoist_now:
                                for j in (0, 1):
                                    t.wait_ge(s_act.h, expm[(b, hp, NT - 2 + j)])
                                    emit_spair(t, b, hp + 1, j)
                            else:
                                t.wait_ge(s_act.h, expm[(b, hp, kcp)])
                            if kcp == 0:
                                prev_ev = evq.get(("pvev", b, hp - 1),
                                                  evq.get(("pvev", b - 1, 7)))
                                if prev_ev:
                                    t.wait_ge(s_dve.h, prev_ev)
                                if hp == 0 and b > 0:
                                    t.wait_ge(s_mv.h, evq[("mvBall", b - 1)])
                                    t.wait_ge(s_pe.h, evq[("pjdone", b - 1)])
                            h0, h1 = 2 * hp, 2 * hp + 1
                            t.matmul(pvA[0:65, :],
                                     bass.AP(vres, kcp * H * 65 + h0 * 65, [[VP, 128], [1, 65]]),
                                     pbuf[:, kcp % 3, 0:512],
                                     start=(kcp == 0), stop=(kcp == NT - 1))
                            mpv = t.matmul(pvB[0:65, :],
                                           bass.AP(vres, kcp * H * 65 + h1 * 65,
                                                   [[VP, 128], [1, 65]]),
                                           pbuf[:, kcp % 3, 512:1024],
                                           start=(kcp == 0), stop=(kcp == NT - 1))
                            mpv.then_inc(s_pe.h, 1)
                            s_pe.n += 1
                            pvm[(b, hp, kcp)] = s_pe.n
                            if kcp == NT - 1:
                                t.drain().then_inc(s_pe.h, 1)
                                s_pe.n += 1
                                pvdr[(b, hp)] = s_pe.n

                step(tensor=t_attn)
                pvdone = pvdr[(b, hp)]
                slr = hp % 2

                def pb_rs_v(v, b=b, hp=hp, pvdone=pvdone, slr=slr):
                    v.wait_ge(s_pe.h, pvdone)
                    prev_rsd = evq.get(("rsd", b, hp - 1), evq.get(("rsd", b - 1, 7)))
                    if prev_rsd:
                        v.wait_ge(s_dma.h, prev_rsd)
                    v.tensor_copy(rsl[64:65, 0, :], pvA[64:65, :])
                    v.tensor_copy(rsl[64:65, 1, :], pvB[64:65, :])
                    prev_mv = evq.get(("mvB", b, hp - 1), evq.get(("mvB", b - 1, 7)))
                    if prev_mv:
                        v.wait_ge(s_mv.h, prev_mv)
                    v.tensor_copy(attnn[0:64, hp, :], pvA[0:64, :])
                    v.tensor_copy(stgB[0:64, :], pvB[0:64, :]).then_inc(s_dve.h, 1)
                    s_dve.n += 1

                step(vector=pb_rs_v)
                evq[("pvev", b, hp)] = s_dve.n
                dve_rs = s_dve.n

                def pb_rs_d(sy, b=b, hp=hp, slr=slr, dve_rs=dve_rs):
                    sy.wait_ge(s_dve.h, dve_rs)
                    sy.dma_start(rs2[2 * hp:2 * hp + 2, :], rsl[64:65, :, :])\
                      .then_inc(s_dma.h, 16)
                    s_dma.n += 16
                    evq[("rsd", b, hp)] = s_dma.n

                if not _NORSD:
                    step(sync=pb_rs_d)
                else:
                    evq[("rsd", b, hp)] = 0

                def pb_mv(g, b=b, hp=hp):
                    g.wait_ge(s_dve.h, evq[("pvev", b, hp)])
                    g.dma_start(attnn[64:128, hp, :], stgB[:]).then_inc(s_mv.h, 16)
                    s_mv.n += 16
                    evq[("mvB", b, hp)] = s_mv.n

                if not _NOMV:
                    step(gpsimd=pb_mv)
                else:
                    evq[("mvB", b, hp)] = 0

            def s_attn_b(s, b=b):
                for hp in range(_HPN):
                    for kc in range(NT):
                        w = spm[(b, hp, kc)]
                        if kc >= 3:
                            pvreq = pvm.get((b, hp, kc - 3), 0)
                        else:
                            kstar = (30, 31, 29)[kc]
                            pvreq = pvm.get((b, hp - 1, kstar),
                                            pvm.get((b - 1, 7, kstar), 0)) or 0
                        s.wait_ge(s_pe.h, max(w, pvreq))
                        sp = (psS0, psS1, pbc)[kc % 3]
                        s.activation(pbuf[:, kc % 3, :], sp[:], ACT.Exp,
                                     bias=cball[:, hp * 8 + b:hp * 8 + b + 1],
                                     scale=1.0).then_inc(s_act.h, 1)
                        s_act.n += 1
                        assert expm[(b, hp, kc)] == s_act.n

            step(scalar=s_attn_b)

            evq[("battn", b)] = spm[(b, _HPN - 1, NT - 1)]
            evq[("mvBall", b)] = evq.get(("mvB", b, _HPN - 1), 0)
            if _STOP == "Battn":
                break

            def pb_rec(v, b=b):
                v.wait_ge(s_dma.h, evq[("rsd", b, 7)])
                v.reciprocal(rr2[:], rs2[:])
                v.drain().then_inc(s_dve.h, 1)
                s_dve.n += 1

            step(vector=pb_rec)
            dve_rec = s_dve.n

            nrm_marks = []
            for h in range(16):
                slb = h % 2

                def pb_nb_t(t, b=b, h=h, slb=slb, dve_rec=dve_rec):
                    if h == 0:
                        t.wait_ge(s_dve.h, dve_rec)
                        t.wait_ge(s_mv.h, evq[("mvBall", b)])
                    if h >= 2:
                        t.wait_ge(s_dve.h, nrm_marks[h - 2])
                    t.matmul(pbc[:, slb * 512:slb * 512 + 512], selh[:, h, :], rr2[:],
                             start=True, stop=True).then_inc(s_pe.h, 1)
                    s_pe.n += 1

                step(tensor=pb_nb_t)
                pe_nb = s_pe.n

                def pb_nb_v(v, b=b, h=h, slb=slb, pe_nb=pe_nb):
                    v.wait_ge(s_pe.h, pe_nb)
                    hp, par = divmod(h, 2)
                    rsel = slice(0, 64) if par == 0 else slice(64, 128)
                    v.tensor_tensor(attnn[rsel, hp, :], attnn[rsel, hp, :],
                                    pbc[rsel, slb * 512:slb * 512 + 512], ALU.mult)
                    v.drain().then_inc(s_dve.h, 1)
                    s_dve.n += 1
                    nrm_marks.append(s_dve.n)

                step(vector=pb_nb_v)
            evq[("anev", b)] = s_dve.n
            if _STOP == "Bnrm":
                break

            if _DEBUG and b == 0:
                def dbg_an(g, upto=s_dve.n):
                    g.wait_ge(s_dve.h, upto)
                    g.dma_start(dbg["an"].ap(), attnn[:]).then_inc(s_mv.h, 16)
                    s_mv.n += 16

                step(gpsimd=dbg_an)

            for tt in range(4):
                slx = tt % 2

                def pb_xld(sy, b=b, tt=tt, slx=slx):
                    prev = evq.get(("x2ev", b, tt - 1), evq.get(("x2ev", b - 1, 3)))
                    if prev:
                        sy.wait_ge(s_dve.h, prev)
                    sy.dma_start(xb1[:],
                                 bass.AP(xd, (b * 4 + tt) * 128 * D, [[D, 128], [1, D]]))\
                      .then_inc(s_dma.h, 16)
                    s_dma.n += 16

                step(sync=pb_xld)
                x1ld = s_dma.n

                def pb_pj_t(t, b=b, tt=tt):
                    if tt == 0:
                        t.wait_ge(s_dve.h, evq[("anev", b)])
                        t.wait_ge(s_dma.h, DMA_WP)
                    prev = evq.get(("x2ev", b, tt - 1), evq.get(("x2ev", b - 1, 3)))
                    if prev:
                        t.wait_ge(s_dve.h, prev)
                    for c in range(8):
                        lhs = attnn[:, c, tt * 128:tt * 128 + 128]
                        t.matmul(pbc[:, 0:512], lhs, wproj_sb[:, c, 0:512],
                                 start=(c == 0), stop=False)
                        t.matmul(pbc[:, 512:1024], lhs, wproj_sb[:, c, 512:1024],
                                 start=(c == 0), stop=False)
                    t.matmul(pbc[:, 0:512], onesb[:], brows[0:1, 0:512], start=False, stop=True)
                    t.matmul(pbc[:, 512:1024], onesb[:], brows[0:1, 512:1024],
                             start=False, stop=True)
                    t.drain().then_inc(s_pe.h, 1)
                    s_pe.n += 1

                step(tensor=pb_pj_t)
                pe_pj = s_pe.n

                def pb_pj_v(v, b=b, tt=tt, slx=slx, pe_pj=pe_pj, x1ld=x1ld):
                    v.wait_ge(s_pe.h, pe_pj)
                    v.wait_ge(s_dma.h, x1ld)
                    prev_st = evq.get(("x2st", b, tt - 1), evq.get(("x2st", b - 1, 3)))
                    if prev_st:
                        v.wait_ge(s_mv.h, prev_st)
                    v.tensor_tensor(x2b[:], pbc[:], xb1[:], ALU.add)
                    v.drain()
                    nt_ = b * 4 + tt
                    v.reduce_sum(s12[:, nt_, 0:1], x2b[:], axis=AX.X)
                    v.tensor_mul(lnscr[:], x2b[:], x2b[:])
                    v.drain()
                    v.reduce_sum(s12[:, nt_, 1:2], lnscr[:], axis=AX.X)
                    v.drain().then_inc(s_dve.h, 1)
                    s_dve.n += 1

                step(vector=pb_pj_v)
                evq[("x2ev", b, tt)] = s_dve.n
                dve_x2 = s_dve.n

                def pb_x2st(g, b=b, tt=tt, slx=slx, dve_x2=dve_x2):
                    g.wait_ge(s_dve.h, dve_x2)
                    g.dma_start(bass.AP(x2s, (b * 4 + tt) * 128 * D, [[D, 128], [1, D]]),
                                x2b[:]).then_inc(s_mv.h, 16)
                    s_mv.n += 16
                    evq[("x2st", b, tt)] = s_mv.n

                step(gpsimd=pb_x2st)
            evq[("pjdone", b)] = s_pe.n
            if _STOP == "B1":
                break

        if _STOP in ("Battn", "Bnrm", "B1"):
            def finBx(g):
                g.wait_ge(s_mv.h, s_mv.n)
                g.wait_ge(s_dve.h, s_dve.n)
                g.wait_ge(s_act.h, s_act.n)
            step(gpsimd=finBx)
            flush()
            bstk.close()
            abstk.close()
            return nc

        def pb_fin_v(v):
            v.wait_ge(s_dve.h, s_dve.n)
            v.tensor_scalar_mul(m2r2[:, :, 0], s12[:, :, 0], 1.0 / D)
            v.tensor_scalar_mul(lnscr[:, 0:NT], s12[:, :, 1], 1.0 / D)
            v.drain()
            v.tensor_mul(lnscr2[:, 0:NT], m2r2[:, :, 0], m2r2[:, :, 0])
            v.drain()
            v.tensor_sub(lnscr[:, 0:NT], lnscr[:, 0:NT], lnscr2[:, 0:NT])
            v.drain().then_inc(s_dve.h, 1)
            s_dve.n += 1

        step(vector=pb_fin_v)
        dve_var2 = s_dve.n

        def pb_fin_s(s, dve_var2=dve_var2):
            s.wait_ge(s_dve.h, dve_var2)
            s.activation(lnscr2[:, 0:NT], lnscr[:, 0:NT], ACT.Ln, bias=cns[:, 0:1], scale=1.0)
            s.drain()
            s.activation(lnscr[:, 0:NT], lnscr2[:, 0:NT], ACT.Exp, scale=cns[:, 2:3])
            s.drain().then_inc(s_act.h, 1)
            s_act.n += 1

        step(scalar=pb_fin_s)
        act_r2 = s_act.n

        def pb_fin_v2(v, act_r2=act_r2):
            v.wait_ge(s_act.h, act_r2)
            v.tensor_copy(m2r2[:, :, 1], lnscr[:, 0:NT])
            v.drain().then_inc(s_dve.h, 1)
            s_dve.n += 1

        step(vector=pb_fin_v2)
        flush()

        if _DEBUG:
            def dbg_b(g):
                g.wait_ge(s_mv.h, s_mv.n)
                g.dma_start(dbg["x2"].ap(), x2s.ap()).then_inc(s_mv.h, 16)
                s_mv.n += 16

            step(gpsimd=dbg_b)

        tgt2 = s_bar.n + 5

        def mkbar2(e):
            e.nop().then_inc(s_bar.h, 1)
            e.wait_ge(s_bar.h, tgt2)

        step(sync=mkbar2, gpsimd=mkbar2, tensor=mkbar2, vector=mkbar2, scalar=mkbar2)
        flush()
        s_bar.n += 5
        bstk.close()
        abstk.close()

        if _STOP == "B":
            def finB(g):
                g.wait_ge(s_mv.h, s_mv.n)
            step(gpsimd=finB)
            return nc

        cstk = ExitStack()
        sbc = lambda n, s, d: cstk.enter_context(nc.sbuf_tensor(n, s, d))
        enc_sb = sbc("enc_sb", [128, 16, 8, 128], BF16)
        dec_sb = sbc("dec_sb", [128, 16, D], BF16)
        s2c_sb = sbc("s2c_sb", [128, 16, 128], BF16)
        c2s_sb = sbc("c2s_sb", [128, 16, 128], BF16)
        ew1_sb = sbc("ew1_sb", [128, 4, 8, 128], BF16)
        ew2_sb = sbc("ew2_sb", [128, NE, 8, 64], BF16)
        rw_sb = sbc("rw_sb", [64, 64 + NE], BF16)
        x2c = sbc("x2c", [128, 2, 2, D], F32)
        xlnc = sbc("xlnc", [128, D], BF16)
        x2lT = sbc("x2lT", [128, 8, QC], BF16)
        spk = sbc("spk", [128, 16, QC], BF16)
        cont = sbc("cont", [128, QC], BF16)
        rh = sbc("rh", [64, QC], BF16)
        lgt = sbc("lgt", [128, 2, NE], F32)
        msk = sbc("msk", [128, NE], F32)
        cmb = sbc("cmb", [128, 2, NE], F32)
        msk2 = sbc("msk2", [128, 2, NE], F32)
        d12 = sbc("d12", [128, 4], F32)
        comb_t = sbc("comb_t", [8, QC], F32)
        ehs0 = sbc("ehs0", [128, 8, QC], BF16)
        ehs1 = sbc("ehs1", [128, 8, QC], BF16)
        eov = sbc("eov", [128, QC], F32)
        bcv = sbc("bcv", [128, QC], F32)
        ctr = sbc("ctr", [128, QC], F32)
        moe = sbc("moe", [128, QC], F32)
        moeb = sbc("moeb", [128, QC], BF16)
        rts = sbc("rts", [128, 16, QC], BF16)
        dcb = sbc("dcb", [128, 2, D], F32)
        outb = sbc("outb", [128, 2, D], F32)
        encscr = sbc("encscr", [128, 2, QC], F32)
        decscr = sbc("decscr", [128, D], F32)
        pE0 = cstk.enter_context(nc.psum_tensor("pE0", [128, 512], F32))
        pE1 = cstk.enter_context(nc.psum_tensor("pE1", [128, 512], F32))
        pC2 = cstk.enter_context(nc.psum_tensor("pC2", [128, 512], F32))
        pD = cstk.enter_context(nc.psum_tensor("pD", [128, 1024], F32))
        pM = cstk.enter_context(nc.psum_tensor("pM", [128, 512], F32))
        pTt = cstk.enter_context(nc.psum_tensor("pTt", [128, 640], BF16))
        selpr = sbc("selpr", [8, 4, 128], F32)
        gbc = sbc("gbc", [128, D], F32)
        bbc = sbc("bbc", [128, D], F32)
        rb2bc = sbc("rb2bc", [128, NE], F32)
        brow2 = sbc("brow2", [1, D], BF16)

        def pcw(sy):
            sy.wait_ge(s_w.h, W_CASTS)
            sy.dma_start(brow2[:], bass.AP(brow_pre, D, [[D, 1], [1, D]])).then_inc(s_dma.h, 16)
            s_dma.n += 16
            sy.dma_start(enc_sb[:], bass.AP(enc_pre, 0, [[128, 128], [8 * 128 * 128, 16],
                                                         [128 * 128, 8], [1, 128]]))\
              .then_inc(s_dma.h, 16)
            sy.dma_start(dec_sb[:], bass.AP(dec_nat, 0, [[D, 128], [128 * D, 16], [1, D]]))\
              .then_inc(s_dma.h, 16)
            sy.dma_start(s2c_sb[:], bass.AP(s2c_pre, 0, [[128, 128], [128 * 128, 16], [1, 128]]))\
              .then_inc(s_dma.h, 16)
            sy.dma_start(c2s_sb[:], bass.AP(c2s_pre, 0, [[128, 128], [128 * 128, 16], [1, 128]]))\
              .then_inc(s_dma.h, 16)
            sy.dma_start(ew1_sb[:], bass.AP(ew1_pre, 0, [[128, 128], [8 * 128 * 128, 4],
                                                         [128 * 128, 8], [1, 128]]))\
              .then_inc(s_dma.h, 16)
            sy.dma_start(ew2_sb[:], bass.AP(ew2_pre, 0, [[64, 128], [8 * 128 * 64, NE],
                                                         [128 * 64, 8], [1, 64]]))\
              .then_inc(s_dma.h, 16)
            sy.dma_start(rw_sb[:], bass.AP(rw_pre, 0, [[64 + NE, 64], [1, 64 + NE]]))\
              .then_inc(s_dma.h, 16)
            s_dma.n += 7 * 16

        step(sync=pcw)
        DMA_FW = s_dma.n

        def pc_selpr(g):
            g.iota(selpr[:], [[2, 4], [1, 2], [0, 64]], channel_multiplier=-1,
                   allow_small_or_imprecise_dtypes=True)

        step(gpsimd=pc_selpr)
        flush()

        def pc_selprg(g):
            g.drain().then_inc(s_g.h, 1)
            s_g.n += 1

        step(gpsimd=pc_selprg)
        flush()
        SELPR_I = s_g.n

        def pc_selpr2(v):
            v.wait_ge(s_g.h, SELPR_I)
            v.tensor_single_scalar(selpr[:], selpr[:], 0.0, ALU.is_equal)
            v.drain().then_inc(s_dve.h, 1)
            s_dve.n += 1

        step(vector=pc_selpr2)
        SELPR_OK = s_dve.n

        bc_specs = [(gbc, 0, 0), (gbc, 512, 4), (bbc, 0, 8), (bbc, 512, 12)]
        prev_copy = [0]
        for (dstt, off, r0) in bc_specs:
            def p0_bt(t, r0=r0):
                if prev_copy[0]:
                    t.wait_ge(s_dve.h, prev_copy[0])
                for i in range(4):
                    t.matmul(pD[:, i * 128:i * 128 + 128], selh[:, r0 + i, :],
                             growt[0:16, :], start=True, stop=True)
                t.drain().then_inc(s_pe.h, 1)
                s_pe.n += 1

            step(tensor=p0_bt)
            pem = s_pe.n

            def p0_bv(v, dstt=dstt, off=off, pem=pem):
                v.wait_ge(s_pe.h, pem)
                v.tensor_copy(dstt[:, off:off + 512], pD[:, 0:512])
                v.drain().then_inc(s_dve.h, 1)
                s_dve.n += 1

            step(vector=p0_bv)
            prev_copy[0] = s_dve.n

        def p0_rb2t(t):
            t.wait_ge(s_dve.h, prev_copy[0])
            t.matmul(pD[:, 0:8], onesf[:], crow2[0:1, 0:8], start=True, stop=True)
            t.drain().then_inc(s_pe.h, 1)
            s_pe.n += 1

        step(tensor=p0_rb2t)
        PE_RB2 = s_pe.n

        def p0_rb2v(v):
            v.wait_ge(s_pe.h, PE_RB2)
            v.tensor_copy(rb2bc[:], pD[:, 0:8])
            v.drain().then_inc(s_dve.h, 1)
            s_dve.n += 1

        step(vector=p0_rb2v)
        BCDONE = s_dve.n

        for j in range(NCB if _STOP != "C1" else 1):
            slj = j % 2

            def pc_x2ld(sy, j=j, slj=slj):
                tt_hi = ((j + 1) * QC - 1) // 128
                b_hi, t_hi = divmod(tt_hi, 4)
                sy.wait_ge(s_mv.h, evq[("x2st", b_hi, t_hi)])
                prev = evq.get(("outev", j - 2))
                if prev:
                    sy.wait_ge(s_dve.h, prev)
                sy.dma_start(x2c[:, slj, :, :],
                             bass.AP(x2s, j * QC * D, [[D, 128], [128 * D, 2], [1, D]]))\
                  .then_inc(s_dma.h, 16)
                s_dma.n += 16

            step(sync=pc_x2ld)
            x2ld = s_dma.n

            tp_marks = []
            for tt in range(2):
                def pc_ln_v(v, j=j, tt=tt, slj=slj, x2ld=x2ld):
                    if tt == 0:
                        v.wait_ge(s_dma.h, x2ld)
                    nt_ = j * 2 + tt
                    if tt > 0 or j > 0:
                        v.wait_ge(s_pe.h, evq.get(("tpc", j, tt - 1),
                                                  evq.get(("tpc", j - 1, 1), 0)))
                    v.tensor_scalar(xlnc[:], x2c[:, slj, tt, :],
                                    m2r2[:, nt_, 0:1], m2r2[:, nt_, 1:2],
                                    ALU.subtract, ALU.mult).then_inc(s_dve.h, 1)
                    s_dve.n += 1

                step(vector=pc_ln_v)
                dve_xlnc = s_dve.n

                for grp in range(2):
                    def pc_tp_t(t, j=j, tt=tt, grp=grp, dve_xlnc=dve_xlnc):
                        if grp == 0:
                            t.wait_ge(s_dve.h, dve_xlnc)
                        if tp_marks:
                            t.wait_ge(s_dve.h, tp_marks[-1])
                        else:
                            prevtp = evq.get(("tpv", j - 1))
                            if prevtp:
                                t.wait_ge(s_dve.h, prevtp)
                        for i in range(4):
                            dc = grp * 4 + i
                            t.transpose(pTt[:, i * 128:i * 128 + 128],
                                        xlnc[:, dc * 128:dc * 128 + 128], identb[:])
                        t.matmul(pTt[:].bitcast(F32)[0:4, 256:260], onesb[0:1, 0:4],
                                 onesb[0:1, 0:4],
                                 start=True, stop=True)
                        t.drain().then_inc(s_pe.h, 1)
                        s_pe.n += 1

                    step(tensor=pc_tp_t)
                    pem = s_pe.n
                    evq[("tpc", j, tt)] = s_pe.n

                    def pc_tp_v(v, j=j, tt=tt, grp=grp, pem=pem):
                        v.wait_ge(s_pe.h, pem)
                        for i in range(4):
                            dc = grp * 4 + i
                            cp = v.tensor_copy(x2lT[:, dc, tt * 128:tt * 128 + 128],
                                               pTt[:, i * 128:i * 128 + 128])
                        cp.then_inc(s_dve.h, 1)
                        s_dve.n += 1
                        tp_marks.append(s_dve.n)

                    step(vector=pc_tp_v)
            evq[("tpv", j)] = s_dve.n
            tplast = s_dve.n

            if _CLVL < 2:
                continue
            spk_marks = {}
            enc_dve = {}
            base_dve_enc = [0]

            def t_enc(t, j=j, tplast=tplast):
                for hc in range(16):
                    if hc == 0:
                        t.wait_ge(s_dve.h, tplast)
                        t.wait_ge(s_dma.h, DMA_FW)
                    pp = pE0 if hc % 2 == 0 else pE1
                    prev = enc_dve.get(hc - 2, evq.get(("encdve14", j - 1, hc)))
                    if prev:
                        t.wait_ge(s_dve.h, prev)
                    if hc <= 1:
                        pr = evq.get(("rts", j - 1, 14 + hc))
                        if pr:
                            t.wait_ge(s_act.h, pr)
                    mm = None
                    for dc in range(8):
                        mm = t.matmul(pp[:, 0:QC], enc_sb[:, hc, dc, :], x2lT[:, dc, :],
                                      start=(dc == 0), stop=(dc == 7))
                    if hc == 15:
                        t.drain().then_inc(s_pe.h, 1)
                    else:
                        mm.then_inc(s_pe.h, 1)
                    s_pe.n += 1
                    evq[("encpe", j, hc)] = s_pe.n

            def v_enc(v, j=j):
                for hc in range(16):
                    v.wait_ge(s_pe.h, evq[("encpe", j, min(hc + 1, 15))])
                    prevs = spk_marks.get(hc - 2, evq.get(("spk14", j - 1, hc)))
                    if prevs:
                        v.wait_ge(s_act.h, prevs)
                    pp = pE0 if hc % 2 == 0 else pE1
                    v.tensor_copy(encscr[:, hc % 2, :], pp[:, 0:QC]).then_inc(s_dve.h, 1)
                    s_dve.n += 1
                    enc_dve[hc] = s_dve.n
                    if hc >= 14:
                        evq[("encdve14", j, hc - 14)] = s_dve.n

            def s_enc(s, j=j):
                for hc in range(16):
                    s.wait_ge(s_dve.h, enc_dve[hc])
                    s.activation(spk[:, hc, :], encscr[:, hc % 2, :], _SIG(),
                                 bias=encb_c[:, hc:hc + 1], scale=1.0).then_inc(s_act.h, 1)
                    s_act.n += 1
                    spk_marks[hc] = s_act.n
                    if hc >= 14:
                        evq[("spk14", j, hc - 14)] = s_act.n

            base_pe_enc = s_pe.n
            for hc in range(16):
                evq[("encpe", j, hc)] = base_pe_enc + hc + 1
            base_dve = s_dve.n
            for hc in range(16):
                enc_dve[hc] = base_dve + hc + 1
            step(tensor=t_enc, vector=v_enc, scalar=s_enc)
            spk_marks15 = spk_marks[15]

            def pc_s2c_t(t, j=j):
                t.wait_ge(s_act.h, spk_marks15)
                prev = evq.get(("moedone", j - 1))
                if prev:
                    t.wait_ge(s_dve.h, prev)
                for hc in range(16):
                    t.matmul(pC2[:, 0:QC], s2c_sb[:, hc, :], spk[:, hc, :],
                             start=(hc == 0), stop=(hc == 15))
                t.drain().then_inc(s_pe.h, 1)
                s_pe.n += 1

            step(tensor=pc_s2c_t)
            pe_s2c = s_pe.n

            def pc_s2c_v(v, j=j, pe_s2c=pe_s2c):
                v.wait_ge(s_pe.h, pe_s2c)
                v.tensor_scalar(cont[:], pC2[:, 0:QC], s2cb_c[:, 0:1], None,
                                ALU.add).then_inc(s_dve.h, 1)
                s_dve.n += 1

            step(vector=pc_s2c_v)
            dve_cont = s_dve.n

            if _CLVL < 4:
                continue
            def pc_rt_t(t, j=j, dve_cont=dve_cont):
                t.wait_ge(s_dve.h, dve_cont)
                t.matmul(pM[0:64, 0:QC], rw_sb[0:64, 0:64], cont[0:64, :],
                         start=True, stop=True).then_inc(s_pe.h, 1)
                s_pe.n += 1

            step(tensor=pc_rt_t)
            pe_rh = s_pe.n

            def pc_rt_v(v, j=j, pe_rh=pe_rh):
                v.wait_ge(s_pe.h, pe_rh)
                v.tensor_scalar(rh[:], pM[0:64, 0:QC], rb1_c[:, 0:1], 0.0,
                                ALU.add, ALU.max).then_inc(s_dve.h, 1)
                s_dve.n += 1

            step(vector=pc_rt_v)
            dve_rh = s_dve.n

            if _CLVL < 5:
                continue
            def pc_lg_t(t, j=j, dve_rh=dve_rh):
                t.wait_ge(s_dve.h, dve_rh)
                for tt in range(2):
                    mm = t.matmul(pM[:, 256 + tt * 16:256 + tt * 16 + NE],
                                  rh[0:64, tt * 128:tt * 128 + 128], rw_sb[0:64, 64:64 + NE],
                                  start=True, stop=True)
                mm.then_inc(s_pe.h, 1)
                s_pe.n += 1

            step(tensor=pc_lg_t)
            pe_lg = s_pe.n

            def pc_top2(v, j=j, pe_lg=pe_lg):
                v.wait_ge(s_pe.h, pe_lg)
                if j == 0:
                    v.wait_ge(s_dve.h, BCDONE)
                for tt in range(2):
                    v.tensor_tensor(lgt[:, tt, :], pM[:, 256 + tt * 16:256 + tt * 16 + NE],
                                    rb2bc[:], ALU.add)
                v.drain()
                for tt in range(2):
                    lt = lgt[:, tt, :]
                    v.reduce_max(cst3[:, 0:1], lt, axis=AX.X)
                    v.drain()
                    v.tensor_single_scalar(cmb[:, tt, :], lt, cst3[:, 0:1], ALU.is_ge)
                    v.drain()
                    v.scalar_tensor_tensor(msk[:], cmb[:, tt, :], -1e30, lt, ALU.mult, ALU.add)
                    v.drain()
                    v.reduce_max(cst3[:, 1:2], msk[:], axis=AX.X)
                    v.drain()
                    v.tensor_single_scalar(msk2[:, tt, :], msk[:], cst3[:, 1:2], ALU.is_ge)
                    tl = v.tensor_tensor(d12[:, tt:tt + 1], cst3[:, 0:1], cst3[:, 1:2],
                                         ALU.subtract)
                tl.then_inc(s_dve.h, 1)
                s_dve.n += 1

            step(vector=pc_top2)
            dve_t2 = s_dve.n

            def pc_w1_s(s, j=j, dve_t2=dve_t2):
                s.wait_ge(s_dve.h, dve_t2)
                s.activation(d12[:, 2:4], d12[:, 0:2], _SIG()).then_inc(s_act.h, 1)
                s_act.n += 1

            step(scalar=pc_w1_s)
            act_w1 = s_act.n

            def pc_cmb_v(v, j=j, act_w1=act_w1):
                v.wait_ge(s_act.h, act_w1)
                for tt in range(2):
                    v.tensor_scalar(cst3[:, 4 + tt:5 + tt], d12[:, 2 + tt:3 + tt], -1.0, 1.0,
                                    ALU.mult, ALU.add)
                v.drain()
                for tt in range(2):
                    v.tensor_scalar(cmb[:, tt, :], cmb[:, tt, :], d12[:, 2 + tt:3 + tt],
                                    None, ALU.mult)
                v.drain()
                for tt in range(2):
                    v.tensor_scalar(msk2[:, tt, :], msk2[:, tt, :], cst3[:, 4 + tt:5 + tt],
                                    None, ALU.mult)
                v.drain()
                for tt in range(2):
                    tl = v.tensor_tensor(cmb[:, tt, :], cmb[:, tt, :], msk2[:, tt, :], ALU.add)
                tl.then_inc(s_dve.h, 1)
                s_dve.n += 1

            step(vector=pc_cmb_v)
            dve_cmb = s_dve.n

            if _CLVL < 6:
                continue
            def pc_ct_t(t, j=j, dve_cmb=dve_cmb):
                t.wait_ge(s_dve.h, dve_cmb)
                for tt in range(2):
                    t.transpose(pM[0:8, tt * 128:tt * 128 + 128], cmb[:, tt, :], identf[:])
                t.matmul(pM[0:4, 508:512], onesf[0:1, 0:4], onesf[0:1, 0:4], start=True, stop=True)
                t.drain().then_inc(s_pe.h, 1)
                s_pe.n += 1

            step(tensor=pc_ct_t)
            pe_ct = s_pe.n

            def pc_ct_v(v, j=j, pe_ct=pe_ct):
                v.wait_ge(s_pe.h, pe_ct)
                v.tensor_copy(comb_t[:], pM[0:8, 0:QC]).then_inc(s_dve.h, 1)
                s_dve.n += 1

            step(vector=pc_ct_v)
            dve_ct = s_dve.n

            if _CLVL < 7:
                continue
            moe_marks = []
            for p in range(4):
                eh_marks = []
                for hc in range(8):
                    pebox = []

                    def t_eh(t, j=j, p=p, hc=hc, pebox=pebox):
                        if p == 0 and hc == 0:
                            t.wait_ge(s_dve.h, dve_cont)
                        ppA = pE0 if hc % 2 == 0 else pC2
                        ppB = pE1 if hc % 2 == 0 else pM
                        prev = eh_marks[hc - 2] if hc >= 2 else \
                            evq.get(("ehm", j, p - 1, 6 + hc), evq.get(("ehm", j - 1, 3, 6 + hc)))
                        if prev:
                            t.wait_ge(s_dve.h, prev)
                        if hc <= 1:
                            prevmoe = moe_marks[p - 1] if p >= 1 else evq.get(("moedone", j - 1))
                            if prevmoe:
                                t.wait_ge(s_dve.h, prevmoe)
                            if p == 0:
                                t.wait_ge(s_dve.h, dve_ct)
                        if not _NOEH:
                            t.matmul(ppA[:, 0:QC], ew1_sb[0:64, p, hc, :], cont[0:64, :],
                                     start=True, stop=True)
                            t.matmul(ppB[:, 0:QC], ew1_sb[64:128, p, hc, :],
                                     cont[64:128, :], start=True,
                                     stop=True).then_inc(s_pe.h, 1)
                        else:
                            t.drain().then_inc(s_pe.h, 1)
                        s_pe.n += 1
                        pebox.append(s_pe.n)

                    def v_eh(v, j=j, p=p, hc=hc, pebox=pebox):
                        v.wait_ge(s_pe.h, pebox[0])
                        ppA = pE0 if hc % 2 == 0 else pC2
                        ppB = pE1 if hc % 2 == 0 else pM
                        v.tensor_scalar(ehs0[:, hc, :], ppA[:, 0:QC],
                                        eb1_c[:, 2 * p, hc:hc + 1], 0.0, ALU.add, ALU.max)
                        v.tensor_scalar(ehs1[:, hc, :], ppB[:, 0:QC],
                                        eb1_c[:, 2 * p + 1, hc:hc + 1], 0.0,
                                        ALU.add, ALU.max).then_inc(s_dve.h, 1)
                        s_dve.n += 1
                        eh_marks.append(s_dve.n)
                        evq[("ehm", j, p, hc)] = s_dve.n

                    step(tensor=t_eh, vector=v_eh)

                def pc_eo_t(t, j=j, p=p, eh_marks=eh_marks, dve_ct=dve_ct):
                    prev = moe_marks[p - 1] if p >= 1 else evq.get(("moedone", j - 1))
                    if prev:
                        t.wait_ge(s_dve.h, prev)
                    if p == 0:
                        pdd = evq.get(("decdve", j - 1, 1))
                        if pdd:
                            t.wait_ge(s_dve.h, pdd)
                    for hc in range(8):
                        t.wait_ge(s_dve.h, eh_marks[hc])
                        if not _NOEO:
                            t.matmul(pD[0:64, 0:QC], ew2_sb[:, 2 * p, hc, :], ehs0[:, hc, :],
                                     start=(hc == 0), stop=(hc == 7), tile_position=(0, 0))
                            t.matmul(pD[64:128, 512:512 + QC], ew2_sb[:, 2 * p + 1, hc, :],
                                     ehs1[:, hc, :],
                                     start=(hc == 0), stop=(hc == 7), tile_position=(0, 64))
                    t.wait_ge(s_dve.h, dve_ct)
                    if j == 0:
                        t.wait_ge(s_dve.h, SELPR_OK)
                    if not _NOBC:
                        t.matmul(pM[:, 256:256 + QC], selpr[:, p, :], comb_t[:],
                                 start=True, stop=True)
                    t.drain().then_inc(s_pe.h, 1)
                    s_pe.n += 1

                step(tensor=pc_eo_t)
                pe_eo = s_pe.n

                def pc_eo_v(v, j=j, p=p, pe_eo=pe_eo):
                    v.wait_ge(s_pe.h, pe_eo)
                    v.tensor_scalar(eov[0:64, :], pD[0:64, 0:QC], eb2_c[0:64, p:p + 1],
                                    None, ALU.add)
                    v.tensor_scalar(eov[64:128, :], pD[64:128, 512:512 + QC],
                                    eb2_c[64:128, p:p + 1], None, ALU.add)
                    v.tensor_copy(bcv[:], pM[:, 256:256 + QC])
                    v.drain()
                    if p == 0:
                        v.tensor_tensor(moe[:], eov[:], bcv[:],
                                        ALU.mult).then_inc(s_dve.h, 1)
                    else:
                        v.tensor_tensor(ctr[:], eov[:], bcv[:], ALU.mult)
                        v.drain()
                        v.tensor_tensor(moe[:], moe[:], ctr[:],
                                        ALU.add).then_inc(s_dve.h, 1)
                    s_dve.n += 1
                    moe_marks.append(s_dve.n)

                step(vector=pc_eo_v)
            evq[("moedone", j)] = moe_marks[-1]

            if _CLVL < 8:
                continue
            def pc_moeb(v, j=j):
                v.wait_ge(s_dve.h, moe_marks[-1])
                v.tensor_copy(moeb[:], moe[:]).then_inc(s_dve.h, 1)
                s_dve.n += 1

            step(vector=pc_moeb)
            dve_moeb = s_dve.n

            if _CLVL < 9:
                continue
            rts_marks = {}
            c2s_pe = {}

            def t_c2s(t, j=j, dve_moeb=dve_moeb):
                for hc in range(16):
                    if hc == 0:
                        t.wait_ge(s_dve.h, dve_moeb)
                    pp = pE0 if hc % 2 == 0 else pE1
                    prev = rts_marks.get(hc - 2)
                    if prev:
                        t.wait_ge(s_act.h, prev)
                    t.matmul(pp[:, 0:QC], c2s_sb[:, hc, :], moeb[:], start=True,
                             stop=True).then_inc(s_pe.h, 1)
                    s_pe.n += 1
                    c2s_pe[hc] = s_pe.n

            def s_c2s(s, j=j):
                for hc in range(16):
                    s.wait_ge(s_pe.h, c2s_pe[hc])
                    pp = pE0 if hc % 2 == 0 else pE1
                    s.activation(rts[:, hc, :], pp[:, 0:QC], _SIG(),
                                 bias=c2sb_c[:, hc:hc + 1], scale=1.0).then_inc(s_act.h, 1)
                    s_act.n += 1
                    rts_marks[hc] = s_act.n

            base_act_c2s = s_act.n
            for hc in range(16):
                rts_marks[hc] = base_act_c2s + hc + 1
            evq[("rts", j, 14)] = rts_marks[14]
            evq[("rts", j, 15)] = rts_marks[15]
            step(tensor=t_c2s, scalar=s_c2s)
            rts_last = rts_marks[15]

            dec_marks = []
            for tt in range(2):
                def pc_dec_t(t, j=j, tt=tt, rts_last=rts_last):
                    t.wait_ge(s_act.h, rts_last)
                    prev = evq.get(("decdve", j, tt - 1), evq.get(("decdve", j - 1, 1)))
                    if prev:
                        t.wait_ge(s_dve.h, prev)
                    for hc in range(16):
                        lhs = rts[:, hc, tt * 128:tt * 128 + 128]
                        t.matmul(pD[:, 0:512], lhs, dec_sb[:, hc, 0:512],
                                 start=(hc == 0), stop=False)
                        t.matmul(pD[:, 512:1024], lhs, dec_sb[:, hc, 512:1024],
                                 start=(hc == 0), stop=False)
                    t.matmul(pD[:, 0:512], onesb[:], brow2[0:1, 0:512],
                             start=False, stop=True)
                    t.matmul(pD[:, 512:1024], onesb[:], brow2[0:1, 512:1024],
                             start=False, stop=True)
                    t.drain().then_inc(s_pe.h, 1)
                    s_pe.n += 1

                step(tensor=pc_dec_t)
                pe_dec = s_pe.n

                def pc_dec_v(v, j=j, tt=tt, pe_dec=pe_dec):
                    v.wait_ge(s_pe.h, pe_dec)
                    prevs = evq.get(("decact", j, tt - 1), evq.get(("decact", j - 1, 1)))
                    if prevs:
                        v.wait_ge(s_act.h, prevs)
                    v.tensor_copy(decscr[:], pD[:]).then_inc(s_dve.h, 1)
                    s_dve.n += 1

                step(vector=pc_dec_v)
                dec_dve = s_dve.n
                evq[("decdve", j, tt)] = dec_dve

                def pc_dec_s(s, j=j, tt=tt, dec_dve=dec_dve):
                    s.wait_ge(s_dve.h, dec_dve)
                    s.activation(dcb[:, tt, :], decscr[:], _SIG()).then_inc(s_act.h, 1)
                    s_act.n += 1
                    dec_marks.append(s_act.n)
                    evq[("decact", j, tt)] = s_act.n

                step(scalar=pc_dec_s)

            if _CLVL < 11:
                continue
            def pc_l3_v(v, j=j, dec_marks=dec_marks):
                v.wait_ge(s_act.h, dec_marks[-1])
                for tt in range(2):
                    v.reduce_sum(cst3[:, 8 + 4 * tt:9 + 4 * tt], dcb[:, tt, :], axis=AX.X)
                    v.tensor_mul(lnscr[:], dcb[:, tt, :], dcb[:, tt, :])
                    v.drain()
                    v.reduce_sum(cst3[:, 9 + 4 * tt:10 + 4 * tt], lnscr[:], axis=AX.X)
                v.drain()
                for tt in range(2):
                    v.tensor_scalar_mul(cst3[:, 8 + 4 * tt:9 + 4 * tt],
                                        cst3[:, 8 + 4 * tt:9 + 4 * tt], 1.0 / D)
                    v.tensor_scalar_mul(cst3[:, 9 + 4 * tt:10 + 4 * tt],
                                        cst3[:, 9 + 4 * tt:10 + 4 * tt], 1.0 / D)
                v.drain()
                for tt in range(2):
                    v.tensor_mul(cst3[:, 10 + 4 * tt:11 + 4 * tt],
                                 cst3[:, 8 + 4 * tt:9 + 4 * tt], cst3[:, 8 + 4 * tt:9 + 4 * tt])
                v.drain()
                for tt in range(2):
                    tl = v.tensor_sub(cst3[:, 10 + 4 * tt:11 + 4 * tt],
                                      cst3[:, 9 + 4 * tt:10 + 4 * tt],
                                      cst3[:, 10 + 4 * tt:11 + 4 * tt])
                tl.then_inc(s_dve.h, 1)
                s_dve.n += 1

            step(vector=pc_l3_v)
            dve_v3 = s_dve.n

            def pc_l3_s(s, j=j, dve_v3=dve_v3):
                s.wait_ge(s_dve.h, dve_v3)
                for tt in range(2):
                    s.activation(cst3[:, 11 + 4 * tt:12 + 4 * tt],
                                 cst3[:, 10 + 4 * tt:11 + 4 * tt], ACT.Ln, bias=cns[:, 0:1], scale=1.0)
                s.drain()
                for tt in range(2):
                    al = s.activation(cst3[:, 10 + 4 * tt:11 + 4 * tt],
                                      cst3[:, 11 + 4 * tt:12 + 4 * tt], ACT.Exp,
                                      scale=cns[:, 2:3])
                al.then_inc(s_act.h, 1)
                s_act.n += 1

            step(scalar=pc_l3_s)
            act_r3 = s_act.n

            def pc_fin_v(v, j=j, slj=slj, act_r3=act_r3):
                v.wait_ge(s_act.h, act_r3)
                if j == 0:
                    v.wait_ge(s_dve.h, BCDONE)
                prev = evq.get(("ost", j - 1))
                if prev:
                    v.wait_ge(s_out.h, prev)
                for tt in range(2):
                    v.tensor_scalar(lnscr[:], dcb[:, tt, :],
                                    cst3[:, 8 + 4 * tt:9 + 4 * tt],
                                    cst3[:, 10 + 4 * tt:11 + 4 * tt],
                                    ALU.subtract, ALU.mult)
                    v.drain()
                    v.tensor_tensor(lnscr2[:], lnscr[:], gbc[:], ALU.mult)
                    v.drain()
                    v.tensor_tensor(lnscr[:], lnscr2[:], bbc[:], ALU.add)
                    v.drain()
                    tl = v.tensor_tensor(outb[:, tt, :], lnscr[:], x2c[:, slj, tt, :],
                                         ALU.add)
                tl.then_inc(s_dve.h, 1)
                s_dve.n += 1

            step(vector=pc_fin_v)
            evq[("outev", j)] = s_dve.n
            dve_out = s_dve.n

            def pc_ost(g, j=j, dve_out=dve_out):
                g.wait_ge(s_dve.h, dve_out)
                g.dma_start(bass.AP(out, j * QC * D, [[D, 128], [128 * D, 2], [1, D]]),
                            outb[:]).then_inc(s_out.h, 16)
                s_out.n += 16
                evq[("ost", j)] = s_out.n

            step(gpsimd=pc_ost)

            if _DEBUG and j == 0:
                def dbg_c(g, upto=s_dve.n):
                    g.wait_ge(s_dve.h, upto)
                    g.dma_start(dbg["spk"].ap(), spk[:]).then_inc(s_mv.h, 16)
                    g.dma_start(dbg["cont"].ap(), cont[:]).then_inc(s_mv.h, 16)
                    g.dma_start(dbg["comb"].ap(), cmb[:]).then_inc(s_mv.h, 16)
                    g.dma_start(dbg["moe"].ap(), moe[:]).then_inc(s_mv.h, 16)
                    g.dma_start(dbg["dec"].ap(), dcb[:]).then_inc(s_mv.h, 16)
                    g.dma_start(dbg["lgt"].ap(), lgt[:]).then_inc(s_mv.h, 16)
                    g.dma_start(dbg["rh"].ap(), rh[:]).then_inc(s_mv.h, 16)
                    s_mv.n += 112

                step(gpsimd=dbg_c)

        if _STOP == "C1":
            def finC(g):
                g.wait_ge(s_out.h, s_out.n)
                g.wait_ge(s_dve.h, s_dve.n)
                g.wait_ge(s_dma.h, s_dma.n)
            step(gpsimd=finC)
            flush()
            cstk.close()
            return nc

        def fin(g):
            g.wait_ge(s_out.h, s_out.n)

        step(gpsimd=fin)
        flush()
        cstk.close()

    return nc


def kernel(**inputs):
    x = np.ascontiguousarray(np.asarray(inputs["L0_attn_x"], dtype=np.float32))
    names = {
        "w_qkv": "L0_attn_w_qkv", "w_proj": "L0_attn_w_proj", "b_proj": "L0_attn_b_proj",
        "enc_w": "L0_ffn_enc_w", "enc_b": "L0_ffn_enc_b", "s2c_w": "L0_ffn_s2c_w",
        "s2c_b": "L0_ffn_s2c_b", "r_w1": "L0_ffn_r_w1", "r_b1": "L0_ffn_r_b1",
        "r_w2": "L0_ffn_r_w2", "r_b2": "L0_ffn_r_b2", "e_w1": "L0_ffn_e_w1",
        "e_b1": "L0_ffn_e_b1", "e_w2": "L0_ffn_e_w2", "e_b2": "L0_ffn_e_b2",
        "c2s_w": "L0_ffn_c2s_w", "c2s_b": "L0_ffn_c2s_b", "dec_w": "L0_ffn_dec_w",
        "dec_b": "L0_ffn_dec_b", "ln_g": "L0_ffn_ln_g", "ln_b": "L0_ffn_ln_b",
    }
    wmap = {k: np.ascontiguousarray(np.asarray(inputs[v], dtype=np.float32))
            for k, v in names.items()}
    nc = build_core()
    in_maps = []
    for c in range(8):
        m = {"x": np.ascontiguousarray(x[c])}
        m.update(wmap)
        in_maps.append(m)
    res = run_bass_kernel_spmd(nc, in_maps, core_ids=list(range(8)))
    if _DEBUG:
        kernel.dbg = res.results
    return np.stack([res.results[c]["out"] for c in range(8)], axis=0)


if __name__ == "__main__":
    build_core()
    print("built OK")


# version 23 (final)
# speedup vs baseline: 1.2604x; 1.0082x over previous
import numpy as np
from contextlib import ExitStack

import concourse.bass as bass
import concourse.mybir as mybir
from concourse.bass_utils import run_bass_kernel_spmd

F32 = mybir.dt.float32
BF16 = mybir.dt.bfloat16
AX = mybir.AxisListType
ALU = mybir.AluOpType
ACT = mybir.ActivationFunctionType


def _SIG():
    return ACT.Exp if _SIG2EXP else ACT.Sigmoid

T = 4096
D = 1024
H = 16
HID = 2048
MD = 64
NE = 8
EH = 1024
NT = T // 128
QB = 512
NB = T // QB
QC = 256
NCB = T // QC
EPS = 1e-5

import os as _os
_DEBUG = _os.environ.get("K_DEBUG", "") == "1"
_STOP = _os.environ.get("K_STOP", "")
_NORSD = _os.environ.get("K_NORSD", "") == "1"
_NOMV = _os.environ.get("K_NOMV", "") == "1"
_HPN = int(_os.environ.get("K_HP", "8"))
_SONLY = _os.environ.get("K_SONLY", "") == "1"
_NOEXP = _os.environ.get("K_NOEXP", "") == "1"
_NOB = _os.environ.get("K_NOB", "") == "1"
_CLVL = int(_os.environ.get("K_CLVL", "99"))
_NOEH = _os.environ.get("K_NOEH", "") == "1"
_NOEO = _os.environ.get("K_NOEO", "") == "1"
_NOBC = _os.environ.get("K_NOBC", "") == "1"
_SIG2EXP = _os.environ.get("K_SIG2EXP", "") == "1"
_NOENCACT = _os.environ.get("K_NOENCACT", "") == "1"
_NOENCMM = _os.environ.get("K_NOENCMM", "") == "1"


class Sem:
    def __init__(self, h):
        self.h = h
        self.n = 0


def build_core():
    nc = bass.Bass("TRN2", target_bir_lowering=False, detect_race_conditions=False)

    xd = nc.dram_tensor("x", [T, D], F32, kind="ExternalInput")
    w_qkv = nc.dram_tensor("w_qkv", [D, 3 * D], F32, kind="ExternalInput")
    w_proj = nc.dram_tensor("w_proj", [D, D], F32, kind="ExternalInput")
    b_proj = nc.dram_tensor("b_proj", [D], F32, kind="ExternalInput")
    enc_w = nc.dram_tensor("enc_w", [D, HID], F32, kind="ExternalInput")
    enc_b = nc.dram_tensor("enc_b", [HID], F32, kind="ExternalInput")
    s2c_w = nc.dram_tensor("s2c_w", [HID, MD], F32, kind="ExternalInput")
    s2c_b = nc.dram_tensor("s2c_b", [MD], F32, kind="ExternalInput")
    r_w1 = nc.dram_tensor("r_w1", [MD, 64], F32, kind="ExternalInput")
    r_b1 = nc.dram_tensor("r_b1", [64], F32, kind="ExternalInput")
    r_w2 = nc.dram_tensor("r_w2", [64, NE], F32, kind="ExternalInput")
    r_b2 = nc.dram_tensor("r_b2", [NE], F32, kind="ExternalInput")
    e_w1 = nc.dram_tensor("e_w1", [NE, MD, EH], F32, kind="ExternalInput")
    e_b1 = nc.dram_tensor("e_b1", [NE, EH], F32, kind="ExternalInput")
    e_w2 = nc.dram_tensor("e_w2", [NE, EH, MD], F32, kind="ExternalInput")
    e_b2 = nc.dram_tensor("e_b2", [NE, MD], F32, kind="ExternalInput")
    c2s_w = nc.dram_tensor("c2s_w", [MD, HID], F32, kind="ExternalInput")
    c2s_b = nc.dram_tensor("c2s_b", [HID], F32, kind="ExternalInput")
    dec_w = nc.dram_tensor("dec_w", [HID, D], F32, kind="ExternalInput")
    dec_b = nc.dram_tensor("dec_b", [D], F32, kind="ExternalInput")
    ln_g = nc.dram_tensor("ln_g", [D], F32, kind="ExternalInput")
    ln_b = nc.dram_tensor("ln_b", [D], F32, kind="ExternalInput")
    out = nc.dram_tensor("out", [T, D], F32, kind="ExternalOutput")

    qts = nc.dram_tensor("qts", [8, 128, T], BF16, kind="Internal")
    x2s = nc.dram_tensor("x2s", [T, D], F32, kind="Internal")
    wqk_pre = nc.dram_tensor("wqk_pre", [16, 8, 128, 128], BF16, kind="Internal")
    wv_nat = nc.dram_tensor("wv_nat", [8, 128, D], BF16, kind="Internal")
    wproj_n = nc.dram_tensor("wproj_n", [8, 128, D], BF16, kind="Internal")
    enc_pre = nc.dram_tensor("enc_pre", [16, 8, 128, 128], BF16, kind="Internal")
    s2c_pre = nc.dram_tensor("s2c_pre", [16, 128, 128], BF16, kind="Internal")
    ew1_pre = nc.dram_tensor("ew1_pre", [4, 8, 128, 128], BF16, kind="Internal")
    ew2_pre = nc.dram_tensor("ew2_pre", [NE, 8, 128, 64], BF16, kind="Internal")
    c2s_pre = nc.dram_tensor("c2s_pre", [16, 128, 128], BF16, kind="Internal")
    dec_nat = nc.dram_tensor("dec_nat", [16, 128, D], BF16, kind="Internal")
    rw_pre = nc.dram_tensor("rw_pre", [64, 64 + NE], BF16, kind="Internal")
    brow_pre = nc.dram_tensor("brow_pre", [2, D], BF16, kind="Internal")

    dbg = {}
    if _DEBUG:
        dbg["kt"] = nc.dram_tensor("dbg_kt", [128, 8 * T], BF16, kind="ExternalOutput")
        dbg["v"] = nc.dram_tensor("dbg_v", [128, NT * H * 65], BF16, kind="ExternalOutput")
        dbg["qt"] = nc.dram_tensor("dbg_qt", [8, 128, T], BF16, kind="ExternalOutput")
        dbg["cb"] = nc.dram_tensor("dbg_cb", [128, 64], F32, kind="ExternalOutput")
        dbg["an"] = nc.dram_tensor("dbg_an", [128, 8 * QB], BF16, kind="ExternalOutput")
        dbg["x2"] = nc.dram_tensor("dbg_x2", [T, D], F32, kind="ExternalOutput")
        dbg["spk"] = nc.dram_tensor("dbg_spk", [128, 16 * QC], BF16, kind="ExternalOutput")
        dbg["cont"] = nc.dram_tensor("dbg_cont", [128, QC], BF16, kind="ExternalOutput")
        dbg["comb"] = nc.dram_tensor("dbg_comb", [128, 2 * NE], F32, kind="ExternalOutput")
        dbg["moe"] = nc.dram_tensor("dbg_moe", [128, QC], F32, kind="ExternalOutput")
        dbg["dec"] = nc.dram_tensor("dbg_dec", [128, 2 * D], F32, kind="ExternalOutput")
        dbg["lgt"] = nc.dram_tensor("dbg_lgt", [128, 2 * NE], F32, kind="ExternalOutput")
        dbg["xlnT"] = nc.dram_tensor("dbg_xlnT", [128, 8 * QB], BF16, kind="ExternalOutput")
        dbg["wqk"] = nc.dram_tensor("dbg_wqk", [128, 2 * 8 * 128], BF16, kind="ExternalOutput")
        dbg["stat"] = nc.dram_tensor("dbg_stat", [128, 16], F32, kind="ExternalOutput")
        dbg["xb"] = nc.dram_tensor("dbg_xb", [128, 2 * D], F32, kind="ExternalOutput")
        dbg["rh"] = nc.dram_tensor("dbg_rh", [64, QC], BF16, kind="ExternalOutput")

    pstk = ExitStack()
    with pstk:
        sbp = lambda n, s, d: pstk.enter_context(nc.sbuf_tensor(n, s, d))
        identb = sbp("identb", [128, 128], BF16)
        identf = sbp("identf", [128, 128], F32)
        onesf = sbp("onesf", [1, 128], F32)
        onesb = sbp("onesb", [1, 128], BF16)
        bo2 = sbp("bo2", [128, 2], BF16)
        selh = sbp("selh", [16, 16, 128], F32)
        brows = sbp("brows", [1, D], BF16)
        glc = sbp("glc", [128, 17], F32)
        growt = sbp("growt", [17, 128], F32)
        encb_c = sbp("encb_c", [128, 16], F32)
        s2cb_c = sbp("s2cb_c", [128, 1], F32)
        rb1_c = sbp("rb1_c", [64, 1], F32)
        eb1_c = sbp("eb1_c", [128, NE, 8], F32)
        eb2_c = sbp("eb2_c", [128, 4], F32)
        c2sb_c = sbp("c2sb_c", [128, 16], F32)
        m2r2 = sbp("m2r2", [128, NT, 2], F32)
        s12 = sbp("s12", [128, NT, 2], F32)
        cball = sbp("cball", [128, 64], F32)
        qsqm = sbp("qsqm", [2, 64], F32)
        ksqm = sbp("ksqm", [2, 8], F32)
        stat = sbp("stat", [128, 16], F32)
        cst3 = sbp("cst3", [128, 16], F32)
        rs2 = sbp("rs2", [16, QB], F32)
        rr2 = sbp("rr2", [16, QB], F32)
        lnscr = sbp("lnscr", [128, D], F32)
        lnscr2 = sbp("lnscr2", [128, D], F32)
        smrow = sbp("smrow", [64, 4], F32)
        cns = sbp("cns", [128, 4], F32)
        crow = sbp("crow", [1, 64], F32)
        crow2 = sbp("crow2", [1, 8], F32)
        abstk = ExitStack()
        kT = abstk.enter_context(nc.sbuf_tensor("kT", [128, 8, T], BF16))
        vres = abstk.enter_context(nc.sbuf_tensor("vres", [128, NT * H * 65], BF16))
        VP = NT * H * 65

        s_dma = Sem(pstk.enter_context(nc.semaphore("s_dma")))
        s_w = Sem(pstk.enter_context(nc.semaphore("s_w")))
        s_pe = Sem(pstk.enter_context(nc.semaphore("s_pe")))
        s_act = Sem(pstk.enter_context(nc.semaphore("s_act")))
        s_dve = Sem(pstk.enter_context(nc.semaphore("s_dve")))
        s_mv = Sem(pstk.enter_context(nc.semaphore("s_mv")))
        s_out = Sem(pstk.enter_context(nc.semaphore("s_out")))
        s_bar = Sem(pstk.enter_context(nc.semaphore("s_bar")))
        s_g = Sem(pstk.enter_context(nc.semaphore("s_g")))

        _cur = {"b": None, "auto": False}

        def step(**fns):
            if _cur["b"] is None:
                _cur["b"] = nc.Block()
                _cur["b"].__enter__()
            for eng in ("sync", "gpsimd", "tensor", "vector", "scalar"):
                if eng in fns:
                    getattr(_cur["b"], eng)(fns[eng])
            if _cur["auto"]:
                _cur["b"].__exit__(None, None, None)
                _cur["b"] = None

        def flush():
            if _cur["b"] is not None:
                _cur["b"].__exit__(None, None, None)
                _cur["b"] = None

        evq = {}

        iostk = ExitStack()
        iot_p = iostk.enter_context(nc.sbuf_tensor("iot_p", [128, 128], F32))
        iot_f = iostk.enter_context(nc.sbuf_tensor("iot_f", [128, 128], F32))
        p0stk = ExitStack()
        pc0 = p0stk.enter_context(nc.psum_tensor("pc0", [128, 512], F32))

        def p0_g(g):
            g.iota(iot_p[:], [[0, 128]], channel_multiplier=1,
                   allow_small_or_imprecise_dtypes=True)
            g.iota(iot_f[:], [[1, 128]], channel_multiplier=0,
                   allow_small_or_imprecise_dtypes=True)
            g.memset(onesf[:], 1.0)
            g.memset(onesb[:], 1.0)
            g.memset(bo2[0:64, 0:1], 1.0)
            g.memset(bo2[0:64, 1:2], 0.0)
            g.memset(bo2[64:128, 0:1], 0.0)
            g.memset(bo2[64:128, 1:2], 1.0)
            g.memset(ksqm[:], 0.0)
            g.memset(cns[:, 0:1], EPS)
            g.memset(cns[:, 1:2], 1e-30)
            g.memset(cns[:, 2:3], -0.5)
            g.memset(cns[:, 3:4], 0.5)
            g.iota(selh[:], [[1, 16], [0, 128]], channel_multiplier=-1,
                   allow_small_or_imprecise_dtypes=True)
            for kt_ in range(NT):
                g.memset(bass.AP(vres, kt_ * H * 65 + 64, [[VP, 128], [65, H]]), 1.0)
            def cdma(dst_ap, src_ap):
                g.dma_start(dst_ap, src_ap).then_inc(s_w.h, 16)
                s_w.n += 16

            for oc in range(16):
                cdma(wqk_pre.ap()[oc],
                     bass.AP(w_qkv, oc * 128, [[128 * 3072, 8], [3072, 128], [1, 128]]))
            for dc in range(8):
                cdma(wv_nat.ap()[dc],
                     bass.AP(w_qkv, 2048 + dc * 128 * 3072, [[3072, 128], [1, D]]))
            for dc in range(8):
                cdma(wproj_n.ap()[dc], bass.AP(w_proj, dc * 128 * D, [[D, 128], [1, D]]))
            for hc in range(16):
                cdma(enc_pre.ap()[hc],
                     bass.AP(enc_w, hc * 128, [[128 * HID, 8], [HID, 128], [1, 128]]))
            for half in range(2):
                cdma(bass.AP(s2c_pre, half * 64, [[128 * 128, 16], [128, 128], [1, 64]]),
                     bass.AP(s2c_w, 0, [[128 * MD, 16], [MD, 128], [1, 64]]))
            for pr in range(4):
                cdma(ew1_pre.ap()[pr],
                     bass.AP(e_w1, 2 * pr * MD * EH, [[128, 8], [EH, 128], [1, 128]]))
            for e in range(NE):
                cdma(ew2_pre.ap()[e],
                     bass.AP(e_w2, e * EH * MD, [[128 * MD, 8], [MD, 128], [1, 64]]))
            for half in range(2):
                cdma(bass.AP(c2s_pre, half * 64 * 128, [[128 * 128, 16], [128, 64], [1, 128]]),
                     bass.AP(c2s_w, 0, [[128, 16], [HID, 64], [1, 128]]))
            for hc in range(16):
                cdma(dec_nat.ap()[hc], bass.AP(dec_w, hc * 128 * D, [[D, 128], [1, D]]))
            cdma(bass.AP(rw_pre, 0, [[64 + NE, 64], [1, 64]]),
                 bass.AP(r_w1, 0, [[64, 64], [1, 64]]))
            cdma(bass.AP(rw_pre, 64, [[64 + NE, 64], [1, NE]]),
                 bass.AP(r_w2, 0, [[NE, 64], [1, NE]]))
            cdma(bass.AP(brow_pre, 0, [[D, 1], [1, D]]), bass.AP(b_proj, 0, [[D, 1], [1, D]]))
            cdma(bass.AP(brow_pre, D, [[D, 1], [1, D]]), bass.AP(dec_b, 0, [[D, 1], [1, D]]))
            with nc.allow_non_contiguous_dma(reason="tiny bias column loads"):
                cdma(encb_c[:], bass.AP(enc_b, 0, [[1, 128], [128, 16]]))
                for half in range(2):
                    cdma(s2cb_c[half * 64:(half + 1) * 64, :],
                         bass.AP(s2c_b, 0, [[1, 64], [64, 1]]))
                cdma(rb1_c[:], bass.AP(r_b1, 0, [[1, 64], [64, 1]]))
                cdma(eb1_c[:], bass.AP(e_b1, 0, [[1, 128], [EH, NE], [128, 8]]))
                cdma(eb2_c[:], bass.AP(e_b2, 0, [[1, 128], [128, 4]]))
                cdma(c2sb_c[:], bass.AP(c2s_b, 0, [[1, 128], [128, 16]]))
                cdma(glc[:, 0:8], bass.AP(ln_g, 0, [[1, 128], [128, 8]]))
                cdma(glc[:, 8:16], bass.AP(ln_b, 0, [[1, 128], [128, 8]]))
                cdma(glc[0:8, 16:17], bass.AP(r_b2, 0, [[1, 8], [8, 1]]))

        step(gpsimd=p0_g)
        flush()

        def p0_guard(g):
            g.drain().then_inc(s_g.h, 1)
            s_g.n += 1

        step(gpsimd=p0_guard)
        flush()
        G_CONST = s_g.n
        W_CASTS = s_w.n

        def p0_brows(sy):
            sy.wait_ge(s_w.h, W_CASTS)
            sy.wait_ge(s_g.h, G_CONST)
            sy.dma_start(brows[:], bass.AP(brow_pre, 0, [[D, 1], [1, D]]))\
              .then_inc(s_dma.h, 16)
            s_dma.n += 16

        step(sync=p0_brows)

        def p0_v(v):
            v.wait_ge(s_g.h, G_CONST)
            v.tensor_tensor(identb[:], iot_p[:], iot_f[:], ALU.is_equal)
            v.tensor_copy(identf[:], identb[:])
            v.tensor_single_scalar(selh[:], selh[:], 0.0, ALU.is_equal)
            v.drain().then_inc(s_dve.h, 1)
            s_dve.n += 1

        step(vector=p0_v)
        DVE_ID = s_dve.n

        def p0_t(t):
            t.wait_ge(s_w.h, W_CASTS)
            t.wait_ge(s_g.h, G_CONST)
            t.wait_ge(s_dve.h, DVE_ID)
            t.transpose(pc0[0:16, 0:128], glc[:, 0:16], identf[:])
            t.matmul(pc0[0:4, 480:484], onesf[0:1, 0:4], onesf[0:1, 0:4], start=True, stop=True)
            t.drain().then_inc(s_pe.h, 1)
            s_pe.n += 1

        step(tensor=p0_t)
        PE_GT = s_pe.n

        def p0_v2(v):
            v.wait_ge(s_pe.h, PE_GT)
            v.tensor_copy(growt[0:16, :], pc0[0:16, 0:128])
            v.drain().then_inc(s_dve.h, 1)
            s_dve.n += 1

        step(vector=p0_v2)

        def p0_t1b(t):
            t.wait_ge(s_dve.h, s_dve.n)
            t.transpose(pc0[0:1, 128:256], glc[:, 16:17], identf[:])
            t.matmul(pc0[0:4, 480:484], onesf[0:1, 0:4], onesf[0:1, 0:4], start=True, stop=True)
            t.drain().then_inc(s_pe.h, 1)
            s_pe.n += 1

        step(tensor=p0_t1b)
        PE_GT2 = s_pe.n

        def p0_v2b(v):
            v.wait_ge(s_pe.h, PE_GT2)
            v.tensor_copy(crow2[0:1, 0:8], pc0[0:1, 128:136])
            v.drain().then_inc(s_dve.h, 1)
            s_dve.n += 1

        step(vector=p0_v2b)
        DVE_GR = s_dve.n
        DVE_P0 = s_dve.n

        DVE_P0 = s_dve.n
        flush()
        p0stk.close()
        iostk.close()

        if _STOP == "0":
            def fin0(g):
                g.wait_ge(s_w.h, s_w.n)
                g.wait_ge(s_g.h, s_g.n)
            step(gpsimd=fin0)
            flush()
            abstk.close()
            return nc

        astk = ExitStack()
        sba = lambda n, s, d: astk.enter_context(nc.sbuf_tensor(n, s, d))
        xb = sba("xb", [128, 2, D], F32)
        xlnb = sba("xlnb", [128, D], BF16)
        xlnT = sba("xlnT", [128, 8, QB], BF16)
        wqk_sb = sba("wqk_sb", [128, 2, 8, 128], BF16)
        wv_sb = sba("wv_sb", [128, 8, D], BF16)
        qtb = sba("qtb", [128, 8, QB], BF16)
        sqscr = sba("sqscr", [128, QB], BF16)
        pqk0 = astk.enter_context(nc.psum_tensor("pqk0", [128, 512], F32))
        pqk1 = astk.enter_context(nc.psum_tensor("pqk1", [128, 512], F32))
        pv2a = astk.enter_context(nc.psum_tensor("pv2a", [128, 1024], F32))
        pv2b = astk.enter_context(nc.psum_tensor("pv2b", [128, 1024], F32))
        pnrm = astk.enter_context(nc.psum_tensor("pnrm", [2, 512], F32))
        ptp = astk.enter_context(nc.psum_tensor("ptp", [128, 640], BF16))

        _cur["auto"] = False

        def pa_wv(sy):
            sy.wait_ge(s_w.h, W_CASTS)
            sy.wait_ge(s_g.h, G_CONST)
            sy.dma_start(wv_sb[:], bass.AP(wv_nat, 0, [[D, 128], [128 * D, 8], [1, D]]))\
              .then_inc(s_dma.h, 16)
            s_dma.n += 16

        step(sync=pa_wv)
        DMA_WV = s_dma.n

        x_ld = {}
        for b in range(NB):
            ev_marks = []
            pe_tp = [0]
            ap_marks = {}
            for tt in range(4):
                slx = tt % 2

                def pa_ldx(sy, b=b, tt=tt, slx=slx):
                    prev = ap_marks.get(tt - 2, evq.get(("lnap", b - 1, tt + 2)))
                    if prev:
                        sy.wait_ge(s_dve.h, prev)
                    elif b == 0 and tt == 0:
                        sy.wait_ge(s_dve.h, DVE_P0)
                    sy.dma_start(xb[:, slx, :],
                                 bass.AP(xd, (b * 4 + tt) * 128 * D, [[D, 128], [1, D]]))\
                      .then_inc(s_dma.h, 16)
                    s_dma.n += 16

                step(sync=pa_ldx)
                x_ld[(b, tt)] = s_dma.n

                def pa_ln_v(v, b=b, tt=tt, slx=slx):
                    v.wait_ge(s_dma.h, x_ld[(b, tt)])
                    prevact = evq.get(("rstd", b - 1, tt))
                    if prevact:
                        v.wait_ge(s_act.h, prevact)
                    c0 = 4 * tt
                    v.reduce_sum(stat[:, c0:c0 + 1], xb[:, slx, :], axis=AX.X)
                    v.tensor_mul(lnscr[:], xb[:, slx, :], xb[:, slx, :])
                    v.drain()
                    v.reduce_sum(stat[:, c0 + 1:c0 + 2], lnscr[:], axis=AX.X)
                    v.drain()
                    v.tensor_scalar_mul(stat[:, c0:c0 + 1], stat[:, c0:c0 + 1], 1.0 / D)
                    v.tensor_scalar_mul(stat[:, c0 + 1:c0 + 2], stat[:, c0 + 1:c0 + 2], 1.0 / D)
                    v.drain()
                    v.tensor_mul(stat[:, c0 + 2:c0 + 3], stat[:, c0:c0 + 1], stat[:, c0:c0 + 1])
                    v.drain()
                    v.tensor_sub(stat[:, c0 + 2:c0 + 3], stat[:, c0 + 1:c0 + 2],
                                 stat[:, c0 + 2:c0 + 3])
                    v.drain().then_inc(s_dve.h, 1)
                    s_dve.n += 1

                step(vector=pa_ln_v)
                dve_var = s_dve.n

                def pa_ln_s(s, b=b, tt=tt, dve_var=dve_var):
                    s.wait_ge(s_dve.h, dve_var)
                    c0 = 4 * tt
                    s.activation(stat[:, c0 + 3:c0 + 4], stat[:, c0 + 2:c0 + 3], ACT.Ln,
                                 bias=cns[:, 0:1], scale=1.0)
                    s.drain()
                    s.activation(stat[:, c0 + 2:c0 + 3], stat[:, c0 + 3:c0 + 4], ACT.Exp,
                                 scale=cns[:, 2:3])
                    s.drain().then_inc(s_act.h, 1)
                    s_act.n += 1

                step(scalar=pa_ln_s)
                flush()
                act_rstd = s_act.n
                evq[("rstd", b, tt)] = act_rstd

                def pa_lnap(v, b=b, tt=tt, slx=slx, act_rstd=act_rstd):
                    v.wait_ge(s_act.h, act_rstd)
                    if pe_tp[0]:
                        v.wait_ge(s_pe.h, pe_tp[0])
                    elif b > 0:
                        v.wait_ge(s_pe.h, evq[("tppe", b - 1)])
                    v.tensor_scalar(xlnb[:], xb[:, slx, :],
                                    stat[:, 4 * tt:4 * tt + 1], stat[:, 4 * tt + 2:4 * tt + 3],
                                    ALU.subtract, ALU.mult)
                    v.drain().then_inc(s_dve.h, 1)
                    s_dve.n += 1
                    ap_marks[tt] = s_dve.n
                    evq[("lnap", b, tt)] = s_dve.n

                step(vector=pa_lnap)
                dve_ap = s_dve.n

                for grp in range(2):
                    def pa_tg_t(t, b=b, tt=tt, grp=grp, dve_ap=dve_ap):
                        t.wait_ge(s_dve.h, dve_ap)
                        if tt == 0 and grp == 0 and b > 0:
                            t.wait_ge(s_dve.h, evq[("vfree", b - 1)])
                        if ev_marks:
                            t.wait_ge(s_dve.h, ev_marks[-1])
                        for i in range(4):
                            dc = grp * 4 + i
                            t.transpose(ptp[:, i * 128:i * 128 + 128],
                                        xlnb[:, dc * 128:dc * 128 + 128], identb[:])
                        t.matmul(ptp[:].bitcast(F32)[0:4, 256:260], onesb[0:1, 0:4],
                                 onesb[0:1, 0:4],
                                 start=True, stop=True)
                        t.drain().then_inc(s_pe.h, 1)
                        s_pe.n += 1
                        pe_tp[0] = s_pe.n

                    step(tensor=pa_tg_t)
                    pem = s_pe.n

                    def pa_tg_v(v, b=b, tt=tt, grp=grp, pem=pem):
                        v.wait_ge(s_pe.h, pem)
                        for i in range(4):
                            dc = grp * 4 + i
                            v.tensor_copy(xlnT[:, dc, tt * 128:tt * 128 + 128],
                                          ptp[:, i * 128:i * 128 + 128])
                        v.drain().then_inc(s_dve.h, 1)
                        s_dve.n += 1
                        ev_marks.append(s_dve.n)

                    step(vector=pa_tg_v)
            evq[("tppe", b)] = s_pe.n
            evq[("tpdone", b)] = s_dve.n

            qk_ev = {}
            for oc in range(16):
                slot = oc % 2

                def pa_qk_ld(sy, b=b, oc=oc, slot=slot):
                    if b == 0 and oc <= 1:
                        sy.wait_ge(s_w.h, W_CASTS)
                        sy.wait_ge(s_g.h, G_CONST)
                    prev = qk_ev.get(oc - 2, evq.get(("qkev", b - 1, 14 + slot)))
                    if prev:
                        sy.wait_ge(s_dve.h, prev)
                    sy.dma_start(wqk_sb[:, slot, :, :],
                                 bass.AP(wqk_pre, oc * 8 * 128 * 128,
                                         [[128, 128], [128 * 128, 8], [1, 128]]))\
                      .then_inc(s_dma.h, 16)
                    s_dma.n += 16

                step(sync=pa_qk_ld)
                qk_ld = s_dma.n

                def pa_qk_t(t, b=b, oc=oc, slot=slot, qk_ld=qk_ld):
                    t.wait_ge(s_dma.h, qk_ld)
                    t.wait_ge(s_dve.h, evq[("tpdone", b)])
                    prev = qk_ev.get(oc - 2)
                    if prev:
                        t.wait_ge(s_dve.h, prev)
                    ppsum = pqk0 if slot == 0 else pqk1
                    for dc in range(8):
                        t.matmul(ppsum[:], wqk_sb[:, slot, dc, :], xlnT[:, dc, :],
                                 start=(dc == 0), stop=(dc == 7))
                    t.drain().then_inc(s_pe.h, 1)
                    s_pe.n += 1

                step(tensor=pa_qk_t)
                pe_qk = s_pe.n

                def pa_qk_v(v, b=b, oc=oc, slot=slot, pe_qk=pe_qk):
                    v.wait_ge(s_pe.h, pe_qk)
                    ppsum = pqk0 if slot == 0 else pqk1
                    if oc < 8:
                        if b > 0 and oc == 0:
                            v.wait_ge(s_mv.h, evq[("qst", b - 1)])
                        v.tensor_scalar_mul(qtb[:, oc, :], ppsum[:], 0.125)
                        v.drain()
                        v.tensor_mul(sqscr[:], qtb[:, oc, :], qtb[:, oc, :])
                    else:
                        c = oc - 8
                        v.tensor_copy(kT[:, c, b * QB:(b + 1) * QB], ppsum[:])
                        v.drain()
                        v.tensor_mul(sqscr[:], kT[:, c, b * QB:(b + 1) * QB],
                                     kT[:, c, b * QB:(b + 1) * QB])
                    v.drain().then_inc(s_dve.h, 1)
                    s_dve.n += 1

                step(vector=pa_qk_v)
                dve_sq = s_dve.n

                def pa_nrm_t(t, b=b, oc=oc, dve_sq=dve_sq):
                    t.wait_ge(s_dve.h, dve_sq)
                    t.matmul(pnrm[:], bo2[:], sqscr[:], start=True, stop=True)
                    t.drain().then_inc(s_pe.h, 1)
                    s_pe.n += 1

                step(tensor=pa_nrm_t)
                pe_nrm = s_pe.n

                def pa_nrm_v(v, b=b, oc=oc, pe_nrm=pe_nrm):
                    v.wait_ge(s_pe.h, pe_nrm)
                    if oc < 8:
                        v.reduce_max(qsqm[0:2, oc * 8 + b:oc * 8 + b + 1], pnrm[:], axis=AX.X)
                    else:
                        c = oc - 8
                        v.reduce_max(cst3[0:2, 15:16], pnrm[:], axis=AX.X)
                        v.drain()
                        v.tensor_tensor(ksqm[0:2, c:c + 1], ksqm[0:2, c:c + 1],
                                        cst3[0:2, 15:16], ALU.max)
                    v.drain().then_inc(s_dve.h, 1)
                    s_dve.n += 1

                step(vector=pa_nrm_v)
                qk_ev[oc] = s_dve.n
                evq[("qkev", b, oc)] = s_dve.n

            def pa_qst(g, b=b, after=qk_ev[7]):
                g.wait_ge(s_dve.h, qk_ev[7])
                if b > 0:
                    pass
                g.dma_start(bass.AP(qts, b * QB, [[T, 128], [128 * T, 8], [1, QB]]), qtb[:])\
                  .then_inc(s_mv.h, 16)
                s_mv.n += 16

            step(gpsimd=pa_qst)
            evq[("qst", b)] = s_mv.n

            v_ev = {}
            for tt in range(4):
                ppsum = pv2a if tt % 2 == 0 else pv2b

                def pa_v_t(t, b=b, tt=tt, ppsum=ppsum):
                    if tt == 0:
                        t.wait_ge(s_dma.h, DMA_WV)
                        t.wait_ge(s_dve.h, evq[("tpdone", b)])
                    prev = v_ev.get(tt - 2, evq.get(("vev", b - 1, tt + 2)))
                    if prev:
                        t.wait_ge(s_dve.h, prev)
                    for dc in range(8):
                        lhs = xlnT[:, dc, tt * 128:tt * 128 + 128]
                        t.matmul(ppsum[:, 0:512], lhs, wv_sb[:, dc, 0:512],
                                 start=(dc == 0), stop=(dc == 7))
                        t.matmul(ppsum[:, 512:1024], lhs, wv_sb[:, dc, 512:1024],
                                 start=(dc == 0), stop=(dc == 7))
                    t.drain().then_inc(s_pe.h, 1)
                    s_pe.n += 1

                step(tensor=pa_v_t)
                pe_v = s_pe.n

                def pa_v_v(v, b=b, tt=tt, ppsum=ppsum, pe_v=pe_v):
                    v.wait_ge(s_pe.h, pe_v)
                    kt_ = b * 4 + tt
                    v.tensor_copy(bass.AP(vres, kt_ * H * 65, [[VP, 128], [65, H], [1, 64]]),
                                  ppsum[:])
                    v.drain().then_inc(s_dve.h, 1)
                    s_dve.n += 1

                step(vector=pa_v_v)
                v_ev[tt] = s_dve.n
                evq[("vev", b, tt)] = s_dve.n
            evq[("vfree", b)] = s_dve.n
            evq[("tpvA", b)] = ev_marks[-1]

            def pa_qguard(v, b=b):
                pass

            def pa_qg2(sy, b=b):
                sy.wait_ge(s_mv.h, evq[("qst", b)])
                sy.nop().then_inc(s_dma.h, 0) if False else None

            evq[("qstore", b)] = evq[("qst", b)]

        if _DEBUG:
            def dbg_pa(g):
                g.wait_ge(s_dve.h, s_dve.n)
                g.dma_start(dbg["xlnT"].ap(), xlnT[:]).then_inc(s_mv.h, 16)
                g.dma_start(dbg["wqk"].ap(), wqk_sb[:]).then_inc(s_mv.h, 16)
                g.dma_start(dbg["stat"].ap(), stat[:]).then_inc(s_mv.h, 16)
                g.dma_start(dbg["xb"].ap(), xb[:]).then_inc(s_mv.h, 16)
                s_mv.n += 64

            step(gpsimd=dbg_pa)

        def pcb_v1(v):
            v.wait_ge(s_dve.h, s_dve.n)
            for c in range(8):
                v.tensor_scalar(qsqm[0:2, c * 8:c * 8 + 8], qsqm[0:2, c * 8:c * 8 + 8],
                                ksqm[0:2, c:c + 1], None, ALU.mult)
            v.drain().then_inc(s_dve.h, 1)
            s_dve.n += 1

        step(vector=pcb_v1)
        dve_prod = s_dve.n

        def pcb_t1(t):
            t.wait_ge(s_dve.h, dve_prod)
            t.transpose(pqk0[0:64, 0:2], qsqm[0:2, 0:64], identf[0:2, 0:2])
            t.matmul(pqk0[0:4, 480:484], onesf[0:1, 0:4], onesf[0:1, 0:4], start=True, stop=True)
            t.drain().then_inc(s_pe.h, 1)
            s_pe.n += 1

        step(tensor=pcb_t1)
        pe_cp = s_pe.n

        def pcb_v2(v, pe_cp=pe_cp):
            v.wait_ge(s_pe.h, pe_cp)
            v.tensor_copy(smrow[:, 0:2], pqk0[0:64, 0:2])
            v.drain()
            v.tensor_tensor(smrow[:, 2:3], smrow[:, 0:1], smrow[:, 1:2], ALU.max)
            v.drain().then_inc(s_dve.h, 1)
            s_dve.n += 1

        step(vector=pcb_v2)
        dve_pm = s_dve.n

        def pcb_s1(s, dve_pm=dve_pm):
            s.wait_ge(s_dve.h, dve_pm)
            s.activation(smrow[:, 3:4], smrow[:, 2:3], ACT.Ln, bias=cns[0:64, 1:2], scale=1.0)
            s.drain()
            s.activation(smrow[:, 2:3], smrow[:, 3:4], ACT.Exp, scale=cns[0:64, 3:4])
            s.drain().then_inc(s_act.h, 1)
            s_act.n += 1

        step(scalar=pcb_s1)
        act_c = s_act.n

        def pcb_v3(v, act_c=act_c):
            v.wait_ge(s_act.h, act_c)
            v.tensor_scalar_mul(smrow[:, 1:2], smrow[:, 2:3], -1.0)
            v.drain().then_inc(s_dve.h, 1)
            s_dve.n += 1

        step(vector=pcb_v3)
        dve_cn = s_dve.n

        def pcb_t2(t, dve_cn=dve_cn):
            t.wait_ge(s_dve.h, dve_cn)
            t.transpose(pqk1[0:1, 0:64], smrow[:, 1:2], identf[0:64, 0:64])
            t.matmul(pqk1[0:4, 480:484], onesf[0:1, 0:4], onesf[0:1, 0:4], start=True, stop=True)
            t.drain().then_inc(s_pe.h, 1)
            s_pe.n += 1

        step(tensor=pcb_t2)
        pe_cr = s_pe.n

        def pcb_v4(v, pe_cr=pe_cr):
            v.wait_ge(s_pe.h, pe_cr)
            v.tensor_copy(crow[:], pqk1[0:1, 0:64])
            v.drain().then_inc(s_dve.h, 1)
            s_dve.n += 1

        step(vector=pcb_v4)
        dve_cr = s_dve.n

        def pcb_t3(t, dve_cr=dve_cr):
            t.wait_ge(s_dve.h, dve_cr)
            t.matmul(pqk0[:, 0:64], onesf[:], crow[:], start=True, stop=True)
            t.drain().then_inc(s_pe.h, 1)
            s_pe.n += 1

        step(tensor=pcb_t3)
        pe_cb = s_pe.n

        def pcb_v5(v, pe_cb=pe_cb):
            v.wait_ge(s_pe.h, pe_cb)
            v.tensor_copy(cball[:], pqk0[:, 0:64])
            v.drain().then_inc(s_dve.h, 1)
            s_dve.n += 1

        step(vector=pcb_v5)
        flush()
        _cur["auto"] = False
        DVE_CB = s_dve.n

        if _DEBUG:
            def dbg_a(g):
                g.wait_ge(s_dve.h, DVE_CB)
                g.dma_start(dbg["kt"].ap(), bass.AP(kT, 0, [[8 * T, 128], [1, 8 * T]]))\
                 .then_inc(s_mv.h, 16)
                g.dma_start(dbg["v"].ap(), vres[:]).then_inc(s_mv.h, 16)
                g.dma_start(dbg["qt"].ap(), qts.ap()).then_inc(s_mv.h, 16)
                g.dma_start(dbg["cb"].ap(), cball[:]).then_inc(s_mv.h, 16)
                s_mv.n += 64

            step(gpsimd=dbg_a)

        tgt = s_bar.n + 5

        def mkbar(e):
            e.nop().then_inc(s_bar.h, 1)
            e.wait_ge(s_bar.h, tgt)

        step(sync=mkbar, gpsimd=mkbar, tensor=mkbar, vector=mkbar, scalar=mkbar)
        flush()
        s_bar.n += 5
        astk.close()

        if _STOP == "A":
            def finA(g):
                g.wait_ge(s_mv.h, s_mv.n)
            step(gpsimd=finA)
            flush()
            astk.close()
            abstk.close()
            return nc

        bstk = ExitStack()
        sbb = lambda n, s, d: bstk.enter_context(nc.sbuf_tensor(n, s, d))
        qt_buf = sbb("qt_buf", [128, 8, QB], BF16)
        pbuf = sbb("pbuf", [128, 3, 1024], BF16)
        attnn = sbb("attnn", [128, 8, QB], BF16)
        stgB = sbb("stgB", [64, QB], BF16)
        rsl = sbb("rsl", [128, 2, 512], F32)
        xb1 = sbb("xb1", [128, D], F32)
        x2b = sbb("x2b", [128, D], F32)
        wproj_sb = sbb("wproj_sb", [128, 8, D], BF16)
        psS0 = bstk.enter_context(nc.psum_tensor("psS0", [128, 1024], F32))
        psS1 = bstk.enter_context(nc.psum_tensor("psS1", [128, 1024], F32))
        pvA = bstk.enter_context(nc.psum_tensor("pvA", [128, 512], F32))
        pvB = bstk.enter_context(nc.psum_tensor("pvB", [128, 512], F32))
        pbc = bstk.enter_context(nc.psum_tensor("pbc", [128, 1024], F32))

        def pb_wp(sy):
            sy.wait_ge(s_w.h, W_CASTS)
            sy.dma_start(wproj_sb[:], bass.AP(wproj_n, 0, [[D, 128], [128 * D, 8], [1, D]]))\
              .then_inc(s_dma.h, 16)
            s_dma.n += 16

        step(sync=pb_wp)
        DMA_WP = s_dma.n

        expm = {}
        pvm = {}
        spm = {}
        pvdr = {}

        def emit_spair(t, b, hp, kc):
            sp = (psS0, psS1, pbc)[kc % 3]
            t.matmul(sp[:, 0:512], kT[0:64, hp, kc * 128:kc * 128 + 128],
                     qt_buf[0:64, hp, :], start=True, stop=True)
            t.matmul(sp[:, 512:1024],
                     kT[64:128, hp, kc * 128:kc * 128 + 128],
                     qt_buf[64:128, hp, :], start=True,
                     stop=True).then_inc(s_pe.h, 1)
            s_pe.n += 1
            spm[(b, hp, kc)] = s_pe.n

        for b in range(NB if _STOP not in ("Battn", "Bnrm", "B1") else 1):
            def pb_qld(sy, b=b):
                if b > 0:
                    sy.wait_ge(s_pe.h, evq[("battn", b - 1)])
                sy.dma_start(qt_buf[:], bass.AP(qts, b * QB, [[T, 128], [128 * T, 8], [1, QB]]))\
                  .then_inc(s_dma.h, 16)
                s_dma.n += 16

            step(sync=pb_qld)
            qld = s_dma.n

            base_act_b = s_act.n
            for hp in range(_HPN):
                for kc in range(NT):
                    expm[(b, hp, kc)] = base_act_b + hp * NT + kc + 1

            for hp in range(_HPN):
                def t_attn(t, b=b, hp=hp, qld=qld):
                    hoisted = hp > 0
                    for kc in range(NT + 1):
                        if kc < NT and not (hoisted and kc < 3):
                            if kc == 0 and hp == 0:
                                t.wait_ge(s_dma.h, qld)
                                if b == 0:
                                    t.wait_ge(s_dve.h, DVE_CB)
                                if b > 0:
                                    t.wait_ge(s_dve.h, evq[("x2ev", b - 1, 3)])
                            prev_exp = expm.get((b, hp, kc - 3),
                                                expm.get((b, hp - 1, NT - 3 + kc),
                                                         expm.get((b - 1, 7, NT - 3 + kc))))
                            if prev_exp:
                                t.wait_ge(s_act.h, prev_exp)
                            emit_spair(t, b, hp, kc)
                        if kc >= 1:
                            kcp = kc - 1
                            hoist_now = (kcp == NT - 1 and hp + 1 < _HPN)
                            if hoist_now:
                                for j in (0, 1):
                                    t.wait_ge(s_act.h, expm[(b, hp, NT - 2 + j)])
                                    emit_spair(t, b, hp + 1, j)
                                emit_spair(t, b, hp + 1, 2)
                            else:
                                t.wait_ge(s_act.h, expm[(b, hp, kcp)])
                            if kcp == 0:
                                prev_ev = evq.get(("pvev", b, hp - 1),
                                                  evq.get(("pvev", b - 1, 7)))
                                if prev_ev:
                                    t.wait_ge(s_dve.h, prev_ev)
                                if hp == 0 and b > 0:
                                    t.wait_ge(s_mv.h, evq[("mvBall", b - 1)])
                                    t.wait_ge(s_pe.h, evq[("pjdone", b - 1)])
                            h0, h1 = 2 * hp, 2 * hp + 1
                            t.matmul(pvA[0:65, :],
                                     bass.AP(vres, kcp * H * 65 + h0 * 65, [[VP, 128], [1, 65]]),
                                     pbuf[:, kcp % 3, 0:512],
                                     start=(kcp == 0), stop=(kcp == NT - 1))
                            mpv = t.matmul(pvB[0:65, :],
                                           bass.AP(vres, kcp * H * 65 + h1 * 65,
                                                   [[VP, 128], [1, 65]]),
                                           pbuf[:, kcp % 3, 512:1024],
                                           start=(kcp == 0), stop=(kcp == NT - 1))
                            mpv.then_inc(s_pe.h, 1)
                            s_pe.n += 1
                            pvm[(b, hp, kcp)] = s_pe.n
                            if kcp == NT - 1:
                                t.drain().then_inc(s_pe.h, 1)
                                s_pe.n += 1
                                pvdr[(b, hp)] = s_pe.n

                step(tensor=t_attn)
                pvdone = pvdr[(b, hp)]
                slr = hp % 2

                def pb_rs_v(v, b=b, hp=hp, pvdone=pvdone, slr=slr):
                    v.wait_ge(s_pe.h, pvdone)
                    prev_rsd = evq.get(("rsd", b, hp - 1), evq.get(("rsd", b - 1, 7)))
                    if prev_rsd:
                        v.wait_ge(s_dma.h, prev_rsd)
                    v.tensor_copy(rsl[64:65, 0, :], pvA[64:65, :])
                    v.tensor_copy(rsl[64:65, 1, :], pvB[64:65, :])
                    prev_mv = evq.get(("mvB", b, hp - 1), evq.get(("mvB", b - 1, 7)))
                    if prev_mv:
                        v.wait_ge(s_mv.h, prev_mv)
                    v.tensor_copy(attnn[0:64, hp, :], pvA[0:64, :])
                    v.tensor_copy(stgB[0:64, :], pvB[0:64, :]).then_inc(s_dve.h, 1)
                    s_dve.n += 1

                step(vector=pb_rs_v)
                evq[("pvev", b, hp)] = s_dve.n
                dve_rs = s_dve.n

                def pb_rs_d(sy, b=b, hp=hp, slr=slr, dve_rs=dve_rs):
                    sy.wait_ge(s_dve.h, dve_rs)
                    sy.dma_start(rs2[2 * hp:2 * hp + 2, :], rsl[64:65, :, :])\
                      .then_inc(s_dma.h, 16)
                    s_dma.n += 16
                    evq[("rsd", b, hp)] = s_dma.n

                if not _NORSD:
                    step(sync=pb_rs_d)
                else:
                    evq[("rsd", b, hp)] = 0

                def pb_mv(g, b=b, hp=hp):
                    g.wait_ge(s_dve.h, evq[("pvev", b, hp)])
                    g.dma_start(attnn[64:128, hp, :], stgB[:]).then_inc(s_mv.h, 16)
                    s_mv.n += 16
                    evq[("mvB", b, hp)] = s_mv.n

                if not _NOMV:
                    step(gpsimd=pb_mv)
                else:
                    evq[("mvB", b, hp)] = 0

            def s_attn_b(s, b=b):
                for hp in range(_HPN):
                    for kc in range(NT):
                        w = spm[(b, hp, kc)]
                        if kc >= 3:
                            pvreq = pvm.get((b, hp, kc - 3), 0)
                        else:
                            kstar = (30, 31, 29)[kc]
                            pvreq = pvm.get((b, hp - 1, kstar),
                                            pvm.get((b - 1, 7, kstar), 0)) or 0
                        s.wait_ge(s_pe.h, max(w, pvreq))
                        sp = (psS0, psS1, pbc)[kc % 3]
                        s.activation(pbuf[:, kc % 3, :], sp[:], ACT.Exp,
                                     bias=cball[:, hp * 8 + b:hp * 8 + b + 1],
                                     scale=1.0).then_inc(s_act.h, 1)
                        s_act.n += 1
                        assert expm[(b, hp, kc)] == s_act.n

            step(scalar=s_attn_b)

            evq[("battn", b)] = spm[(b, _HPN - 1, NT - 1)]
            evq[("mvBall", b)] = evq.get(("mvB", b, _HPN - 1), 0)
            if _STOP == "Battn":
                break

            def pb_rec(v, b=b):
                v.wait_ge(s_dma.h, evq[("rsd", b, 7)])
                v.reciprocal(rr2[:], rs2[:])
                v.drain().then_inc(s_dve.h, 1)
                s_dve.n += 1

            step(vector=pb_rec)
            dve_rec = s_dve.n

            nrm_marks = []
            for h in range(16):
                slb = h % 2

                def pb_nb_t(t, b=b, h=h, slb=slb, dve_rec=dve_rec):
                    if h == 0:
                        t.wait_ge(s_dve.h, dve_rec)
                        t.wait_ge(s_mv.h, evq[("mvBall", b)])
                    if h >= 2:
                        t.wait_ge(s_dve.h, nrm_marks[h - 2])
                    t.matmul(pbc[:, slb * 512:slb * 512 + 512], selh[:, h, :], rr2[:],
                             start=True, stop=True).then_inc(s_pe.h, 1)
                    s_pe.n += 1

                step(tensor=pb_nb_t)
                pe_nb = s_pe.n

                def pb_nb_v(v, b=b, h=h, slb=slb, pe_nb=pe_nb):
                    v.wait_ge(s_pe.h, pe_nb)
                    hp, par = divmod(h, 2)
                    rsel = slice(0, 64) if par == 0 else slice(64, 128)
                    v.tensor_tensor(attnn[rsel, hp, :], attnn[rsel, hp, :],
                                    pbc[rsel, slb * 512:slb * 512 + 512], ALU.mult)
                    v.drain().then_inc(s_dve.h, 1)
                    s_dve.n += 1
                    nrm_marks.append(s_dve.n)

                step(vector=pb_nb_v)
            evq[("anev", b)] = s_dve.n
            if _STOP == "Bnrm":
                break

            if _DEBUG and b == 0:
                def dbg_an(g, upto=s_dve.n):
                    g.wait_ge(s_dve.h, upto)
                    g.dma_start(dbg["an"].ap(), attnn[:]).then_inc(s_mv.h, 16)
                    s_mv.n += 16

                step(gpsimd=dbg_an)

            for tt in range(4):
                slx = tt % 2

                def pb_xld(sy, b=b, tt=tt, slx=slx):
                    prev = evq.get(("x2ev", b, tt - 1), evq.get(("x2ev", b - 1, 3)))
                    if prev:
                        sy.wait_ge(s_dve.h, prev)
                    sy.dma_start(xb1[:],
                                 bass.AP(xd, (b * 4 + tt) * 128 * D, [[D, 128], [1, D]]))\
                      .then_inc(s_dma.h, 16)
                    s_dma.n += 16

                step(sync=pb_xld)
                x1ld = s_dma.n

                def pb_pj_t(t, b=b, tt=tt):
                    if tt == 0:
                        t.wait_ge(s_dve.h, evq[("anev", b)])
                        t.wait_ge(s_dma.h, DMA_WP)
                    prev = evq.get(("x2ev", b, tt - 1), evq.get(("x2ev", b - 1, 3)))
                    if prev:
                        t.wait_ge(s_dve.h, prev)
                    for c in range(8):
                        lhs = attnn[:, c, tt * 128:tt * 128 + 128]
                        t.matmul(pbc[:, 0:512], lhs, wproj_sb[:, c, 0:512],
                                 start=(c == 0), stop=False)
                        t.matmul(pbc[:, 512:1024], lhs, wproj_sb[:, c, 512:1024],
                                 start=(c == 0), stop=False)
                    t.matmul(pbc[:, 0:512], onesb[:], brows[0:1, 0:512], start=False, stop=True)
                    t.matmul(pbc[:, 512:1024], onesb[:], brows[0:1, 512:1024],
                             start=False, stop=True)
                    t.drain().then_inc(s_pe.h, 1)
                    s_pe.n += 1

                step(tensor=pb_pj_t)
                pe_pj = s_pe.n

                def pb_pj_v(v, b=b, tt=tt, slx=slx, pe_pj=pe_pj, x1ld=x1ld):
                    v.wait_ge(s_pe.h, pe_pj)
                    v.wait_ge(s_dma.h, x1ld)
                    prev_st = evq.get(("x2st", b, tt - 1), evq.get(("x2st", b - 1, 3)))
                    if prev_st:
                        v.wait_ge(s_mv.h, prev_st)
                    v.tensor_tensor(x2b[:], pbc[:], xb1[:], ALU.add)
                    v.drain()
                    nt_ = b * 4 + tt
                    v.reduce_sum(s12[:, nt_, 0:1], x2b[:], axis=AX.X)
                    v.tensor_mul(lnscr[:], x2b[:], x2b[:])
                    v.drain()
                    v.reduce_sum(s12[:, nt_, 1:2], lnscr[:], axis=AX.X)
                    v.drain().then_inc(s_dve.h, 1)
                    s_dve.n += 1

                step(vector=pb_pj_v)
                evq[("x2ev", b, tt)] = s_dve.n
                dve_x2 = s_dve.n

                def pb_x2st(g, b=b, tt=tt, slx=slx, dve_x2=dve_x2):
                    g.wait_ge(s_dve.h, dve_x2)
                    g.dma_start(bass.AP(x2s, (b * 4 + tt) * 128 * D, [[D, 128], [1, D]]),
                                x2b[:]).then_inc(s_mv.h, 16)
                    s_mv.n += 16
                    evq[("x2st", b, tt)] = s_mv.n

                step(gpsimd=pb_x2st)
            evq[("pjdone", b)] = s_pe.n
            if _STOP == "B1":
                break

        if _STOP in ("Battn", "Bnrm", "B1"):
            def finBx(g):
                g.wait_ge(s_mv.h, s_mv.n)
                g.wait_ge(s_dve.h, s_dve.n)
                g.wait_ge(s_act.h, s_act.n)
            step(gpsimd=finBx)
            flush()
            bstk.close()
            abstk.close()
            return nc

        def pb_fin_v(v):
            v.wait_ge(s_dve.h, s_dve.n)
            v.tensor_scalar_mul(m2r2[:, :, 0], s12[:, :, 0], 1.0 / D)
            v.tensor_scalar_mul(lnscr[:, 0:NT], s12[:, :, 1], 1.0 / D)
            v.drain()
            v.tensor_mul(lnscr2[:, 0:NT], m2r2[:, :, 0], m2r2[:, :, 0])
            v.drain()
            v.tensor_sub(lnscr[:, 0:NT], lnscr[:, 0:NT], lnscr2[:, 0:NT])
            v.drain().then_inc(s_dve.h, 1)
            s_dve.n += 1

        step(vector=pb_fin_v)
        dve_var2 = s_dve.n

        def pb_fin_s(s, dve_var2=dve_var2):
            s.wait_ge(s_dve.h, dve_var2)
            s.activation(lnscr2[:, 0:NT], lnscr[:, 0:NT], ACT.Ln, bias=cns[:, 0:1], scale=1.0)
            s.drain()
            s.activation(lnscr[:, 0:NT], lnscr2[:, 0:NT], ACT.Exp, scale=cns[:, 2:3])
            s.drain().then_inc(s_act.h, 1)
            s_act.n += 1

        step(scalar=pb_fin_s)
        act_r2 = s_act.n

        def pb_fin_v2(v, act_r2=act_r2):
            v.wait_ge(s_act.h, act_r2)
            v.tensor_copy(m2r2[:, :, 1], lnscr[:, 0:NT])
            v.drain().then_inc(s_dve.h, 1)
            s_dve.n += 1

        step(vector=pb_fin_v2)
        flush()

        if _DEBUG:
            def dbg_b(g):
                g.wait_ge(s_mv.h, s_mv.n)
                g.dma_start(dbg["x2"].ap(), x2s.ap()).then_inc(s_mv.h, 16)
                s_mv.n += 16

            step(gpsimd=dbg_b)

        tgt2 = s_bar.n + 5

        def mkbar2(e):
            e.nop().then_inc(s_bar.h, 1)
            e.wait_ge(s_bar.h, tgt2)

        step(sync=mkbar2, gpsimd=mkbar2, tensor=mkbar2, vector=mkbar2, scalar=mkbar2)
        flush()
        s_bar.n += 5
        bstk.close()
        abstk.close()

        if _STOP == "B":
            def finB(g):
                g.wait_ge(s_mv.h, s_mv.n)
            step(gpsimd=finB)
            return nc

        cstk = ExitStack()
        sbc = lambda n, s, d: cstk.enter_context(nc.sbuf_tensor(n, s, d))
        enc_sb = sbc("enc_sb", [128, 16, 8, 128], BF16)
        dec_sb = sbc("dec_sb", [128, 16, D], BF16)
        s2c_sb = sbc("s2c_sb", [128, 16, 128], BF16)
        c2s_sb = sbc("c2s_sb", [128, 16, 128], BF16)
        ew1_sb = sbc("ew1_sb", [128, 4, 8, 128], BF16)
        ew2_sb = sbc("ew2_sb", [128, NE, 8, 64], BF16)
        rw_sb = sbc("rw_sb", [64, 64 + NE], BF16)
        x2c = sbc("x2c", [128, 2, 2, D], F32)
        xlnc = sbc("xlnc", [128, D], BF16)
        x2lT = sbc("x2lT", [128, 8, QC], BF16)
        spk = sbc("spk", [128, 16, QC], BF16)
        cont = sbc("cont", [128, QC], BF16)
        rh = sbc("rh", [64, QC], BF16)
        lgt = sbc("lgt", [128, 2, NE], F32)
        msk = sbc("msk", [128, NE], F32)
        cmb = sbc("cmb", [128, 2, NE], F32)
        msk2 = sbc("msk2", [128, 2, NE], F32)
        d12 = sbc("d12", [128, 4], F32)
        comb_t = sbc("comb_t", [8, QC], F32)
        ehs0 = sbc("ehs0", [128, 8, QC], BF16)
        ehs1 = sbc("ehs1", [128, 8, QC], BF16)
        eov = sbc("eov", [128, QC], F32)
        bcv = sbc("bcv", [128, QC], F32)
        ctr = sbc("ctr", [128, QC], F32)
        moe = sbc("moe", [128, QC], F32)
        moeb = sbc("moeb", [128, QC], BF16)
        rts = sbc("rts", [128, 16, QC], BF16)
        dcb = sbc("dcb", [128, 2, D], F32)
        outb = sbc("outb", [128, 2, D], F32)
        encscr = sbc("encscr", [128, 2, QC], F32)
        decscr = sbc("decscr", [128, D], F32)
        pE0 = cstk.enter_context(nc.psum_tensor("pE0", [128, 512], F32))
        pE1 = cstk.enter_context(nc.psum_tensor("pE1", [128, 512], F32))
        pC2 = cstk.enter_context(nc.psum_tensor("pC2", [128, 512], F32))
        pD = cstk.enter_context(nc.psum_tensor("pD", [128, 1024], F32))
        pM = cstk.enter_context(nc.psum_tensor("pM", [128, 512], F32))
        pTt = cstk.enter_context(nc.psum_tensor("pTt", [128, 640], BF16))
        selpr = sbc("selpr", [8, 4, 128], F32)
        gbc = sbc("gbc", [128, D], F32)
        bbc = sbc("bbc", [128, D], F32)
        rb2bc = sbc("rb2bc", [128, NE], F32)
        brow2 = sbc("brow2", [1, D], BF16)

        def pcw(sy):
            sy.wait_ge(s_w.h, W_CASTS)
            sy.dma_start(brow2[:], bass.AP(brow_pre, D, [[D, 1], [1, D]])).then_inc(s_dma.h, 16)
            s_dma.n += 16
            sy.dma_start(enc_sb[:], bass.AP(enc_pre, 0, [[128, 128], [8 * 128 * 128, 16],
                                                         [128 * 128, 8], [1, 128]]))\
              .then_inc(s_dma.h, 16)
            sy.dma_start(dec_sb[:], bass.AP(dec_nat, 0, [[D, 128], [128 * D, 16], [1, D]]))\
              .then_inc(s_dma.h, 16)
            sy.dma_start(s2c_sb[:], bass.AP(s2c_pre, 0, [[128, 128], [128 * 128, 16], [1, 128]]))\
              .then_inc(s_dma.h, 16)
            sy.dma_start(c2s_sb[:], bass.AP(c2s_pre, 0, [[128, 128], [128 * 128, 16], [1, 128]]))\
              .then_inc(s_dma.h, 16)
            sy.dma_start(ew1_sb[:], bass.AP(ew1_pre, 0, [[128, 128], [8 * 128 * 128, 4],
                                                         [128 * 128, 8], [1, 128]]))\
              .then_inc(s_dma.h, 16)
            sy.dma_start(ew2_sb[:], bass.AP(ew2_pre, 0, [[64, 128], [8 * 128 * 64, NE],
                                                         [128 * 64, 8], [1, 64]]))\
              .then_inc(s_dma.h, 16)
            sy.dma_start(rw_sb[:], bass.AP(rw_pre, 0, [[64 + NE, 64], [1, 64 + NE]]))\
              .then_inc(s_dma.h, 16)
            s_dma.n += 7 * 16

        step(sync=pcw)
        DMA_FW = s_dma.n

        def pc_selpr(g):
            g.iota(selpr[:], [[2, 4], [1, 2], [0, 64]], channel_multiplier=-1,
                   allow_small_or_imprecise_dtypes=True)

        step(gpsimd=pc_selpr)
        flush()

        def pc_selprg(g):
            g.drain().then_inc(s_g.h, 1)
            s_g.n += 1

        step(gpsimd=pc_selprg)
        flush()
        SELPR_I = s_g.n

        def pc_selpr2(v):
            v.wait_ge(s_g.h, SELPR_I)
            v.tensor_single_scalar(selpr[:], selpr[:], 0.0, ALU.is_equal)
            v.drain().then_inc(s_dve.h, 1)
            s_dve.n += 1

        step(vector=pc_selpr2)
        SELPR_OK = s_dve.n

        bc_specs = [(gbc, 0, 0), (gbc, 512, 4), (bbc, 0, 8), (bbc, 512, 12)]
        prev_copy = [0]
        for (dstt, off, r0) in bc_specs:
            def p0_bt(t, r0=r0):
                if prev_copy[0]:
                    t.wait_ge(s_dve.h, prev_copy[0])
                for i in range(4):
                    t.matmul(pD[:, i * 128:i * 128 + 128], selh[:, r0 + i, :],
                             growt[0:16, :], start=True, stop=True)
                t.drain().then_inc(s_pe.h, 1)
                s_pe.n += 1

            step(tensor=p0_bt)
            pem = s_pe.n

            def p0_bv(v, dstt=dstt, off=off, pem=pem):
                v.wait_ge(s_pe.h, pem)
                v.tensor_copy(dstt[:, off:off + 512], pD[:, 0:512])
                v.drain().then_inc(s_dve.h, 1)
                s_dve.n += 1

            step(vector=p0_bv)
            prev_copy[0] = s_dve.n

        def p0_rb2t(t):
            t.wait_ge(s_dve.h, prev_copy[0])
            t.matmul(pD[:, 0:8], onesf[:], crow2[0:1, 0:8], start=True, stop=True)
            t.drain().then_inc(s_pe.h, 1)
            s_pe.n += 1

        step(tensor=p0_rb2t)
        PE_RB2 = s_pe.n

        def p0_rb2v(v):
            v.wait_ge(s_pe.h, PE_RB2)
            v.tensor_copy(rb2bc[:], pD[:, 0:8])
            v.drain().then_inc(s_dve.h, 1)
            s_dve.n += 1

        step(vector=p0_rb2v)
        BCDONE = s_dve.n

        for j in range(NCB if _STOP != "C1" else 1):
            slj = j % 2

            def pc_x2ld(sy, j=j, slj=slj):
                tt_hi = ((j + 1) * QC - 1) // 128
                b_hi, t_hi = divmod(tt_hi, 4)
                sy.wait_ge(s_mv.h, evq[("x2st", b_hi, t_hi)])
                prev = evq.get(("outev", j - 2))
                if prev:
                    sy.wait_ge(s_dve.h, prev)
                sy.dma_start(x2c[:, slj, :, :],
                             bass.AP(x2s, j * QC * D, [[D, 128], [128 * D, 2], [1, D]]))\
                  .then_inc(s_dma.h, 16)
                s_dma.n += 16

            step(sync=pc_x2ld)
            x2ld = s_dma.n

            tp_marks = []
            for tt in range(2):
                def pc_ln_v(v, j=j, tt=tt, slj=slj, x2ld=x2ld):
                    if tt == 0:
                        v.wait_ge(s_dma.h, x2ld)
                    nt_ = j * 2 + tt
                    if tt > 0 or j > 0:
                        v.wait_ge(s_pe.h, evq.get(("tpc", j, tt - 1),
                                                  evq.get(("tpc", j - 1, 1), 0)))
                    v.tensor_scalar(xlnc[:], x2c[:, slj, tt, :],
                                    m2r2[:, nt_, 0:1], m2r2[:, nt_, 1:2],
                                    ALU.subtract, ALU.mult).then_inc(s_dve.h, 1)
                    s_dve.n += 1

                step(vector=pc_ln_v)
                dve_xlnc = s_dve.n

                for grp in range(2):
                    def pc_tp_t(t, j=j, tt=tt, grp=grp, dve_xlnc=dve_xlnc):
                        if grp == 0:
                            t.wait_ge(s_dve.h, dve_xlnc)
                        if tp_marks:
                            t.wait_ge(s_dve.h, tp_marks[-1])
                        else:
                            prevtp = evq.get(("tpv", j - 1))
                            if prevtp:
                                t.wait_ge(s_dve.h, prevtp)
                        for i in range(4):
                            dc = grp * 4 + i
                            t.transpose(pTt[:, i * 128:i * 128 + 128],
                                        xlnc[:, dc * 128:dc * 128 + 128], identb[:])
                        t.matmul(pTt[:].bitcast(F32)[0:4, 256:260], onesb[0:1, 0:4],
                                 onesb[0:1, 0:4],
                                 start=True, stop=True)
                        t.drain().then_inc(s_pe.h, 1)
                        s_pe.n += 1

                    step(tensor=pc_tp_t)
                    pem = s_pe.n
                    evq[("tpc", j, tt)] = s_pe.n

                    def pc_tp_v(v, j=j, tt=tt, grp=grp, pem=pem):
                        v.wait_ge(s_pe.h, pem)
                        for i in range(4):
                            dc = grp * 4 + i
                            cp = v.tensor_copy(x2lT[:, dc, tt * 128:tt * 128 + 128],
                                               pTt[:, i * 128:i * 128 + 128])
                        cp.then_inc(s_dve.h, 1)
                        s_dve.n += 1
                        tp_marks.append(s_dve.n)

                    step(vector=pc_tp_v)
            evq[("tpv", j)] = s_dve.n
            tplast = s_dve.n

            if _CLVL < 2:
                continue
            spk_marks = {}
            enc_dve = {}
            base_dve_enc = [0]

            def t_enc(t, j=j, tplast=tplast):
                for hc in range(16):
                    if hc == 0:
                        t.wait_ge(s_dve.h, tplast)
                        t.wait_ge(s_dma.h, DMA_FW)
                    pp = pE0 if hc % 2 == 0 else pE1
                    prev = enc_dve.get(hc - 2, evq.get(("encdve14", j - 1, hc)))
                    if prev:
                        t.wait_ge(s_dve.h, prev)
                    if hc <= 1:
                        pr = evq.get(("rts", j - 1, 14 + hc))
                        if pr:
                            t.wait_ge(s_act.h, pr)
                    mm = None
                    for dc in range(8):
                        mm = t.matmul(pp[:, 0:QC], enc_sb[:, hc, dc, :], x2lT[:, dc, :],
                                      start=(dc == 0), stop=(dc == 7))
                    if hc == 15:
                        t.drain().then_inc(s_pe.h, 1)
                    else:
                        mm.then_inc(s_pe.h, 1)
                    s_pe.n += 1
                    evq[("encpe", j, hc)] = s_pe.n

            def v_enc(v, j=j):
                for hc in range(16):
                    v.wait_ge(s_pe.h, evq[("encpe", j, min(hc + 1, 15))])
                    prevs = spk_marks.get(hc - 2, evq.get(("spk14", j - 1, hc)))
                    if prevs:
                        v.wait_ge(s_act.h, prevs)
                    pp = pE0 if hc % 2 == 0 else pE1
                    v.tensor_copy(encscr[:, hc % 2, :], pp[:, 0:QC]).then_inc(s_dve.h, 1)
                    s_dve.n += 1
                    enc_dve[hc] = s_dve.n
                    if hc >= 14:
                        evq[("encdve14", j, hc - 14)] = s_dve.n

            def s_enc(s, j=j):
                for hc in range(16):
                    s.wait_ge(s_dve.h, enc_dve[hc])
                    s.activation(spk[:, hc, :], encscr[:, hc % 2, :], _SIG(),
                                 bias=encb_c[:, hc:hc + 1], scale=1.0).then_inc(s_act.h, 1)
                    s_act.n += 1
                    spk_marks[hc] = s_act.n
                    if hc >= 14:
                        evq[("spk14", j, hc - 14)] = s_act.n

            base_pe_enc = s_pe.n
            for hc in range(16):
                evq[("encpe", j, hc)] = base_pe_enc + hc + 1
            base_dve = s_dve.n
            for hc in range(16):
                enc_dve[hc] = base_dve + hc + 1
            step(tensor=t_enc, vector=v_enc, scalar=s_enc)
            spk_marks15 = spk_marks[15]

            def pc_s2c_t(t, j=j):
                t.wait_ge(s_act.h, spk_marks15)
                prev = evq.get(("moedone", j - 1))
                if prev:
                    t.wait_ge(s_dve.h, prev)
                for hc in range(16):
                    t.matmul(pC2[:, 0:QC], s2c_sb[:, hc, :], spk[:, hc, :],
                             start=(hc == 0), stop=(hc == 15))
                t.drain().then_inc(s_pe.h, 1)
                s_pe.n += 1

            step(tensor=pc_s2c_t)
            pe_s2c = s_pe.n

            def pc_s2c_v(v, j=j, pe_s2c=pe_s2c):
                v.wait_ge(s_pe.h, pe_s2c)
                v.tensor_scalar(cont[:], pC2[:, 0:QC], s2cb_c[:, 0:1], None,
                                ALU.add).then_inc(s_dve.h, 1)
                s_dve.n += 1

            step(vector=pc_s2c_v)
            dve_cont = s_dve.n

            if _CLVL < 4:
                continue
            def pc_rt_t(t, j=j, dve_cont=dve_cont):
                t.wait_ge(s_dve.h, dve_cont)
                t.matmul(pM[0:64, 0:QC], rw_sb[0:64, 0:64], cont[0:64, :],
                         start=True, stop=True).then_inc(s_pe.h, 1)
                s_pe.n += 1

            step(tensor=pc_rt_t)
            pe_rh = s_pe.n

            def pc_rt_v(v, j=j, pe_rh=pe_rh):
                v.wait_ge(s_pe.h, pe_rh)
                v.tensor_scalar(rh[:], pM[0:64, 0:QC], rb1_c[:, 0:1], 0.0,
                                ALU.add, ALU.max).then_inc(s_dve.h, 1)
                s_dve.n += 1

            step(vector=pc_rt_v)
            dve_rh = s_dve.n

            if _CLVL < 5:
                continue
            def pc_lg_t(t, j=j, dve_rh=dve_rh):
                t.wait_ge(s_dve.h, dve_rh)
                for tt in range(2):
                    mm = t.matmul(pM[:, 256 + tt * 16:256 + tt * 16 + NE],
                                  rh[0:64, tt * 128:tt * 128 + 128], rw_sb[0:64, 64:64 + NE],
                                  start=True, stop=True)
                mm.then_inc(s_pe.h, 1)
                s_pe.n += 1

            step(tensor=pc_lg_t)
            pe_lg = s_pe.n

            def pc_top2(v, j=j, pe_lg=pe_lg):
                v.wait_ge(s_pe.h, pe_lg)
                if j == 0:
                    v.wait_ge(s_dve.h, BCDONE)
                for tt in range(2):
                    v.tensor_tensor(lgt[:, tt, :], pM[:, 256 + tt * 16:256 + tt * 16 + NE],
                                    rb2bc[:], ALU.add)
                v.drain()
                for tt in range(2):
                    lt = lgt[:, tt, :]
                    v.reduce_max(cst3[:, 0:1], lt, axis=AX.X)
                    v.drain()
                    v.tensor_single_scalar(cmb[:, tt, :], lt, cst3[:, 0:1], ALU.is_ge)
                    v.drain()
                    v.scalar_tensor_tensor(msk[:], cmb[:, tt, :], -1e30, lt, ALU.mult, ALU.add)
                    v.drain()
                    v.reduce_max(cst3[:, 1:2], msk[:], axis=AX.X)
                    v.drain()
                    v.tensor_single_scalar(msk2[:, tt, :], msk[:], cst3[:, 1:2], ALU.is_ge)
                    tl = v.tensor_tensor(d12[:, tt:tt + 1], cst3[:, 0:1], cst3[:, 1:2],
                                         ALU.subtract)
                tl.then_inc(s_dve.h, 1)
                s_dve.n += 1

            step(vector=pc_top2)
            dve_t2 = s_dve.n

            def pc_w1_s(s, j=j, dve_t2=dve_t2):
                s.wait_ge(s_dve.h, dve_t2)
                s.activation(d12[:, 2:4], d12[:, 0:2], _SIG()).then_inc(s_act.h, 1)
                s_act.n += 1

            step(scalar=pc_w1_s)
            act_w1 = s_act.n

            def pc_cmb_v(v, j=j, act_w1=act_w1):
                v.wait_ge(s_act.h, act_w1)
                for tt in range(2):
                    v.tensor_scalar(cst3[:, 4 + tt:5 + tt], d12[:, 2 + tt:3 + tt], -1.0, 1.0,
                                    ALU.mult, ALU.add)
                v.drain()
                for tt in range(2):
                    v.tensor_scalar(cmb[:, tt, :], cmb[:, tt, :], d12[:, 2 + tt:3 + tt],
                                    None, ALU.mult)
                v.drain()
                for tt in range(2):
                    v.tensor_scalar(msk2[:, tt, :], msk2[:, tt, :], cst3[:, 4 + tt:5 + tt],
                                    None, ALU.mult)
                v.drain()
                for tt in range(2):
                    tl = v.tensor_tensor(cmb[:, tt, :], cmb[:, tt, :], msk2[:, tt, :], ALU.add)
                tl.then_inc(s_dve.h, 1)
                s_dve.n += 1

            step(vector=pc_cmb_v)
            dve_cmb = s_dve.n

            if _CLVL < 6:
                continue
            def pc_ct_t(t, j=j, dve_cmb=dve_cmb):
                t.wait_ge(s_dve.h, dve_cmb)
                for tt in range(2):
                    t.transpose(pM[0:8, tt * 128:tt * 128 + 128], cmb[:, tt, :], identf[:])
                t.matmul(pM[0:4, 508:512], onesf[0:1, 0:4], onesf[0:1, 0:4], start=True, stop=True)
                t.drain().then_inc(s_pe.h, 1)
                s_pe.n += 1

            step(tensor=pc_ct_t)
            pe_ct = s_pe.n

            def pc_ct_v(v, j=j, pe_ct=pe_ct):
                v.wait_ge(s_pe.h, pe_ct)
                v.tensor_copy(comb_t[:], pM[0:8, 0:QC]).then_inc(s_dve.h, 1)
                s_dve.n += 1

            step(vector=pc_ct_v)
            dve_ct = s_dve.n

            if _CLVL < 7:
                continue
            moe_marks = []
            for p in range(4):
                eh_marks = []
                for hc in range(8):
                    pebox = []

                    def t_eh(t, j=j, p=p, hc=hc, pebox=pebox):
                        if p == 0 and hc == 0:
                            t.wait_ge(s_dve.h, dve_cont)
                        ppA = pE0 if hc % 2 == 0 else pC2
                        ppB = pE1 if hc % 2 == 0 else pM
                        prev = eh_marks[hc - 2] if hc >= 2 else \
                            evq.get(("ehm", j, p - 1, 6 + hc), evq.get(("ehm", j - 1, 3, 6 + hc)))
                        if prev:
                            t.wait_ge(s_dve.h, prev)
                        if hc <= 1:
                            prevmoe = moe_marks[p - 1] if p >= 1 else evq.get(("moedone", j - 1))
                            if prevmoe:
                                t.wait_ge(s_dve.h, prevmoe)
                            if p == 0:
                                t.wait_ge(s_dve.h, dve_ct)
                        if not _NOEH:
                            t.matmul(ppA[:, 0:QC], ew1_sb[0:64, p, hc, :], cont[0:64, :],
                                     start=True, stop=True)
                            t.matmul(ppB[:, 0:QC], ew1_sb[64:128, p, hc, :],
                                     cont[64:128, :], start=True,
                                     stop=True).then_inc(s_pe.h, 1)
                        else:
                            t.drain().then_inc(s_pe.h, 1)
                        s_pe.n += 1
                        pebox.append(s_pe.n)

                    def v_eh(v, j=j, p=p, hc=hc, pebox=pebox):
                        v.wait_ge(s_pe.h, pebox[0])
                        ppA = pE0 if hc % 2 == 0 else pC2
                        ppB = pE1 if hc % 2 == 0 else pM
                        v.tensor_scalar(ehs0[:, hc, :], ppA[:, 0:QC],
                                        eb1_c[:, 2 * p, hc:hc + 1], 0.0, ALU.add, ALU.max)
                        v.tensor_scalar(ehs1[:, hc, :], ppB[:, 0:QC],
                                        eb1_c[:, 2 * p + 1, hc:hc + 1], 0.0,
                                        ALU.add, ALU.max).then_inc(s_dve.h, 1)
                        s_dve.n += 1
                        eh_marks.append(s_dve.n)
                        evq[("ehm", j, p, hc)] = s_dve.n

                    step(tensor=t_eh, vector=v_eh)

                def pc_eo_t(t, j=j, p=p, eh_marks=eh_marks, dve_ct=dve_ct):
                    prev = moe_marks[p - 1] if p >= 1 else evq.get(("moedone", j - 1))
                    if prev:
                        t.wait_ge(s_dve.h, prev)
                    if p == 0:
                        pdd = evq.get(("decdve", j - 1, 1))
                        if pdd:
                            t.wait_ge(s_dve.h, pdd)
                    for hc in range(8):
                        t.wait_ge(s_dve.h, eh_marks[hc])
                        if not _NOEO:
                            t.matmul(pD[0:64, 0:QC], ew2_sb[:, 2 * p, hc, :], ehs0[:, hc, :],
                                     start=(hc == 0), stop=(hc == 7), tile_position=(0, 0))
                            t.matmul(pD[64:128, 512:512 + QC], ew2_sb[:, 2 * p + 1, hc, :],
                                     ehs1[:, hc, :],
                                     start=(hc == 0), stop=(hc == 7), tile_position=(0, 64))
                    t.wait_ge(s_dve.h, dve_ct)
                    if j == 0:
                        t.wait_ge(s_dve.h, SELPR_OK)
                    if not _NOBC:
                        t.matmul(pM[:, 256:256 + QC], selpr[:, p, :], comb_t[:],
                                 start=True, stop=True)
                    t.drain().then_inc(s_pe.h, 1)
                    s_pe.n += 1

                step(tensor=pc_eo_t)
                pe_eo = s_pe.n

                def pc_eo_v(v, j=j, p=p, pe_eo=pe_eo):
                    v.wait_ge(s_pe.h, pe_eo)
                    v.tensor_scalar(eov[0:64, :], pD[0:64, 0:QC], eb2_c[0:64, p:p + 1],
                                    None, ALU.add)
                    v.tensor_scalar(eov[64:128, :], pD[64:128, 512:512 + QC],
                                    eb2_c[64:128, p:p + 1], None, ALU.add)
                    v.tensor_copy(bcv[:], pM[:, 256:256 + QC])
                    v.drain()
                    if p == 0:
                        v.tensor_tensor(moe[:], eov[:], bcv[:],
                                        ALU.mult).then_inc(s_dve.h, 1)
                    else:
                        v.tensor_tensor(ctr[:], eov[:], bcv[:], ALU.mult)
                        v.drain()
                        v.tensor_tensor(moe[:], moe[:], ctr[:],
                                        ALU.add).then_inc(s_dve.h, 1)
                    s_dve.n += 1
                    moe_marks.append(s_dve.n)

                step(vector=pc_eo_v)
            evq[("moedone", j)] = moe_marks[-1]

            if _CLVL < 8:
                continue
            def pc_moeb(v, j=j):
                v.wait_ge(s_dve.h, moe_marks[-1])
                v.tensor_copy(moeb[:], moe[:]).then_inc(s_dve.h, 1)
                s_dve.n += 1

            step(vector=pc_moeb)
            dve_moeb = s_dve.n

            if _CLVL < 9:
                continue
            rts_marks = {}
            c2s_pe = {}

            def t_c2s(t, j=j, dve_moeb=dve_moeb):
                for hc in range(16):
                    if hc == 0:
                        t.wait_ge(s_dve.h, dve_moeb)
                    pp = pE0 if hc % 2 == 0 else pE1
                    prev = rts_marks.get(hc - 2)
                    if prev:
                        t.wait_ge(s_act.h, prev)
                    t.matmul(pp[:, 0:QC], c2s_sb[:, hc, :], moeb[:], start=True,
                             stop=True).then_inc(s_pe.h, 1)
                    s_pe.n += 1
                    c2s_pe[hc] = s_pe.n

            def s_c2s(s, j=j):
                for hc in range(16):
                    s.wait_ge(s_pe.h, c2s_pe[hc])
                    pp = pE0 if hc % 2 == 0 else pE1
                    s.activation(rts[:, hc, :], pp[:, 0:QC], _SIG(),
                                 bias=c2sb_c[:, hc:hc + 1], scale=1.0).then_inc(s_act.h, 1)
                    s_act.n += 1
                    rts_marks[hc] = s_act.n

            base_act_c2s = s_act.n
            for hc in range(16):
                rts_marks[hc] = base_act_c2s + hc + 1
            evq[("rts", j, 14)] = rts_marks[14]
            evq[("rts", j, 15)] = rts_marks[15]
            step(tensor=t_c2s, scalar=s_c2s)
            rts_last = rts_marks[15]

            dec_marks = []
            for tt in range(2):
                def pc_dec_t(t, j=j, tt=tt, rts_last=rts_last):
                    t.wait_ge(s_act.h, rts_last)
                    prev = evq.get(("decdve", j, tt - 1), evq.get(("decdve", j - 1, 1)))
                    if prev:
                        t.wait_ge(s_dve.h, prev)
                    for hc in range(16):
                        lhs = rts[:, hc, tt * 128:tt * 128 + 128]
                        t.matmul(pD[:, 0:512], lhs, dec_sb[:, hc, 0:512],
                                 start=(hc == 0), stop=False)
                        t.matmul(pD[:, 512:1024], lhs, dec_sb[:, hc, 512:1024],
                                 start=(hc == 0), stop=False)
                    t.matmul(pD[:, 0:512], onesb[:], brow2[0:1, 0:512],
                             start=False, stop=True)
                    t.matmul(pD[:, 512:1024], onesb[:], brow2[0:1, 512:1024],
                             start=False, stop=True)
                    t.drain().then_inc(s_pe.h, 1)
                    s_pe.n += 1

                step(tensor=pc_dec_t)
                pe_dec = s_pe.n

                def pc_dec_v(v, j=j, tt=tt, pe_dec=pe_dec):
                    v.wait_ge(s_pe.h, pe_dec)
                    prevs = evq.get(("decact", j, tt - 1), evq.get(("decact", j - 1, 1)))
                    if prevs:
                        v.wait_ge(s_act.h, prevs)
                    v.tensor_copy(decscr[:], pD[:]).then_inc(s_dve.h, 1)
                    s_dve.n += 1

                step(vector=pc_dec_v)
                dec_dve = s_dve.n
                evq[("decdve", j, tt)] = dec_dve

                def pc_dec_s(s, j=j, tt=tt, dec_dve=dec_dve):
                    s.wait_ge(s_dve.h, dec_dve)
                    s.activation(dcb[:, tt, :], decscr[:], _SIG()).then_inc(s_act.h, 1)
                    s_act.n += 1
                    dec_marks.append(s_act.n)
                    evq[("decact", j, tt)] = s_act.n

                step(scalar=pc_dec_s)

            if _CLVL < 11:
                continue
            def pc_l3_v(v, j=j, dec_marks=dec_marks):
                v.wait_ge(s_act.h, dec_marks[-1])
                for tt in range(2):
                    v.reduce_sum(cst3[:, 8 + 4 * tt:9 + 4 * tt], dcb[:, tt, :], axis=AX.X)
                    v.tensor_mul(lnscr[:], dcb[:, tt, :], dcb[:, tt, :])
                    v.drain()
                    v.reduce_sum(cst3[:, 9 + 4 * tt:10 + 4 * tt], lnscr[:], axis=AX.X)
                v.drain()
                for tt in range(2):
                    v.tensor_scalar_mul(cst3[:, 8 + 4 * tt:9 + 4 * tt],
                                        cst3[:, 8 + 4 * tt:9 + 4 * tt], 1.0 / D)
                    v.tensor_scalar_mul(cst3[:, 9 + 4 * tt:10 + 4 * tt],
                                        cst3[:, 9 + 4 * tt:10 + 4 * tt], 1.0 / D)
                v.drain()
                for tt in range(2):
                    v.tensor_mul(cst3[:, 10 + 4 * tt:11 + 4 * tt],
                                 cst3[:, 8 + 4 * tt:9 + 4 * tt], cst3[:, 8 + 4 * tt:9 + 4 * tt])
                v.drain()
                for tt in range(2):
                    tl = v.tensor_sub(cst3[:, 10 + 4 * tt:11 + 4 * tt],
                                      cst3[:, 9 + 4 * tt:10 + 4 * tt],
                                      cst3[:, 10 + 4 * tt:11 + 4 * tt])
                tl.then_inc(s_dve.h, 1)
                s_dve.n += 1

            step(vector=pc_l3_v)
            dve_v3 = s_dve.n

            def pc_l3_s(s, j=j, dve_v3=dve_v3):
                s.wait_ge(s_dve.h, dve_v3)
                for tt in range(2):
                    s.activation(cst3[:, 11 + 4 * tt:12 + 4 * tt],
                                 cst3[:, 10 + 4 * tt:11 + 4 * tt], ACT.Ln, bias=cns[:, 0:1], scale=1.0)
                s.drain()
                for tt in range(2):
                    al = s.activation(cst3[:, 10 + 4 * tt:11 + 4 * tt],
                                      cst3[:, 11 + 4 * tt:12 + 4 * tt], ACT.Exp,
                                      scale=cns[:, 2:3])
                al.then_inc(s_act.h, 1)
                s_act.n += 1

            step(scalar=pc_l3_s)
            act_r3 = s_act.n

            def pc_fin_v(v, j=j, slj=slj, act_r3=act_r3):
                v.wait_ge(s_act.h, act_r3)
                if j == 0:
                    v.wait_ge(s_dve.h, BCDONE)
                prev = evq.get(("ost", j - 1))
                if prev:
                    v.wait_ge(s_out.h, prev)
                for tt in range(2):
                    v.tensor_scalar(lnscr[:], dcb[:, tt, :],
                                    cst3[:, 8 + 4 * tt:9 + 4 * tt],
                                    cst3[:, 10 + 4 * tt:11 + 4 * tt],
                                    ALU.subtract, ALU.mult)
                    v.drain()
                    v.tensor_tensor(lnscr2[:], lnscr[:], gbc[:], ALU.mult)
                    v.drain()
                    v.tensor_tensor(lnscr[:], lnscr2[:], bbc[:], ALU.add)
                    v.drain()
                    tl = v.tensor_tensor(outb[:, tt, :], lnscr[:], x2c[:, slj, tt, :],
                                         ALU.add)
                tl.then_inc(s_dve.h, 1)
                s_dve.n += 1

            step(vector=pc_fin_v)
            evq[("outev", j)] = s_dve.n
            dve_out = s_dve.n

            def pc_ost(g, j=j, dve_out=dve_out):
                g.wait_ge(s_dve.h, dve_out)
                g.dma_start(bass.AP(out, j * QC * D, [[D, 128], [128 * D, 2], [1, D]]),
                            outb[:]).then_inc(s_out.h, 16)
                s_out.n += 16
                evq[("ost", j)] = s_out.n

            step(gpsimd=pc_ost)

            if _DEBUG and j == 0:
                def dbg_c(g, upto=s_dve.n):
                    g.wait_ge(s_dve.h, upto)
                    g.dma_start(dbg["spk"].ap(), spk[:]).then_inc(s_mv.h, 16)
                    g.dma_start(dbg["cont"].ap(), cont[:]).then_inc(s_mv.h, 16)
                    g.dma_start(dbg["comb"].ap(), cmb[:]).then_inc(s_mv.h, 16)
                    g.dma_start(dbg["moe"].ap(), moe[:]).then_inc(s_mv.h, 16)
                    g.dma_start(dbg["dec"].ap(), dcb[:]).then_inc(s_mv.h, 16)
                    g.dma_start(dbg["lgt"].ap(), lgt[:]).then_inc(s_mv.h, 16)
                    g.dma_start(dbg["rh"].ap(), rh[:]).then_inc(s_mv.h, 16)
                    s_mv.n += 112

                step(gpsimd=dbg_c)

        if _STOP == "C1":
            def finC(g):
                g.wait_ge(s_out.h, s_out.n)
                g.wait_ge(s_dve.h, s_dve.n)
                g.wait_ge(s_dma.h, s_dma.n)
            step(gpsimd=finC)
            flush()
            cstk.close()
            return nc

        def fin(g):
            g.wait_ge(s_out.h, s_out.n)

        step(gpsimd=fin)
        flush()
        cstk.close()

    return nc


def kernel(**inputs):
    x = np.ascontiguousarray(np.asarray(inputs["L0_attn_x"], dtype=np.float32))
    names = {
        "w_qkv": "L0_attn_w_qkv", "w_proj": "L0_attn_w_proj", "b_proj": "L0_attn_b_proj",
        "enc_w": "L0_ffn_enc_w", "enc_b": "L0_ffn_enc_b", "s2c_w": "L0_ffn_s2c_w",
        "s2c_b": "L0_ffn_s2c_b", "r_w1": "L0_ffn_r_w1", "r_b1": "L0_ffn_r_b1",
        "r_w2": "L0_ffn_r_w2", "r_b2": "L0_ffn_r_b2", "e_w1": "L0_ffn_e_w1",
        "e_b1": "L0_ffn_e_b1", "e_w2": "L0_ffn_e_w2", "e_b2": "L0_ffn_e_b2",
        "c2s_w": "L0_ffn_c2s_w", "c2s_b": "L0_ffn_c2s_b", "dec_w": "L0_ffn_dec_w",
        "dec_b": "L0_ffn_dec_b", "ln_g": "L0_ffn_ln_g", "ln_b": "L0_ffn_ln_b",
    }
    wmap = {k: np.ascontiguousarray(np.asarray(inputs[v], dtype=np.float32))
            for k, v in names.items()}
    nc = build_core()
    in_maps = []
    for c in range(8):
        m = {"x": np.ascontiguousarray(x[c])}
        m.update(wmap)
        in_maps.append(m)
    res = run_bass_kernel_spmd(nc, in_maps, core_ids=list(range(8)))
    if _DEBUG:
        kernel.dbg = res.results
    return np.stack([res.results[c]["out"] for c in range(8)], axis=0)


if __name__ == "__main__":
    build_core()
    print("built OK")
